# version 11
# baseline (speedup 1.0000x reference)
import numpy as np
import concourse.bass as bass
import concourse.mybir as mybir
from concourse.bass_utils import run_bass_kernel_spmd
from concourse.alu_op_type import AluOpType as OP

F32 = mybir.dt.float32
BF16 = mybir.dt.bfloat16
AF = mybir.ActivationFunctionType
AX = mybir.AxisListType

C = 1024
T = 8192
CC = 8
G = 256
NG = T // G
F = 4096
FC = F // 128
KSEL = T // 2
N_BIS = 46
LN_EPS = 1e-5


class Sched:

    ENGINES = ("PE", "DVE", "ACT", "SP", "GP")
    SEMS = ("PE", "DVE", "ACT", "SP", "GP", "SPX", "GPX", "AQ", "AQX")
    QUEUES = {"SP": ("SP", "SP", "SPX"), "GP": ("GP", "GP", "GPX"),
              "AQ": ("ACT", "AQ", "AQX")}

    def __init__(self):
        self.ops = {e: [] for e in self.ENGINES}
        self.cnt = {s: 0 for s in self.SEMS}
        self.buf = {}
        self.mark = {e: {s: 0 for s in self.SEMS} for e in self.ENGINES}

    def _waits_for(self, eng, deps):
        by_e = {}
        for (de, dv) in deps:
            by_e[de] = max(by_e.get(de, 0), dv)
        waits = []
        for de in self.SEMS:
            dv = by_e.get(de, 0)
            if dv > self.mark[eng][de]:
                waits.append((de, dv))
                self.mark[eng][de] = dv
        return waits

    def _deps(self, reads, writes):
        deps = []
        for b in reads:
            st = self.buf.get(b)
            if st and st["w"] is not None:
                deps.append(st["w"])
        for b in writes:
            st = self.buf.get(b)
            if st:
                if st["w"] is not None:
                    deps.append(st["w"])
                deps.extend(st["r"])
        return deps

    def _register(self, reads, writes, sem, val):
        for b in reads:
            self.buf.setdefault(b, {"w": None, "r": []})["r"].append((sem, val))
        for b in writes:
            self.buf[b] = {"w": (sem, val), "r": []}

    def op(self, eng, fn, reads=(), writes=(), tag=""):
        if eng in self.QUEUES:
            return self.dma_op(eng, fn, reads, writes, tag)
        waits = self._waits_for(eng, self._deps(reads, writes))
        self.cnt[eng] += 1
        self._register(reads, writes, eng, self.cnt[eng])
        self.ops[eng].append((waits, fn, 1, eng, tag))

    def dma_op(self, q, fn, reads=(), writes=(), tag="", drain=True):
        stream, sem, semx = self.QUEUES[q]
        waits = self._waits_for(stream, self._deps(reads, writes))
        self.cnt[sem] += 16
        self.ops[stream].append((waits, fn, 16, sem, tag))
        if drain:
            dw = self._waits_for(stream, [(sem, self.cnt[sem])])
            self.cnt[semx] += 1
            self._register(reads, writes, semx, self.cnt[semx])
            self.ops[stream].append((dw, lambda e: e.nop(), 1, semx,
                                     tag + "_dr"))

    def fence(self, eng, counts):
        waits = self._waits_for(eng, [(s, counts.get(s, 0)) for s in self.SEMS])
        if waits:
            self.ops[eng].append((waits, None, 0, eng, "fence"))

    def snapshot(self):
        return dict(self.cnt)


def build_program():
    nc = bass.Bass("TRN2", target_bir_lowering=False)

    xT_h = nc.dram_tensor("xT", [C, T], F32, kind="ExternalInput").ap()
    wkT_h = nc.dram_tensor("wkT", [C, C], F32, kind="ExternalInput").ap()
    wvT_h = nc.dram_tensor("wvT", [C, C], F32, kind="ExternalInput").ap()
    wrT_h = nc.dram_tensor("wrT", [C, C], F32, kind="ExternalInput").ap()
    woT_h = nc.dram_tensor("woT", [C, C], F32, kind="ExternalInput").ap()
    w1_h = nc.dram_tensor("w1", [C, F], F32, kind="ExternalInput").ap()
    w2_h = nc.dram_tensor("w2", [F, C], F32, kind="ExternalInput").ap()
    tmk_h = nc.dram_tensor("tmk", [128, CC], F32, kind="ExternalInput").ap()
    tmv_h = nc.dram_tensor("tmv", [128, CC], F32, kind="ExternalInput").ap()
    tmr_h = nc.dram_tensor("tmr", [128, CC], F32, kind="ExternalInput").ap()
    td_h = nc.dram_tensor("td", [128, CC], F32, kind="ExternalInput").ap()
    tf_h = nc.dram_tensor("tf", [128, CC], F32, kind="ExternalInput").ap()
    wrt_h = nc.dram_tensor("wrt", [128, CC], F32, kind="ExternalInput").ap()
    b1_h = nc.dram_tensor("b1", [128, FC], F32, kind="ExternalInput").ap()
    b2_h = nc.dram_tensor("b2", [128, CC], F32, kind="ExternalInput").ap()
    onescol_h = nc.dram_tensor("onescol", [128, 1], F32, kind="ExternalInput").ap()
    onescolb_h = nc.dram_tensor("onescolb", [128, 1], BF16, kind="ExternalInput").ap()
    onesrow_h = nc.dram_tensor("onesrow", [1, 128], F32, kind="ExternalInput").ap()
    ident_h = nc.dram_tensor("ident", [128, 128], F32, kind="ExternalInput").ap()
    cst_h = nc.dram_tensor("cst", [128, 4], F32, kind="ExternalInput").ap()
    iota_h = nc.dram_tensor("iota", [128, 64], mybir.dt.int32,
                            kind="ExternalInput").ap()
    triu_h = nc.dram_tensor("triu", [128, 128], F32, kind="ExternalInput").ap()
    xo_h = nc.dram_tensor("xo", [C, T], F32, kind="ExternalOutput").ap()
    x1T_h = nc.dram_tensor("x1T", [C, T], F32, kind="Internal").ap()
    srow_h = nc.dram_tensor("srow", [1, T], F32, kind="Internal").ap()
    lncr_h = nc.dram_tensor("lncr", [1, T], F32, kind="Internal").ap()
    lncm_h = nc.dram_tensor("lncm", [1, T], F32, kind="Internal").ap()
    wmrow_h = nc.dram_tensor("wmrow", [1, T], F32, kind="Internal").ap()
    x1rows_h = nc.dram_tensor("x1rows", [T, C], BF16, kind="Internal").ap()
    ufull_h = nc.dram_tensor("ufull", [T, C], BF16, kind="Internal").ap()
    idxb_h = nc.dram_tensor("idxb", [T, 1], mybir.dt.int32, kind="Internal").ap()

    from contextlib import ExitStack
    stack = ExitStack()
    sb = lambda n, s, t: stack.enter_context(nc.sbuf_tensor(n, s, t))
    psm = lambda n, s: stack.enter_context(nc.psum_tensor(n, s, F32))

    AW = sb("AW", [128, 32768], BF16)
    AR1 = sb("AR1", [128, 32768], BF16)
    AR2 = sb("AR2", [128, 32768], BF16)

    ident_t = sb("identt", [128, 128], F32)
    onescol_t = sb("onescolt", [128, 1], F32)
    onescolb_t = sb("onescolbt", [128, 1], BF16)
    onesrow_t = sb("onesrowt", [1, 128], F32)
    cst_t = sb("cstt", [128, 4], F32)
    tmk_t = sb("tmkt", [128, CC], F32)
    tmv_t = sb("tmvt", [128, CC], F32)
    tmr_t = sb("tmrt", [128, CC], F32)
    td_t = sb("tdt", [128, CC], F32)
    tf_t = sb("tft", [128, CC], F32)
    wrt_t = sb("wrtt", [128, CC], F32)
    b1_t = sb("b1t", [128, FC], F32)
    b2_t = sb("b2t", [128, CC], F32)
    eu_t = sb("eut", [128, CC], F32)
    dd_t = sb("ddt", [128, CC], F32)
    rows_t = sb("rowst", [1, 1792], F32)
    scoreT_t = sb("scoreTt", [128, 64], F32)
    geT_t = sb("geTt", [128, 64], F32)
    sigT_t = sb("sigTt", [128, 64], F32)
    bis_t = sb("bist", [1, 16], F32)
    wmtr_t = sb("wmtrt", [64, 128], F32)
    wmr_t = sb("wmrt", [1, 2 * G], F32)
    iota_t = sb("iotat", [128, 64], mybir.dt.int32)
    triu_t = sb("triut", [128, 128], F32)
    posI_t = sb("posIt", [128, 64], mybir.dt.int32)
    idxc_t = sb("idxct", [128, 4], mybir.dt.int32)
    fcol_t = sb("fcolt", [128, 8], F32)
    csB_t = sb("csBt", [1, 64], F32)
    csC_t = sb("csCt", [64, 128], F32)
    wmb2_t = sb("wmb2t", [128, G], BF16)

    def carve(arena, off_bytes, cols, dt):
        el = off_bytes // 2
        w = cols * (2 if dt == F32 else 1)
        ap = arena[:, el:el + w]
        if dt == F32:
            ap = ap.bitcast(F32)
        return ap

    def attnW(m, cci):
        off = (m * CC + cci) * 1024
        return AW[:, off:off + 1024]

    def w2W(fi):
        return AW[:, fi * 1024:(fi + 1) * 1024]

    def w1W(cci):
        return AR1[:, cci * 4096:(cci + 1) * 4096]

    xTg = [carve(AR1, 0, 2048, F32), carve(AR1, 8192, 2048, F32),
           carve(AR1, 16384, 2048, F32)]
    EK_p = [carve(AR1, 24576, 2048, F32), carve(AR1, 32768, 2048, F32)]
    EKV_p = [carve(AR1, 40960, 2048, F32), carve(AR1, 49152, 2048, F32)]
    SCR = carve(AR1, 57344, 2048, BF16)

    xk_p = [carve(AR2, 0, 2048, BF16), carve(AR2, 12288, 2048, BF16)]
    xv_p = [carve(AR2, 4096, 2048, BF16), carve(AR2, 16384, 2048, BF16)]
    xr_p = [carve(AR2, 8192, 2048, BF16), carve(AR2, 20480, 2048, BF16)]
    sig_p = [carve(AR2, 24576, 2048, BF16), carve(AR2, 28672, 2048, BF16)]
    Asc_t = carve(AR2, 32768, CC * 257, F32)
    Bsc_t = carve(AR2, 40992, CC * 257, F32)
    xsh_t = carve(AR2, 49216, CC * 257, BF16)
    ATX = carve(AR2, 53328, 2048, F32)

    P = [psm(f"PS{i}", [128, 512]) for i in range(3)]
    PN = [f"PS{i}" for i in range(3)]
    PW = [psm(f"PW{i}", [128, 512]) for i in range(2)]
    PWN = [f"PW{i}" for i in range(2)]
    PSTS = psm("PSTS", [128, 512])
    PSTQ = psm("PSTQ", [1, 256])
    PBB = psm("PBB", [128, 512])
    PB0 = PBB[:, 0:256]
    PB1 = PBB[:, 256:512]

    sems = {}
    for e in Sched.SEMS:
        sems[e] = stack.enter_context(nc.semaphore("sem" + e))
    S = Sched()

    def dma(q, out, in_, reads, writes, tag="", drain=True, transpose=False):
        S.dma_op(q, lambda e, o=out, i=in_, tr=transpose: e.dma_start(
            out=o, in_=i, transpose=tr), reads, writes, tag, drain=drain)

    def pe(fn, reads, writes, tag=""):
        S.op("PE", fn, reads, writes, tag)

    def dve(fn, reads, writes, tag=""):
        S.op("DVE", fn, reads, writes, tag)

    def act(fn, reads, writes, tag=""):
        S.op("ACT", fn, reads, writes, tag)

    cc_s = lambda cc: slice(cc * G, (cc + 1) * G)
    cb_s = lambda cc: slice(cc * 257 + 1, cc * 257 + 257)
    cs_s = lambda cc: slice(cc * 257, cc * 257 + 256)
    bnd_s = slice(0, CC * 257, 257)
    lst_s = slice(256, CC * 257, 257)

    RW_M, RW_MSQ, RW_VAR, RW_RSTD, RW_MR, RW_S, RW_T1 = (
        0, 256, 512, 768, 1024, 1280, 1536)
    row = lambda off: rows_t[0:1, off:off + G]
    PSTs = PSTS[0:1, 0:256]
    PSTq = PSTQ[0:1, 0:256]

    def mm_contract(e, lhsT_fn, rhs_t, out_ap, n, rhs_sl_fn):
        last = None
        for i in range(n):
            last = e.matmul(out_ap, lhsT_fn(i), rhs_t[:, rhs_sl_fn(i)],
                            start=(i == 0), stop=(i == n - 1))
        return last

    def stat_bf(e, rhs_t, dst):
        return mm_contract(e, lambda i: onescolb_t[:, :], rhs_t,
                           dst, CC, cc_s)

    def stat_f32(e, rhs_t, dst):
        return mm_contract(e, lambda i: onescol_t[:, :], rhs_t,
                           dst, CC, cc_s)

    def rowchain(tag=""):
        dve(lambda e: e.tensor_scalar(row(RW_M), PSTs, 1.0 / C,
                                      None, OP.mult), ["PSTs"], ["rm"], tag)
        dve(lambda e: e.tensor_tensor(row(RW_MSQ), row(RW_M), row(RW_M),
                                      OP.mult), ["rm"], ["rmsq"], tag)
        dve(lambda e: e.scalar_tensor_tensor(row(RW_VAR), PSTq,
                                             1.0 / C, row(RW_MSQ),
                                             OP.mult, OP.subtract),
            ["PSTq", "rmsq"], ["rvar"], tag)
        act(lambda e: e.activation(row(RW_RSTD), row(RW_VAR), AF.Ln,
                                   bias=cst_t[0:1, 1:2]),
            ["rvar", "cst"], ["rrstd"], tag)
        act(lambda e: e.activation(row(RW_RSTD), row(RW_RSTD), AF.Exp,
                                   scale=-0.5), ["rrstd"], ["rrstd"], tag)
        dve(lambda e: e.tensor_tensor(row(RW_MR), row(RW_M), row(RW_RSTD),
                                      OP.mult), ["rm", "rrstd"], ["rmr"], tag)

    def bcast_pair(extra_reads=(), tag=""):
        pe(lambda e: e.matmul(PBB[:, 0:2 * G], onesrow_t[:, :],
                              rows_t[0:1, RW_RSTD:RW_RSTD + 2 * G],
                              start=True, stop=True),
           ["onesrow", "rrstd", "rmr"] + list(extra_reads),
           ["PB0", "PB1"], tag)

    mmro = [0]

    def next_p():
        i = mmro[0] % 3
        mmro[0] += 1
        return P[i], PN[i]

    mmw = [0]

    def next_pw():
        i = mmw[0] % 2
        mmw[0] += 1
        return PW[i], PWN[i]

    dma("SP", ident_t[:, :], ident_h, [], ["ident"])
    dma("SP", onescol_t[:, :], onescol_h, [], ["onescol"])
    dma("SP", onescolb_t[:, :], onescolb_h, [], ["onescolb"])
    dma("SP", onesrow_t[:, :], onesrow_h, [], ["onesrow"])
    dma("SP", cst_t[:, :], cst_h, [], ["cst"])
    for nm, t, h in [("tmk", tmk_t, tmk_h), ("tmv", tmv_t, tmv_h),
                     ("tmr", tmr_t, tmr_h), ("td", td_t, td_h),
                     ("tf", tf_t, tf_h), ("wrt", wrt_t, wrt_h),
                     ("b1", b1_t, b1_h), ("b2", b2_t, b2_h)]:
        dma("SP", t[:, :], h, [], [nm])

    for m, h in enumerate([wkT_h, wvT_h, wrT_h, woT_h]):
        for cci in range(CC):
            dma("GP", attnW(m, cci), h[cci * 128:(cci + 1) * 128, :],
                [], [f"aw{m}_{cci}"])

    dma("SP", iota_t[:, :], iota_h, [], ["iota"])
    dma("SP", triu_t[:, :], triu_h, [], ["triu"])
    dve(lambda e: e.memset(SCR[:, 0:1024], 0.0), [], ["SCR"])
    for k in range(T // 128):
        dma("GP", ufull_h[k * 128:(k + 1) * 128, :],
            SCR[:, 0:1024], ["SCR"], ["ufull"], f"uz{k}",
            drain=(k == T // 128 - 1))
    act(lambda e: e.activation(eu_t[:, :], tf_t[:, :], AF.Exp), ["tf"], ["eu"])
    act(lambda e: e.activation(dd_t[:, :], td_t[:, :], AF.Exp), ["td"], ["dd"])
    dve(lambda e: e.tensor_scalar(dd_t[:, :], dd_t[:, :], -1.0, None, OP.mult),
        ["dd"], ["dd"])
    act(lambda e: e.activation(dd_t[:, :], dd_t[:, :], AF.Exp), ["dd"], ["dd"])

    dve(lambda e: e.memset(xsh_t[:, bnd_s], 0.0), [], ["xsh_b"])
    dve(lambda e: e.memset(Asc_t[:, bnd_s], 0.0), [],
        [f"A{i}b" for i in range(CC)])
    dve(lambda e: e.memset(Bsc_t[:, bnd_s], 0.0), [],
        [f"B{i}b" for i in range(CC)])

    pe(lambda e: mm_contract(e, lambda i: wrt_t[:, i:i + 1], onescol_t,
                             PSTS[0:1, 0:1], CC, lambda i: slice(0, 1)),
       ["wrt", "onescol"], ["PSTs"], "swr")
    dve(lambda e: e.tensor_copy(bis_t[0:1, 8:9], PSTS[0:1, 0:1]),
        ["PSTs"], ["swr"])

    def cpart_dram(h, g):
        return h[:, g * G:(g + 1) * G].rearrange("(cc p) j -> p cc j", p=128)

    def cpart_sbuf(ap):
        return ap.rearrange("p (cc j) -> p cc j", cc=CC)

    def load_xT(g):
        b = g % 3
        dma("SP", cpart_sbuf(xTg[b][:, :]), cpart_dram(xT_h, g),
            [], [f"xTg{b}"], f"ldx{g}")

    def stage_S1(g):
        buf = g % 3
        par = g % 2
        xg = f"xTg{buf}"
        xk_t, xv_t, xr_t = xk_p[par], xv_p[par], xr_p[par]
        if g > 0:
            dve(lambda e: e.tensor_copy(xsh_t[:, bnd_s], xsh_t[:, lst_s]),
                [f"xsh{i}" for i in range(CC)], ["xsh_b"])
        act(lambda e, b=buf: e.activation(SCR[:, :], xTg[b][:, :], AF.Square),
            [xg, "SCR"], ["SCR"], "sqa")
        pe(lambda e, b=buf: stat_f32(e, xTg[b], PSTs),
           [xg, "onescol", "PSTs"], ["PSTs"], "sum_a")
        pe(lambda e: stat_bf(e, SCR, PSTq),
           ["SCR", "onescolb", "PSTq"], ["PSTq"], "sq_a")
        rowchain("ln_a")
        bcast_pair(tag="bc_a")

        for cci in range(CC):
            dve(lambda e, cc=cci, b=buf: e.tensor_tensor(
                xsh_t[:, cb_s(cc)], xTg[b][:, cc_s(cc)], PB0[:, 0:G], OP.mult),
                [xg, "PB0"], [f"xsh{cci}"])
        for cci in range(CC):
            dve(lambda e, cc=cci: e.tensor_tensor(
                xsh_t[:, cb_s(cc)], xsh_t[:, cb_s(cc)], PB1[:, 0:G],
                OP.subtract), [f"xsh{cci}", "PB1"], [f"xsh{cci}"])

        for cci in range(CC):
            dve(lambda e, cc=cci, t=xk_t: e.tensor_tensor(
                t[:, cc_s(cc)], xsh_t[:, cb_s(cc)], xsh_t[:, cs_s(cc)],
                OP.subtract), [f"xsh{cci}", "xsh_b"], [f"xk{par}_{cci}"])
        for tname, tile, tmname, tmcol in [("xr", xr_t, "tmr", tmr_t),
                                           ("xv", xv_t, "tmv", tmv_t)]:
            for cci in range(CC):
                dve(lambda e, cc=cci, tl=tile, tm=tmcol, kk=xk_t:
                    e.scalar_tensor_tensor(
                        tl[:, cc_s(cc)], kk[:, cc_s(cc)], tm[:, cc:cc + 1],
                        xsh_t[:, cs_s(cc)], OP.mult, OP.add),
                    [f"xk{par}_{cci}", tmname, f"xsh{cci}", "xsh_b"],
                    [f"{tname}{par}_{cci}"])
        for cci in range(CC):
            dve(lambda e, cc=cci, kk=xk_t: e.scalar_tensor_tensor(
                kk[:, cc_s(cc)], kk[:, cc_s(cc)], tmk_t[:, cc:cc + 1],
                xsh_t[:, cs_s(cc)], OP.mult, OP.add),
                [f"xk{par}_{cci}", "tmk", f"xsh{cci}", "xsh_b"],
                [f"xk{par}_{cci}"])

    def stage_S2(g):
        par = g % 2
        EK_t, EKV_t, sig_t = EK_p[par], EKV_p[par], sig_p[par]
        xk_t, xv_t, xr_t = xk_p[par], xv_p[par], xr_p[par]
        XK = [f"xk{par}_{i}" for i in range(CC)]
        XV = [f"xv{par}_{i}" for i in range(CC)]
        XR = [f"xr{par}_{i}" for i in range(CC)]
        for cco in range(CC):
            pk, pkn = next_p()
            pv, pvn = next_p()
            pe(lambda e, p=pk, co=cco, t=xk_t: mm_contract(
                e, lambda i, co_=co: attnW(0, i)[:, co_ * 128:(co_ + 1) * 128],
                t, p[:, 0:G], CC, cc_s),
               XK + [f"aw0_{i}" for i in range(CC)], [pkn], f"km{g}_{cco}")
            act(lambda e, p=pk, cc=cco, t=EK_t: e.activation(
                t[:, cc_s(cc)], p[:, 0:G], AF.Exp), [pkn], [f"EK{par}_{cco}"])
            pe(lambda e, p=pv, co=cco, t=xv_t: mm_contract(
                e, lambda i, co_=co: attnW(1, i)[:, co_ * 128:(co_ + 1) * 128],
                t, p[:, 0:G], CC, cc_s),
               XV + [f"aw1_{i}" for i in range(CC)], [pvn], f"vm{g}_{cco}")
            dve(lambda e, p=pv, cc=cco, ek=EK_t, ev=EKV_t: e.tensor_tensor(
                ev[:, cc_s(cc)], ek[:, cc_s(cc)], p[:, 0:G], OP.mult),
                [f"EK{par}_{cco}", pvn], [f"EKV{par}_{cco}"])
        for cco in range(CC):
            pr, prn = next_p()
            pe(lambda e, p=pr, co=cco, t=xr_t: mm_contract(
                e, lambda i, co_=co: attnW(2, i)[:, co_ * 128:(co_ + 1) * 128],
                t, p[:, 0:G], CC, cc_s),
               XR + [f"aw2_{i}" for i in range(CC)], [prn], f"rm{g}_{cco}")
            act(lambda e, p=pr, cc=cco, t=sig_t: e.activation(
                t[:, cc_s(cc)], p[:, 0:G], AF.Sigmoid),
                [prn], [f"sig{par}_{cco}"])

    def stage_S3(g):
        par = g % 2
        EK_t, EKV_t, sig_t = EK_p[par], EKV_p[par], sig_p[par]
        ekn = lambda cc: f"EK{par}_{cc}"
        evn = lambda cc: f"EKV{par}_{cc}"
        for cci in range(CC):
            dve(lambda e, cc=cci, ev=EKV_t: e.tensor_tensor_scan(
                Asc_t[:, cb_s(cc)], dd_t[:, cc:cc + 1].broadcast_to([128, G]),
                ev[:, cc_s(cc)], Asc_t[:, cc * 257:cc * 257 + 1],
                OP.mult, OP.add),
                [evn(cci), "dd", f"A{cci}b"], [f"A{cci}"])
            dve(lambda e, cc=cci, ek=EK_t: e.tensor_tensor_scan(
                Bsc_t[:, cb_s(cc)], dd_t[:, cc:cc + 1].broadcast_to([128, G]),
                ek[:, cc_s(cc)], Bsc_t[:, cc * 257:cc * 257 + 1],
                OP.mult, OP.add),
                [ekn(cci), "dd", f"B{cci}b"], [f"B{cci}"])

        for cci in range(CC):
            dve(lambda e, cc=cci, ev=EKV_t: e.scalar_tensor_tensor(
                ev[:, cc_s(cc)], Asc_t[:, cs_s(cc)], eu_t[:, cc:cc + 1],
                ev[:, cc_s(cc)], OP.mult, OP.add),
                [f"A{cci}", f"A{cci}b", "eu", evn(cci)], [evn(cci)])
            dve(lambda e, cc=cci, ek=EK_t: e.scalar_tensor_tensor(
                ek[:, cc_s(cc)], Bsc_t[:, cs_s(cc)], eu_t[:, cc:cc + 1],
                ek[:, cc_s(cc)], OP.mult, OP.add),
                [f"B{cci}", f"B{cci}b", "eu", ekn(cci)], [ekn(cci)])
        dve(lambda e: e.tensor_copy(Asc_t[:, bnd_s], Asc_t[:, lst_s]),
            [f"A{i}" for i in range(CC)], [f"A{i}b" for i in range(CC)])
        dve(lambda e: e.tensor_copy(Bsc_t[:, bnd_s], Bsc_t[:, lst_s]),
            [f"B{i}" for i in range(CC)], [f"B{i}b" for i in range(CC)])
        EKall = [ekn(i) for i in range(CC)]
        EVall = [evn(i) for i in range(CC)]
        act(lambda e, ek=EK_t: e.activation(ek[:, :], ek[:, :], AF.Ln),
            EKall, EKall, "invD")
        act(lambda e, ek=EK_t: e.activation(ek[:, :], ek[:, :], AF.Exp,
                                            scale=-1.0), EKall, EKall, "invD")
        dve(lambda e, ek=EK_t, ev=EKV_t: e.tensor_tensor(
            ev[:, :], ev[:, :], ek[:, :], OP.mult),
            EKall + EVall, EVall, "ymul")
        dve(lambda e, ev=EKV_t, sg=sig_t: e.tensor_tensor(
            sg[:, :], ev[:, :], sg[:, :], OP.mult),
            EVall + [f"sig{par}_{i}" for i in range(CC)],
            [f"ry{par}"], "rymul")

    def stage_S4(g):
        buf = g % 3
        par = g % 2
        xg = f"xTg{buf}"
        sig_t = sig_p[par]
        for cco in range(CC):
            po, pon = next_pw()
            pe(lambda e, p=po, co=cco, t=sig_t: mm_contract(
                e, lambda i, co_=co: attnW(3, i)[:, co_ * 128:(co_ + 1) * 128],
                t, p[:, 0:G], CC, cc_s),
               [f"ry{par}"] + [f"sig{par}_{i}" for i in range(CC)] +
               [f"aw3_{i}" for i in range(CC)],
               [pon], f"om{g}_{cco}")
            act(lambda e, p=po, cc=cco: e.activation(
                ATX[:, cc_s(cc)], p[:, 0:G], AF.Copy), [pon],
                [f"at{cco}", f"x1{cco}"])

        ATall = [f"at{i}" for i in range(CC)]
        act(lambda e: e.activation(SCR[:, :], ATX[:, :], AF.Square),
            ATall + ["SCR"], ["SCR"], "sqb")
        pe(lambda e: stat_f32(e, ATX, PSTs),
           ATall + ["onescol", "PSTs"], ["PSTs"], "sum_b")
        pe(lambda e: stat_bf(e, SCR, PSTq),
           ["SCR", "onescolb", "PSTq"], ["PSTq"], "sq_b")
        rowchain("ln_b")
        bcast_pair(tag="bc_b")
        for cci in range(CC):
            dve(lambda e, cc=cci: e.tensor_tensor(
                ATX[:, cc_s(cc)], ATX[:, cc_s(cc)], PB0[:, 0:G],
                OP.mult), [f"at{cci}", "PB0"], [f"x1{cci}", f"at{cci}"])
        for cci in range(CC):
            dve(lambda e, cc=cci: e.tensor_tensor(
                ATX[:, cc_s(cc)], ATX[:, cc_s(cc)], PB1[:, 0:G],
                OP.subtract), [f"x1{cci}", "PB1"], [f"x1{cci}"])
        for cci in range(CC):
            dve(lambda e, cc=cci, b=buf: e.tensor_tensor(
                ATX[:, cc_s(cc)], ATX[:, cc_s(cc)], xTg[b][:, cc_s(cc)],
                OP.add), [f"x1{cci}", xg], [f"x1{cci}"])

        X1ALL = [f"x1{i}" for i in range(CC)]
        act(lambda e: e.activation(SCR[:, :], ATX[:, :], AF.Square),
            X1ALL + ["SCR"], ["SCR"], "sqc")
        pe(lambda e: stat_f32(e, ATX, PSTs),
           X1ALL + ["onescol", "PSTs"], ["PSTs"], "sum_c")
        pe(lambda e: stat_bf(e, SCR, PSTq),
           ["SCR", "onescolb", "PSTq"], ["PSTq"], "sq_c")
        rowchain("ln_c")
        pe(lambda e: mm_contract(e, lambda i: wrt_t[:, i:i + 1], ATX,
                                 PSTs, CC, cc_s),
           X1ALL + ["wrt"], ["PSTs"], "smm")
        dve(lambda e: e.tensor_scalar(row(RW_T1), row(RW_M), bis_t[0:1, 8:9],
                                      None, OP.mult), ["rm", "swr"], ["rt1"])
        dve(lambda e: e.tensor_tensor(row(RW_T1), PSTs,
                                      row(RW_T1), OP.subtract),
            ["PSTs", "rt1"], ["rt1"])
        dve(lambda e: e.tensor_tensor(row(RW_S), row(RW_T1), row(RW_RSTD),
                                      OP.mult), ["rt1", "rrstd"], ["rs"])
        dma("GP", srow_h[0:1, g * G:(g + 1) * G], row(RW_S), ["rs"],
            [f"srow{g}"])
        dma("GP", lncr_h[0:1, g * G:(g + 1) * G], row(RW_RSTD), ["rrstd"],
            [f"lncr{g}"])
        dma("GP", lncm_h[0:1, g * G:(g + 1) * G], row(RW_MR), ["rmr"],
            [f"lncm{g}"])

        dma("SP", cpart_dram(x1T_h, g), cpart_sbuf(ATX[:, :]),
            X1ALL, [f"x1st{g}"])
        for h in range(2):
            for cc in range(CC):
                pt, ptn = next_pw()
                pe(lambda e, p=pt, hh=h, c=cc: e.transpose(
                    p[:, 0:128],
                    ATX[:, c * G + hh * 128:c * G + (hh + 1) * 128],
                    ident_t[:, :]),
                   [f"x1{cc}", "ident"], [ptn], f"x1t{g}_{h}_{cc}")
                act(lambda e, p=pt, hh=h, c=cc: e.activation(
                    SCR[:, hh * 1024 + c * 128:hh * 1024 + (c + 1) * 128],
                    p[:, 0:128], AF.Copy),
                    [ptn, "SCR"], ["SCR"], f"x1e{g}_{h}_{cc}")
        dma("SP", x1rows_h[g * G:(g + 1) * G, :].rearrange(
            "(h p) c -> p h c", p=128),
            SCR[:, :].rearrange("p (h c) -> p h c", h=2),
            ["SCR"], ["x1rows"], f"x1r{g}")

    for i in range(-2, NG):
        if i + 2 < NG:
            load_xT(i + 2)
        if 0 <= i + 1 < NG:
            stage_S2(i + 1)
        if i >= 0:
            stage_S3(i)
        if i + 2 < NG:
            stage_S1(i + 2)
        if i >= 0:
            stage_S4(i)

    attn_end = S.snapshot()
    S.fence("GP", attn_end)
    for cci in range(CC):
        dma("GP", w1W(cci), w1_h[cci * 128:(cci + 1) * 128, :], [], [f"w1_{cci}"])
    for fi in range(FC):
        dma("GP", w2W(fi), w2_h[fi * 128:(fi + 1) * 128, :], [], [f"w2_{fi}"])

    dma("SP", wmtr_t[:, :], srow_h.rearrange("a (c p) -> (a c) p", p=128),
        [f"srow{gg}" for gg in range(NG)], ["wmtr"])
    pe(lambda e: e.transpose(PB0[:, 0:64], wmtr_t[:, :], ident_t[0:64, 0:64]),
       ["wmtr", "ident"], ["PB0"], "str")
    dve(lambda e: e.tensor_copy(scoreT_t[:, :], PB0[:, 0:64]),
        ["PB0"], ["scoreT"])

    dve(lambda e: e.reduce_max(geT_t[:, 0:1], scoreT_t[:, :], axis=AX.X,
                               apply_absolute_value=True),
        ["scoreT"], ["geT"])
    pe(lambda e: e.matmul(PSTS[0:1, 0:1], geT_t[:, 0:1], onescol_t[:, 0:1],
                          start=True, stop=True),
       ["geT", "onescol"], ["PSTs"], "bsum")
    dve(lambda e: e.tensor_scalar(bis_t[0:1, 1:2], PSTS[0:1, 0:1], 1.0, 1.0,
                                  OP.mult, OP.add), ["PSTs"], ["hi"])
    dve(lambda e: e.tensor_scalar(bis_t[0:1, 0:1], bis_t[0:1, 1:2], -1.0,
                                  None, OP.mult), ["hi"], ["lo"])
    dve(lambda e: e.tensor_tensor(bis_t[0:1, 2:3], bis_t[0:1, 0:1],
                                  bis_t[0:1, 1:2], OP.add), ["lo", "hi"], ["th"])
    dve(lambda e: e.tensor_scalar(bis_t[0:1, 2:3], bis_t[0:1, 2:3], 0.5, None,
                                  OP.mult), ["th"], ["th"])

    for it in range(N_BIS):
        pe(lambda e: e.matmul(PB0[:, 0:1], onesrow_t[:, :], bis_t[0:1, 2:3],
                              start=True, stop=True),
           ["th", "onesrow", "PB0"], ["PB0"], f"bth{it}")
        dve(lambda e: e.tensor_scalar(geT_t[:, :], scoreT_t[:, :],
                                      PB0[:, 0:1], None, OP.is_ge),
            ["scoreT", "PB0"], ["geT"])
        dve(lambda e: e.reduce_sum(sigT_t[:, 0:1], geT_t[:, :], axis=AX.X),
            ["geT"], ["sigT"])
        pe(lambda e: e.matmul(PSTS[0:1, 0:1], sigT_t[:, 0:1],
                              onescol_t[:, 0:1], start=True, stop=True),
           ["sigT", "onescol", "PSTs"], ["PSTs"], f"bcnt{it}")
        dve(lambda e: e.tensor_scalar(bis_t[0:1, 3:4], PSTS[0:1, 0:1],
                                      float(KSEL), None, OP.is_ge),
            ["PSTs"], ["flag"])
        dve(lambda e: e.tensor_tensor(bis_t[0:1, 4:5], bis_t[0:1, 2:3],
                                      bis_t[0:1, 0:1], OP.subtract),
            ["th", "lo"], ["t4"])
        dve(lambda e: e.tensor_tensor(bis_t[0:1, 4:5], bis_t[0:1, 4:5],
                                      bis_t[0:1, 3:4], OP.mult),
            ["t4", "flag"], ["t4"])
        dve(lambda e: e.tensor_tensor(bis_t[0:1, 0:1], bis_t[0:1, 0:1],
                                      bis_t[0:1, 4:5], OP.add),
            ["lo", "t4"], ["lo"])
        dve(lambda e: e.tensor_tensor(bis_t[0:1, 5:6], bis_t[0:1, 1:2],
                                      bis_t[0:1, 2:3], OP.subtract),
            ["hi", "th"], ["t5"])
        dve(lambda e: e.tensor_tensor(bis_t[0:1, 5:6], bis_t[0:1, 5:6],
                                      bis_t[0:1, 3:4], OP.mult),
            ["t5", "flag"], ["t5"])
        dve(lambda e: e.tensor_tensor(bis_t[0:1, 1:2], bis_t[0:1, 2:3],
                                      bis_t[0:1, 5:6], OP.add),
            ["th", "t5"], ["hi"])
        dve(lambda e: e.tensor_tensor(bis_t[0:1, 2:3], bis_t[0:1, 0:1],
                                      bis_t[0:1, 1:2], OP.add),
            ["lo", "hi"], ["th"])
        dve(lambda e: e.tensor_scalar(bis_t[0:1, 2:3], bis_t[0:1, 2:3], 0.5,
                                      None, OP.mult), ["th"], ["th"])

    pe(lambda e: e.matmul(PB0[:, 0:1], onesrow_t[:, :], bis_t[0:1, 0:1],
                          start=True, stop=True),
       ["lo", "onesrow", "PB0"], ["PB0"], "bfin")
    dve(lambda e: e.tensor_scalar(geT_t[:, :], scoreT_t[:, :], PB0[:, 0:1],
                                  None, OP.is_ge), ["scoreT", "PB0"], ["geT"])
    act(lambda e: e.activation(sigT_t[:, :], scoreT_t[:, :], AF.Sigmoid),
        ["scoreT", "sigT"], ["sigT"])
    dve(lambda e: e.tensor_tensor(sigT_t[:, :], geT_t[:, :], sigT_t[:, :],
                                  OP.mult), ["geT", "sigT"], ["sigT"])
    pe(lambda e: e.transpose(PB1[0:64, 0:128], sigT_t[:, :], ident_t[:, :]),
       ["sigT", "ident", "PB1"], ["PB1"], "wtr")
    dve(lambda e: e.tensor_copy(wmtr_t[:, :], PB1[0:64, 0:128]),
        ["PB1"], ["wmtr"])
    dma("GP", wmrow_h.rearrange("a (c p) -> (a c) p", p=128), wmtr_t[:, :],
        ["wmtr"], ["wmrow"])

    pe(lambda e: e.matmul(PSTS[0:1, 0:64], onescol_t[:, :], geT_t[:, :],
                          start=True, stop=True),
       ["geT", "onescol", "PSTs"], ["PSTs"], "csum")
    dve(lambda e: e.tensor_copy(csB_t[:, :], PSTS[0:1, 0:64]), ["PSTs"], ["csB"])
    pe(lambda e: e.matmul(PB0[0:64, 0:128], csB_t[:, :], onesrow_t[:, :],
                          start=True, stop=True),
       ["csB", "onesrow", "PB0"], ["PB0"], "csbc")
    dve(lambda e: e.tensor_copy(csC_t[:, :], PB0[0:64, 0:128]), ["PB0"], ["csC"])
    pe(lambda e: (e.matmul(PB1[:, 0:64], triu_t[:, :], geT_t[:, :],
                           start=True, stop=False),
                  e.matmul(PB1[:, 0:64], csC_t[:, :], triu_t[0:64, 0:64],
                           start=False, stop=True))[-1],
       ["triu", "geT", "csC", "PB1"], ["PB1"], "posmm")
    dve(lambda e: e.tensor_scalar(sigT_t[:, :], PB1[:, 0:64], 1.0, None,
                                  OP.add), ["PB1"], ["sigT"])
    dve(lambda e: e.tensor_tensor(geT_t[:, :], geT_t[:, :], sigT_t[:, :],
                                  OP.mult), ["geT", "sigT"], ["geT"])
    dve(lambda e: e.tensor_scalar(sigT_t[:, :], geT_t[:, :], 0.0, None,
                                  OP.is_equal), ["geT"], ["sigT"])
    dve(lambda e: e.scalar_tensor_tensor(geT_t[:, :], sigT_t[:, :], 8192.0,
                                         geT_t[:, :], OP.mult, OP.add),
        ["geT", "sigT"], ["geT"])
    dve(lambda e: e.tensor_scalar(geT_t[:, :], geT_t[:, :], -1.0, None,
                                  OP.add), ["geT"], ["geT"])
    dve(lambda e: e.tensor_copy(posI_t[:, :], geT_t[:, :]), ["geT"], ["posI"])
    for c in range(64):
        S.dma_op("GP", lambda e, cc=c: e.indirect_dma_start(
            idxb_h[0:T, :],
            bass.IndirectOffsetOnAxis(posI_t[:, cc:cc + 1], 0),
            iota_t[:, cc:cc + 1], None),
            ["posI", "iota"], ["idxb"], f"isc{c}", drain=True)

    bis_end = S.snapshot()
    for e in ("DVE", "ACT", "PE", "SP"):
        S.fence(e, bis_end)

    NSG = KSEL // G
    fx1Tg = carve(AR2, 0, 2048, F32)
    x1g_p = [carve(AR2, 8192, 1024, BF16),
             carve(AR2, 10240, 1024, BF16),
             carve(AR2, 12288, 1024, BF16),
             carve(AR2, 14336, 1024, BF16)]
    hbf_t = carve(AR2, 16384, 2048, BF16)
    gar_t = carve(AR2, 20480, FC * G, BF16)
    hu_t = carve(AR2, 36864, 2048, F32)
    SQf_t = carve(AR2, 45056, 2048, BF16)
    x2T_t = carve(AR2, 49152, 2048, F32)
    ubf_t = carve(AR2, 61440, 2048, BF16)

    def w1mm(e, pr_, co0):
        last = None
        for half in range(2):
            for cci in range(CC):
                last = e.matmul(
                    pr_[:, half * G:(half + 1) * G],
                    w1W(cci)[:, (co0 + half) * 128:(co0 + half + 1) * 128],
                    hbf_t[:, cc_s(cci)],
                    start=(cci == 0), stop=(cci == CC - 1))
        return last

    def w2mm(e, pr_, cco_):
        last = None
        for fi in range(FC):
            last = e.matmul(pr_[:, 0:G],
                            w2W(fi)[:, cco_ * 128:(cco_ + 1) * 128],
                            gar_t[:, fi * G:(fi + 1) * G],
                            start=(fi == 0), stop=(fi == FC - 1))
        return last

    def stage_Cf(s):
        sp = s % 2
        for h in range(2):
            dma("SP", idxc_t[:, sp * 2 + h:sp * 2 + h + 1],
                idxb_h[s * G + h * 128:s * G + (h + 1) * 128, :],
                ["idxb", f"idxc{sp}"], [f"idxc{sp}"], f"lix{s}_{h}",
                drain=(h == 1))
        for h in range(2):
            xb = x1g_p[2 * sp + h]
            xbn = f"x1g{2 * sp + h}"
            S.dma_op("GP", lambda e, o=xb, c=sp * 2 + h: e.indirect_dma_start(
                o[:, :], None, x1rows_h[0:T, :],
                bass.IndirectOffsetOnAxis(idxc_t[:, c:c + 1], 0)),
                ["x1rows", f"idxc{sp}", xbn], [xbn], f"gx{s}_{h}")
            dve(lambda e, o=xb: e.reduce_sum(fcol_t[:, 0:1], o[:, :],
                                             axis=AX.X), [xbn], ["fc0"])
            act(lambda e, o=xb: e.activation(ubf_t[:, 0:1024], o[:, :],
                                             AF.Square), [xbn, "ubfR"],
                ["ubfR"])
            dve(lambda e: e.reduce_sum(fcol_t[:, 1:2], ubf_t[:, 0:1024],
                                       axis=AX.X), ["ubfR"], ["fc1"])
            dve(lambda e: e.tensor_scalar(fcol_t[:, 0:1], fcol_t[:, 0:1],
                                          -1.0 / C, None, OP.mult),
                ["fc0"], ["fc0"])
            dve(lambda e: e.tensor_tensor(fcol_t[:, 2:3], fcol_t[:, 0:1],
                                          fcol_t[:, 0:1], OP.mult),
                ["fc0"], ["fc2"])
            dve(lambda e: e.scalar_tensor_tensor(fcol_t[:, 1:2],
                                                 fcol_t[:, 1:2], 1.0 / C,
                                                 fcol_t[:, 2:3],
                                                 OP.mult, OP.subtract),
                ["fc1", "fc2"], ["fc1"])
            act(lambda e: e.activation(fcol_t[:, 1:2], fcol_t[:, 1:2], AF.Ln,
                                       bias=cst_t[:, 1:2]),
                ["fc1", "cst"], ["fc1"])
            act(lambda e: e.activation(fcol_t[:, 1:2], fcol_t[:, 1:2], AF.Exp,
                                       scale=-0.5), ["fc1"], ["fc1"])
            dve(lambda e, o=xb: e.tensor_scalar(o[:, :], o[:, :],
                                                fcol_t[:, 0:1],
                                                fcol_t[:, 1:2],
                                                OP.add, OP.mult),
                [xbn, "fc0", "fc1"], [xbn])
        for h in range(2):
            xb = x1g_p[2 * sp + h]
            xbn = f"x1g{2 * sp + h}"
            for cc in range(CC):
                dma("SP", hbf_t[:, cc * G + h * 128:cc * G + (h + 1) * 128],
                    xb[:, cc * 128:(cc + 1) * 128],
                    [xbn, "hbR"], ["hbR"], f"ht{s}_{h}_{cc}",
                    transpose=True, drain=(h == 1 and cc == CC - 1))

    def stage_Df(s):
        for pair in range(FC // 2):
            pA, pAn = next_p()
            pe(lambda e, p=pA, co0=2 * pair, f=w1mm: f(e, p, co0),
               ["hbR"] + [f"w1_{i}" for i in range(CC)],
               [pAn], f"w1m{s}_{pair}")
            act(lambda e, p=pA, pr_=pair: e.activation(
                gar_t[:, 2 * pr_ * G:(2 * pr_ + 1) * G], p[:, 0:G],
                AF.Gelu_apprx_tanh, bias=b1_t[:, 2 * pr_:2 * pr_ + 1]),
                [pAn, "b1", "garR"], ["garR"])
            act(lambda e, p=pA, pr_=pair: e.activation(
                gar_t[:, (2 * pr_ + 1) * G:(2 * pr_ + 2) * G], p[:, G:2 * G],
                AF.Gelu_apprx_tanh, bias=b1_t[:, 2 * pr_ + 1:2 * pr_ + 2]),
                [pAn, "b1", "garR"], ["garR"])

    def stage_Af(s):
        sp = s % 2
        for cco in range(CC):
            pu, pun = next_pw()
            pe(lambda e, p=pu, co=cco, f=w2mm: f(e, p, co),
               ["garR"] + [f"w2_{i}" for i in range(FC)], [pun],
               f"w2m{s}_{cco}")
            act(lambda e, p=pu, cc=cco: e.activation(
                ubf_t[:, cc_s(cc)], p[:, 0:G], AF.Identity,
                bias=b2_t[:, cc:cc + 1]), [pun, "b2", "ubfR"], ["ubfR"])
        for h in range(2):
            for cc in range(CC):
                dma("SP", SQf_t[:, h * 1024 + cc * 128:h * 1024 + (cc + 1) * 128],
                    ubf_t[:, cc * G + h * 128:cc * G + (h + 1) * 128],
                    ["ubfR", "SQfR"], ["SQfR"], f"ut{s}_{h}_{cc}",
                    transpose=True, drain=(h == 1 and cc == CC - 1))
        for h in range(2):
            S.dma_op("GP", lambda e, hh=h, c=sp * 2 + h: e.indirect_dma_start(
                ufull_h[0:T, :],
                bass.IndirectOffsetOnAxis(idxc_t[:, c:c + 1], 0),
                SQf_t[:, hh * 1024:(hh + 1) * 1024], None),
                ["SQfR", f"idxc{sp}", "ufull"], ["ufull"], f"us{s}_{h}",
                drain=True)

    def load_fx1(g):
        dma("SP", cpart_sbuf(fx1Tg[:, :]), cpart_dram(x1T_h, g),
            [f"x1st{g}", "fx1"], ["fx1"], f"lf{g}")

    def load_wm(g, b):
        dma("GP", wmr_t[0:1, b * G:(b + 1) * G],
            wmrow_h[0:1, g * G:(g + 1) * G], ["wmrow", f"wmr{b}"],
            [f"wmr{b}"], f"lwm{g}")

    def load_lnc(g):
        dma("GP", row(RW_RSTD), lncr_h[0:1, g * G:(g + 1) * G],
            [f"lncr{g}", "rrstd"], ["rrstd"], f"llr{g}")
        dma("GP", row(RW_MR), lncm_h[0:1, g * G:(g + 1) * G],
            [f"lncm{g}", "rmr"], ["rmr"], f"llm{g}")

    def stage_F(g):
        fb = g % 2
        for cc in range(CC):
            dma("SP", gar_t[:, fb * 2048 + cc * G:fb * 2048 + (cc + 1) * G],
                ufull_h[g * G:(g + 1) * G, cc * 128:(cc + 1) * 128],
                ["ufull", "garR"], ["garR"], f"uT{g}_{cc}",
                transpose=True, drain=(cc == CC - 1))
        load_lnc(g)
        pe(lambda e, b=fb: e.matmul(PSTS[:, 256:256 + G], onesrow_t[:, :],
                                    wmr_t[0:1, b * G:(b + 1) * G],
                                    start=True, stop=True),
           [f"wmr{fb}", "onesrow", "PSTs"], ["PSTs"], "fwmb")
        dve(lambda e: e.tensor_copy(wmb2_t[:, :], PSTS[:, 256:256 + G]),
            ["PSTs"], ["wmb"])
        bcast_pair(extra_reads=("wmb",), tag="bc_f")
        for cci in range(CC):
            dve(lambda e, cc=cci: e.tensor_tensor(
                hu_t[:, cc_s(cc)], fx1Tg[:, cc_s(cc)], PB0[:, 0:G],
                OP.mult), ["fx1", "PB0"], ["huR"])
            dve(lambda e, cc=cci: e.tensor_tensor(
                hu_t[:, cc_s(cc)], hu_t[:, cc_s(cc)], PB1[:, 0:G],
                OP.subtract), ["huR", "PB1"], ["huR"])
            dve(lambda e, cc=cci, b=fb: e.tensor_tensor(
                x2T_t[:, cc_s(cc)],
                gar_t[:, b * 2048 + cc * G:b * 2048 + (cc + 1) * G],
                wmb2_t[:, :], OP.mult), ["garR", "wmb"], ["x2R"])
            dve(lambda e, cc=cci: e.tensor_tensor(
                hu_t[:, cc_s(cc)], hu_t[:, cc_s(cc)], x2T_t[:, cc_s(cc)],
                OP.add), ["huR", "x2R"], ["huR"])
        act(lambda e: e.activation(SQf_t[:, :], hu_t[:, :], AF.Square),
            ["huR", "SQfR"], ["SQfR"], "sqd")
        pe(lambda e: stat_f32(e, hu_t, PSTs),
           ["huR", "onescol", "PSTs"], ["PSTs"], "sum_d")
        pe(lambda e: stat_bf(e, SQf_t, PSTq),
           ["SQfR", "onescolb", "PSTq"], ["PSTq"], "sq_d")
        rowchain("ln_d")
        bcast_pair(tag="bc_d")
        for cci in range(CC):
            dve(lambda e, cc=cci: e.tensor_tensor(
                x2T_t[:, cc_s(cc)], hu_t[:, cc_s(cc)], PB0[:, 0:G], OP.mult),
                ["huR", "PB0"], ["x2R"])
            dve(lambda e, cc=cci: e.tensor_tensor(
                x2T_t[:, cc_s(cc)], x2T_t[:, cc_s(cc)], PB1[:, 0:G],
                OP.subtract), ["x2R", "PB1"], ["x2R"])
            dve(lambda e, cc=cci: e.tensor_tensor(
                x2T_t[:, cc_s(cc)], x2T_t[:, cc_s(cc)], fx1Tg[:, cc_s(cc)],
                OP.add), ["x2R", "fx1"], ["x2R"])
        dma("SP", cpart_dram(xo_h, g), cpart_sbuf(x2T_t[:, :]),
            ["x2R"], [f"xost{g}"])

    load_wm(0, 0)
    stage_Cf(0)
    stage_Df(0)
    for s in range(NSG):
        load_fx1(s)
        stage_Af(s)
        if s + 1 < NSG:
            stage_Cf(s + 1)
        stage_F(s)
        if s + 1 < NSG:
            load_wm(s + 1, (s + 1) % 2)
            stage_Df(s + 1)
    for g in range(NSG, NG):
        load_fx1(g)
        load_wm(g, g % 2)
        stage_F(g)

    final = S.snapshot()
    S.fence("SP", {s: final[s] for s in ("SP", "GP", "SPX", "GPX")})

    with nc.Block() as blk:
        def emitter(name):
            def run(eng):
                for waits, fn, inc, sem, tag in S.ops[name]:
                    for (de, dv) in waits:
                        eng.wait_ge(sems[de], dv)
                    if fn is not None:
                        inst = fn(eng)
                        if inc:
                            inst.then_inc(sems[sem], inc)
            return run

        blk.tensor(emitter("PE"))
        blk.vector(emitter("DVE"))
        blk.scalar(emitter("ACT"))
        blk.sync(emitter("SP"))
        blk.gpsimd(emitter("GP"))

    stack.close()
    return nc


_NC_CACHE = {}


def kernel(L0_attn_x, L0_attn_time_decay, L0_attn_time_first,
           L0_attn_time_mix_k, L0_attn_time_mix_v, L0_attn_time_mix_r,
           L0_attn_Wk, L0_attn_Wv, L0_attn_Wr, L0_attn_Wo,
           L0_ffn_w_router, L0_ffn_W1, L0_ffn_b1, L0_ffn_W2, L0_ffn_b2):
    import ml_dtypes
    x = np.asarray(L0_attn_x, np.float32)
    B = x.shape[0]
    assert B == 8 and x.shape[1] == T and x.shape[2] == C

    colize = lambda v: np.ascontiguousarray(
        np.asarray(v, np.float32).reshape(CC, 128).T)
    cst = np.zeros((128, 4), np.float32)
    cst[:, 0] = 1.0
    cst[:, 1] = LN_EPS
    cst[:, 2] = 1.0 / C

    common = dict(
        wkT=np.ascontiguousarray(np.asarray(L0_attn_Wk, np.float32).T),
        wvT=np.ascontiguousarray(np.asarray(L0_attn_Wv, np.float32).T),
        wrT=np.ascontiguousarray(np.asarray(L0_attn_Wr, np.float32).T),
        woT=np.ascontiguousarray(np.asarray(L0_attn_Wo, np.float32).T),
        w1=np.ascontiguousarray(np.asarray(L0_ffn_W1, np.float32)),
        w2=np.ascontiguousarray(np.asarray(L0_ffn_W2, np.float32)),
        tmk=colize(np.asarray(L0_attn_time_mix_k).reshape(-1)),
        tmv=colize(np.asarray(L0_attn_time_mix_v).reshape(-1)),
        tmr=colize(np.asarray(L0_attn_time_mix_r).reshape(-1)),
        td=colize(L0_attn_time_decay), tf=colize(L0_attn_time_first),
        wrt=colize(L0_ffn_w_router),
        b1=np.ascontiguousarray(np.asarray(L0_ffn_b1, np.float32).reshape(FC, 128).T),
        b2=colize(L0_ffn_b2),
        iota=np.ascontiguousarray(
            (np.arange(64)[None, :] * 128 +
             np.arange(128)[:, None]).astype(np.int32)),
        triu=np.triu(np.ones((128, 128), np.float32), 1),
        onescol=np.ones((128, 1), np.float32),
        onescolb=np.ones((128, 1), ml_dtypes.bfloat16),
        onesrow=np.ones((1, 128), np.float32),
        ident=np.eye(128, dtype=np.float32),
        cst=cst,
    )
    in_maps = []
    for b in range(B):
        m = dict(common)
        m["xT"] = np.ascontiguousarray(x[b].T)
        in_maps.append(m)

    if "v2" not in _NC_CACHE:
        _NC_CACHE["v2"] = build_program()
    nc = _NC_CACHE["v2"]
    res = run_bass_kernel_spmd(nc, in_maps, core_ids=list(range(B)))
    out = np.empty((B, T, C), np.float32)
    for b in range(B):
        out[b] = res.results[b]["xo"].T
    return out


# version 12 (final)
# speedup vs baseline: 1.0546x; 1.0546x over previous
import numpy as np
import concourse.bass as bass
import concourse.mybir as mybir
from concourse.bass_utils import run_bass_kernel_spmd
from concourse.alu_op_type import AluOpType as OP

F32 = mybir.dt.float32
BF16 = mybir.dt.bfloat16
AF = mybir.ActivationFunctionType
AX = mybir.AxisListType

C = 1024
T = 8192
CC = 8
G = 256
NG = T // G
F = 4096
FC = F // 128
KSEL = T // 2
N_BIS = 46
LN_EPS = 1e-5


class Sched:

    ENGINES = ("PE", "DVE", "ACT", "SP", "GP")
    SEMS = ("PE", "DVE", "ACT", "SP", "GP", "SPX", "GPX", "AQ", "AQX")
    QUEUES = {"SP": ("SP", "SP", "SPX"), "GP": ("GP", "GP", "GPX"),
              "AQ": ("ACT", "AQ", "AQX")}

    def __init__(self):
        self.ops = {e: [] for e in self.ENGINES}
        self.cnt = {s: 0 for s in self.SEMS}
        self.buf = {}
        self.mark = {e: {s: 0 for s in self.SEMS} for e in self.ENGINES}

    def _waits_for(self, eng, deps):
        by_e = {}
        for (de, dv) in deps:
            by_e[de] = max(by_e.get(de, 0), dv)
        waits = []
        for de in self.SEMS:
            dv = by_e.get(de, 0)
            if dv > self.mark[eng][de]:
                waits.append((de, dv))
                self.mark[eng][de] = dv
        return waits

    def _deps(self, reads, writes):
        deps = []
        for b in reads:
            st = self.buf.get(b)
            if st and st["w"] is not None:
                deps.append(st["w"])
        for b in writes:
            st = self.buf.get(b)
            if st:
                if st["w"] is not None:
                    deps.append(st["w"])
                deps.extend(st["r"])
        return deps

    def _register(self, reads, writes, sem, val):
        for b in reads:
            self.buf.setdefault(b, {"w": None, "r": []})["r"].append((sem, val))
        for b in writes:
            self.buf[b] = {"w": (sem, val), "r": []}

    def op(self, eng, fn, reads=(), writes=(), tag=""):
        if eng in self.QUEUES:
            return self.dma_op(eng, fn, reads, writes, tag)
        waits = self._waits_for(eng, self._deps(reads, writes))
        self.cnt[eng] += 1
        self._register(reads, writes, eng, self.cnt[eng])
        self.ops[eng].append((waits, fn, 1, eng, tag))

    def dma_op(self, q, fn, reads=(), writes=(), tag="", drain=True):
        stream, sem, semx = self.QUEUES[q]
        waits = self._waits_for(stream, self._deps(reads, writes))
        self.cnt[sem] += 16
        self.ops[stream].append((waits, fn, 16, sem, tag))
        if drain:
            dw = self._waits_for(stream, [(sem, self.cnt[sem])])
            self.cnt[semx] += 1
            self._register(reads, writes, semx, self.cnt[semx])
            self.ops[stream].append((dw, lambda e: e.nop(), 1, semx,
                                     tag + "_dr"))

    def fence(self, eng, counts):
        waits = self._waits_for(eng, [(s, counts.get(s, 0)) for s in self.SEMS])
        if waits:
            self.ops[eng].append((waits, None, 0, eng, "fence"))

    def snapshot(self):
        return dict(self.cnt)


def build_program():
    nc = bass.Bass("TRN2", target_bir_lowering=False)

    xT_h = nc.dram_tensor("xT", [C, T], F32, kind="ExternalInput").ap()
    wkT_h = nc.dram_tensor("wkT", [C, C], F32, kind="ExternalInput").ap()
    wvT_h = nc.dram_tensor("wvT", [C, C], F32, kind="ExternalInput").ap()
    wrT_h = nc.dram_tensor("wrT", [C, C], F32, kind="ExternalInput").ap()
    woT_h = nc.dram_tensor("woT", [C, C], F32, kind="ExternalInput").ap()
    w1_h = nc.dram_tensor("w1", [C, F], F32, kind="ExternalInput").ap()
    w2_h = nc.dram_tensor("w2", [F, C], F32, kind="ExternalInput").ap()
    tmk_h = nc.dram_tensor("tmk", [128, CC], F32, kind="ExternalInput").ap()
    tmv_h = nc.dram_tensor("tmv", [128, CC], F32, kind="ExternalInput").ap()
    tmr_h = nc.dram_tensor("tmr", [128, CC], F32, kind="ExternalInput").ap()
    td_h = nc.dram_tensor("td", [128, CC], F32, kind="ExternalInput").ap()
    tf_h = nc.dram_tensor("tf", [128, CC], F32, kind="ExternalInput").ap()
    wrt_h = nc.dram_tensor("wrt", [128, CC], F32, kind="ExternalInput").ap()
    b1_h = nc.dram_tensor("b1", [128, FC], F32, kind="ExternalInput").ap()
    b2_h = nc.dram_tensor("b2", [128, CC], F32, kind="ExternalInput").ap()
    onescol_h = nc.dram_tensor("onescol", [128, 1], F32, kind="ExternalInput").ap()
    onescolb_h = nc.dram_tensor("onescolb", [128, 1], BF16, kind="ExternalInput").ap()
    onesrow_h = nc.dram_tensor("onesrow", [1, 128], F32, kind="ExternalInput").ap()
    ident_h = nc.dram_tensor("ident", [128, 128], F32, kind="ExternalInput").ap()
    cst_h = nc.dram_tensor("cst", [128, 4], F32, kind="ExternalInput").ap()
    iota_h = nc.dram_tensor("iota", [128, 64], mybir.dt.int32,
                            kind="ExternalInput").ap()
    triu_h = nc.dram_tensor("triu", [128, 128], F32, kind="ExternalInput").ap()
    xo_h = nc.dram_tensor("xo", [C, T], F32, kind="ExternalOutput").ap()
    x1T_h = nc.dram_tensor("x1T", [C, T], F32, kind="Internal").ap()
    srow_h = nc.dram_tensor("srow", [1, T], F32, kind="Internal").ap()
    lncr_h = nc.dram_tensor("lncr", [1, T], F32, kind="Internal").ap()
    lncm_h = nc.dram_tensor("lncm", [1, T], F32, kind="Internal").ap()
    wmrow_h = nc.dram_tensor("wmrow", [1, T], F32, kind="Internal").ap()
    x1rows_h = nc.dram_tensor("x1rows", [T, C], BF16, kind="Internal").ap()
    ufull_h = nc.dram_tensor("ufull", [T, C], BF16, kind="Internal").ap()
    idxb_h = nc.dram_tensor("idxb", [T, 1], mybir.dt.int32, kind="Internal").ap()

    from contextlib import ExitStack
    stack = ExitStack()
    sb = lambda n, s, t: stack.enter_context(nc.sbuf_tensor(n, s, t))
    psm = lambda n, s: stack.enter_context(nc.psum_tensor(n, s, F32))

    AW = sb("AW", [128, 32768], BF16)
    AR1 = sb("AR1", [128, 32768], BF16)
    AR2 = sb("AR2", [128, 32768], BF16)

    ident_t = sb("identt", [128, 128], F32)
    identb_t = sb("identbt", [128, 128], BF16)
    onescol_t = sb("onescolt", [128, 1], F32)
    onescolb_t = sb("onescolbt", [128, 1], BF16)
    onesrow_t = sb("onesrowt", [1, 128], F32)
    cst_t = sb("cstt", [128, 4], F32)
    tmk_t = sb("tmkt", [128, CC], F32)
    tmv_t = sb("tmvt", [128, CC], F32)
    tmr_t = sb("tmrt", [128, CC], F32)
    td_t = sb("tdt", [128, CC], F32)
    tf_t = sb("tft", [128, CC], F32)
    wrt_t = sb("wrtt", [128, CC], F32)
    b1_t = sb("b1t", [128, FC], F32)
    b2_t = sb("b2t", [128, CC], F32)
    eu_t = sb("eut", [128, CC], F32)
    dd_t = sb("ddt", [128, CC], F32)
    rows_t = sb("rowst", [1, 1792], F32)
    scoreT_t = sb("scoreTt", [128, 64], F32)
    geT_t = sb("geTt", [128, 64], F32)
    sigT_t = sb("sigTt", [128, 64], F32)
    bis_t = sb("bist", [1, 16], F32)
    wmtr_t = sb("wmtrt", [64, 128], F32)
    wmr_t = sb("wmrt", [1, 2 * G], F32)
    iota_t = sb("iotat", [128, 64], mybir.dt.int32)
    triu_t = sb("triut", [128, 128], F32)
    posI_t = sb("posIt", [128, 64], mybir.dt.int32)
    idxc_t = sb("idxct", [128, 4], mybir.dt.int32)
    fcol_t = sb("fcolt", [128, 8], F32)
    csB_t = sb("csBt", [1, 64], F32)
    csC_t = sb("csCt", [64, 128], F32)
    wmb2_t = sb("wmb2t", [128, G], BF16)

    def carve(arena, off_bytes, cols, dt):
        el = off_bytes // 2
        w = cols * (2 if dt == F32 else 1)
        ap = arena[:, el:el + w]
        if dt == F32:
            ap = ap.bitcast(F32)
        return ap

    def attnW(m, cci):
        off = (m * CC + cci) * 1024
        return AW[:, off:off + 1024]

    def w2W(fi):
        return AW[:, fi * 1024:(fi + 1) * 1024]

    def w1W(cci):
        return AR1[:, cci * 4096:(cci + 1) * 4096]

    xTg = [carve(AR1, 0, 2048, F32), carve(AR1, 8192, 2048, F32),
           carve(AR1, 16384, 2048, F32)]
    EK_p = [carve(AR1, 24576, 2048, F32), carve(AR1, 32768, 2048, F32)]
    EKV_p = [carve(AR1, 40960, 2048, F32), carve(AR1, 49152, 2048, F32)]
    SCR = carve(AR1, 57344, 2048, BF16)

    xk_p = [carve(AR2, 0, 2048, BF16), carve(AR2, 12288, 2048, BF16)]
    xv_p = [carve(AR2, 4096, 2048, BF16), carve(AR2, 16384, 2048, BF16)]
    xr_p = [carve(AR2, 8192, 2048, BF16), carve(AR2, 20480, 2048, BF16)]
    sig_p = [carve(AR2, 24576, 2048, BF16), carve(AR2, 28672, 2048, BF16)]
    Asc_t = carve(AR2, 32768, CC * 257, F32)
    Bsc_t = carve(AR2, 40992, CC * 257, F32)
    xsh_t = carve(AR2, 49216, CC * 257, BF16)
    ATX = carve(AR2, 53328, 2048, F32)

    P = [psm(f"PS{i}", [128, 512]) for i in range(2)]
    PN = [f"PS{i}" for i in range(2)]
    PW = [psm(f"PW{i}", [128, 512]) for i in range(2)]
    PWN = [f"PW{i}" for i in range(2)]
    PSTS = psm("PSTS", [1, 256])
    PSTQ = psm("PSTQ", [1, 256])
    PB0 = psm("PB0", [128, 512])
    PB1 = psm("PB1", [128, 512])

    sems = {}
    for e in Sched.SEMS:
        sems[e] = stack.enter_context(nc.semaphore("sem" + e))
    S = Sched()

    def dma(q, out, in_, reads, writes, tag="", drain=True, transpose=False):
        S.dma_op(q, lambda e, o=out, i=in_, tr=transpose: e.dma_start(
            out=o, in_=i, transpose=tr), reads, writes, tag, drain=drain)

    def pe(fn, reads, writes, tag=""):
        S.op("PE", fn, reads, writes, tag)

    def dve(fn, reads, writes, tag=""):
        S.op("DVE", fn, reads, writes, tag)

    def act(fn, reads, writes, tag=""):
        S.op("ACT", fn, reads, writes, tag)

    cc_s = lambda cc: slice(cc * G, (cc + 1) * G)
    cb_s = lambda cc: slice(cc * 257 + 1, cc * 257 + 257)
    cs_s = lambda cc: slice(cc * 257, cc * 257 + 256)
    bnd_s = slice(0, CC * 257, 257)
    lst_s = slice(256, CC * 257, 257)

    RW_M, RW_MSQ, RW_VAR, RW_RSTD, RW_MR, RW_S, RW_T1 = (
        0, 256, 512, 768, 1024, 1280, 1536)
    row = lambda off: rows_t[0:1, off:off + G]
    PSTs = PSTS[0:1, 0:256]
    PSTq = PSTQ[0:1, 0:256]

    def mm_contract(e, lhsT_fn, rhs_t, out_ap, n, rhs_sl_fn):
        last = None
        for i in range(n):
            last = e.matmul(out_ap, lhsT_fn(i), rhs_t[:, rhs_sl_fn(i)],
                            start=(i == 0), stop=(i == n - 1))
        return last

    def stat_bf(e, rhs_t, dst):
        return mm_contract(e, lambda i: onescolb_t[:, :], rhs_t,
                           dst, CC, cc_s)

    def stat_f32(e, rhs_t, dst):
        return mm_contract(e, lambda i: onescol_t[:, :], rhs_t,
                           dst, CC, cc_s)

    def rowchain(tag=""):
        dve(lambda e: e.tensor_scalar(row(RW_M), PSTs, 1.0 / C,
                                      None, OP.mult), ["PSTs"], ["rm"], tag)
        dve(lambda e: e.tensor_tensor(row(RW_MSQ), row(RW_M), row(RW_M),
                                      OP.mult), ["rm"], ["rmsq"], tag)
        dve(lambda e: e.scalar_tensor_tensor(row(RW_VAR), PSTq,
                                             1.0 / C, row(RW_MSQ),
                                             OP.mult, OP.subtract),
            ["PSTq", "rmsq"], ["rvar"], tag)
        act(lambda e: e.activation(row(RW_RSTD), row(RW_VAR), AF.Ln,
                                   bias=cst_t[0:1, 1:2]),
            ["rvar", "cst"], ["rrstd"], tag)
        act(lambda e: e.activation(row(RW_RSTD), row(RW_RSTD), AF.Exp,
                                   scale=-0.5), ["rrstd"], ["rrstd"], tag)
        dve(lambda e: e.tensor_tensor(row(RW_MR), row(RW_M), row(RW_RSTD),
                                      OP.mult), ["rm", "rrstd"], ["rmr"], tag)

    def bcast_pair(extra_reads=(), tag=""):
        pe(lambda e: e.matmul(PB0[:, 0:G], onesrow_t[:, :], row(RW_RSTD),
                              start=True, stop=True),
           ["onesrow", "rrstd"] + list(extra_reads), ["PB0"], tag)
        pe(lambda e: e.matmul(PB1[:, 0:G], onesrow_t[:, :], row(RW_MR),
                              start=True, stop=True),
           ["onesrow", "rmr"] + list(extra_reads), ["PB1"], tag)

    mmro = [0]

    def next_p():
        i = mmro[0] % 2
        mmro[0] += 1
        return P[i], PN[i]

    mmw = [0]

    def next_pw():
        i = mmw[0] % 2
        mmw[0] += 1
        return PW[i], PWN[i]

    dma("SP", ident_t[:, :], ident_h, [], ["ident"])
    dma("SP", onescol_t[:, :], onescol_h, [], ["onescol"])
    dma("SP", onescolb_t[:, :], onescolb_h, [], ["onescolb"])
    dma("SP", onesrow_t[:, :], onesrow_h, [], ["onesrow"])
    dma("SP", cst_t[:, :], cst_h, [], ["cst"])
    for nm, t, h in [("tmk", tmk_t, tmk_h), ("tmv", tmv_t, tmv_h),
                     ("tmr", tmr_t, tmr_h), ("td", td_t, td_h),
                     ("tf", tf_t, tf_h), ("wrt", wrt_t, wrt_h),
                     ("b1", b1_t, b1_h), ("b2", b2_t, b2_h)]:
        dma("SP", t[:, :], h, [], [nm])

    for m, h in enumerate([wkT_h, wvT_h, wrT_h, woT_h]):
        for cci in range(CC):
            dma("GP", attnW(m, cci), h[cci * 128:(cci + 1) * 128, :],
                [], [f"aw{m}_{cci}"])

    dma("SP", iota_t[:, :], iota_h, [], ["iota"])
    dma("SP", triu_t[:, :], triu_h, [], ["triu"])
    dve(lambda e: e.memset(SCR[:, 0:1024], 0.0), [], ["SCR"])
    for k in range(T // 128):
        dma("GP", ufull_h[k * 128:(k + 1) * 128, :],
            SCR[:, 0:1024], ["SCR"], ["ufull"], f"uz{k}",
            drain=(k == T // 128 - 1))
    dve(lambda e: e.tensor_copy(identb_t[:, :], ident_t[:, :]),
        ["ident"], ["identb"])
    act(lambda e: e.activation(eu_t[:, :], tf_t[:, :], AF.Exp), ["tf"], ["eu"])
    act(lambda e: e.activation(dd_t[:, :], td_t[:, :], AF.Exp), ["td"], ["dd"])
    dve(lambda e: e.tensor_scalar(dd_t[:, :], dd_t[:, :], -1.0, None, OP.mult),
        ["dd"], ["dd"])
    act(lambda e: e.activation(dd_t[:, :], dd_t[:, :], AF.Exp), ["dd"], ["dd"])

    dve(lambda e: e.memset(xsh_t[:, bnd_s], 0.0), [], ["xsh_b"])
    dve(lambda e: e.memset(Asc_t[:, bnd_s], 0.0), [],
        [f"A{i}b" for i in range(CC)])
    dve(lambda e: e.memset(Bsc_t[:, bnd_s], 0.0), [],
        [f"B{i}b" for i in range(CC)])

    pe(lambda e: mm_contract(e, lambda i: wrt_t[:, i:i + 1], onescol_t,
                             PSTS[0:1, 0:1], CC, lambda i: slice(0, 1)),
       ["wrt", "onescol"], ["PSTs"], "swr")
    dve(lambda e: e.tensor_copy(bis_t[0:1, 8:9], PSTS[0:1, 0:1]),
        ["PSTs"], ["swr"])

    def cpart_dram(h, g):
        return h[:, g * G:(g + 1) * G].rearrange("(cc p) j -> p cc j", p=128)

    def cpart_sbuf(ap):
        return ap.rearrange("p (cc j) -> p cc j", cc=CC)

    def load_xT(g):
        b = g % 3
        dma("SP", cpart_sbuf(xTg[b][:, :]), cpart_dram(xT_h, g),
            [], [f"xTg{b}"], f"ldx{g}")

    def stage_S1(g):
        buf = g % 3
        par = g % 2
        xg = f"xTg{buf}"
        xk_t, xv_t, xr_t = xk_p[par], xv_p[par], xr_p[par]
        if g > 0:
            dve(lambda e: e.tensor_copy(xsh_t[:, bnd_s], xsh_t[:, lst_s]),
                [f"xsh{i}" for i in range(CC)], ["xsh_b"])
        act(lambda e, b=buf: e.activation(SCR[:, :], xTg[b][:, :], AF.Square),
            [xg, "SCR"], ["SCR"], "sqa")
        pe(lambda e, b=buf: stat_f32(e, xTg[b], PSTs),
           [xg, "onescol", "PSTs"], ["PSTs"], "sum_a")
        pe(lambda e: stat_bf(e, SCR, PSTq),
           ["SCR", "onescolb", "PSTq"], ["PSTq"], "sq_a")
        rowchain("ln_a")
        bcast_pair(tag="bc_a")

        for cci in range(CC):
            dve(lambda e, cc=cci, b=buf: e.tensor_tensor(
                xsh_t[:, cb_s(cc)], xTg[b][:, cc_s(cc)], PB0[:, 0:G], OP.mult),
                [xg, "PB0"], [f"xsh{cci}"])
        for cci in range(CC):
            dve(lambda e, cc=cci: e.tensor_tensor(
                xsh_t[:, cb_s(cc)], xsh_t[:, cb_s(cc)], PB1[:, 0:G],
                OP.subtract), [f"xsh{cci}", "PB1"], [f"xsh{cci}"])

        for cci in range(CC):
            dve(lambda e, cc=cci, t=xk_t: e.tensor_tensor(
                t[:, cc_s(cc)], xsh_t[:, cb_s(cc)], xsh_t[:, cs_s(cc)],
                OP.subtract), [f"xsh{cci}", "xsh_b"], [f"xk{par}_{cci}"])
        for tname, tile, tmname, tmcol in [("xr", xr_t, "tmr", tmr_t),
                                           ("xv", xv_t, "tmv", tmv_t)]:
            for cci in range(CC):
                dve(lambda e, cc=cci, tl=tile, tm=tmcol, kk=xk_t:
                    e.scalar_tensor_tensor(
                        tl[:, cc_s(cc)], kk[:, cc_s(cc)], tm[:, cc:cc + 1],
                        xsh_t[:, cs_s(cc)], OP.mult, OP.add),
                    [f"xk{par}_{cci}", tmname, f"xsh{cci}", "xsh_b"],
                    [f"{tname}{par}_{cci}"])
        for cci in range(CC):
            dve(lambda e, cc=cci, kk=xk_t: e.scalar_tensor_tensor(
                kk[:, cc_s(cc)], kk[:, cc_s(cc)], tmk_t[:, cc:cc + 1],
                xsh_t[:, cs_s(cc)], OP.mult, OP.add),
                [f"xk{par}_{cci}", "tmk", f"xsh{cci}", "xsh_b"],
                [f"xk{par}_{cci}"])

    def stage_S2(g):
        par = g % 2
        EK_t, EKV_t, sig_t = EK_p[par], EKV_p[par], sig_p[par]
        xk_t, xv_t, xr_t = xk_p[par], xv_p[par], xr_p[par]
        XK = [f"xk{par}_{i}" for i in range(CC)]
        XV = [f"xv{par}_{i}" for i in range(CC)]
        XR = [f"xr{par}_{i}" for i in range(CC)]
        for cco in range(CC):
            pk, pkn = next_p()
            pv, pvn = next_p()
            pe(lambda e, p=pk, co=cco, t=xk_t: mm_contract(
                e, lambda i, co_=co: attnW(0, i)[:, co_ * 128:(co_ + 1) * 128],
                t, p[:, 0:G], CC, cc_s),
               XK + [f"aw0_{i}" for i in range(CC)], [pkn], f"km{g}_{cco}")
            act(lambda e, p=pk, cc=cco, t=EK_t: e.activation(
                t[:, cc_s(cc)], p[:, 0:G], AF.Exp), [pkn], [f"EK{par}_{cco}"])
            pe(lambda e, p=pv, co=cco, t=xv_t: mm_contract(
                e, lambda i, co_=co: attnW(1, i)[:, co_ * 128:(co_ + 1) * 128],
                t, p[:, 0:G], CC, cc_s),
               XV + [f"aw1_{i}" for i in range(CC)], [pvn], f"vm{g}_{cco}")
            dve(lambda e, p=pv, cc=cco, ek=EK_t, ev=EKV_t: e.tensor_tensor(
                ev[:, cc_s(cc)], ek[:, cc_s(cc)], p[:, 0:G], OP.mult),
                [f"EK{par}_{cco}", pvn], [f"EKV{par}_{cco}"])
        for cco in range(CC):
            pr, prn = next_p()
            pe(lambda e, p=pr, co=cco, t=xr_t: mm_contract(
                e, lambda i, co_=co: attnW(2, i)[:, co_ * 128:(co_ + 1) * 128],
                t, p[:, 0:G], CC, cc_s),
               XR + [f"aw2_{i}" for i in range(CC)], [prn], f"rm{g}_{cco}")
            act(lambda e, p=pr, cc=cco, t=sig_t: e.activation(
                t[:, cc_s(cc)], p[:, 0:G], AF.Sigmoid),
                [prn], [f"sig{par}_{cco}"])

    def stage_S3(g):
        par = g % 2
        EK_t, EKV_t, sig_t = EK_p[par], EKV_p[par], sig_p[par]
        ekn = lambda cc: f"EK{par}_{cc}"
        evn = lambda cc: f"EKV{par}_{cc}"
        for cci in range(CC):
            dve(lambda e, cc=cci, ev=EKV_t: e.tensor_tensor_scan(
                Asc_t[:, cb_s(cc)], dd_t[:, cc:cc + 1].broadcast_to([128, G]),
                ev[:, cc_s(cc)], Asc_t[:, cc * 257:cc * 257 + 1],
                OP.mult, OP.add),
                [evn(cci), "dd", f"A{cci}b"], [f"A{cci}"])
            dve(lambda e, cc=cci, ek=EK_t: e.tensor_tensor_scan(
                Bsc_t[:, cb_s(cc)], dd_t[:, cc:cc + 1].broadcast_to([128, G]),
                ek[:, cc_s(cc)], Bsc_t[:, cc * 257:cc * 257 + 1],
                OP.mult, OP.add),
                [ekn(cci), "dd", f"B{cci}b"], [f"B{cci}"])

        for cci in range(CC):
            dve(lambda e, cc=cci, ev=EKV_t: e.scalar_tensor_tensor(
                ev[:, cc_s(cc)], Asc_t[:, cs_s(cc)], eu_t[:, cc:cc + 1],
                ev[:, cc_s(cc)], OP.mult, OP.add),
                [f"A{cci}", f"A{cci}b", "eu", evn(cci)], [evn(cci)])
            dve(lambda e, cc=cci, ek=EK_t: e.scalar_tensor_tensor(
                ek[:, cc_s(cc)], Bsc_t[:, cs_s(cc)], eu_t[:, cc:cc + 1],
                ek[:, cc_s(cc)], OP.mult, OP.add),
                [f"B{cci}", f"B{cci}b", "eu", ekn(cci)], [ekn(cci)])
        dve(lambda e: e.tensor_copy(Asc_t[:, bnd_s], Asc_t[:, lst_s]),
            [f"A{i}" for i in range(CC)], [f"A{i}b" for i in range(CC)])
        dve(lambda e: e.tensor_copy(Bsc_t[:, bnd_s], Bsc_t[:, lst_s]),
            [f"B{i}" for i in range(CC)], [f"B{i}b" for i in range(CC)])
        EKall = [ekn(i) for i in range(CC)]
        EVall = [evn(i) for i in range(CC)]
        act(lambda e, ek=EK_t: e.activation(ek[:, :], ek[:, :], AF.Ln),
            EKall, EKall, "invD")
        act(lambda e, ek=EK_t: e.activation(ek[:, :], ek[:, :], AF.Exp,
                                            scale=-1.0), EKall, EKall, "invD")
        dve(lambda e, ek=EK_t, ev=EKV_t: e.tensor_tensor(
            ev[:, :], ev[:, :], ek[:, :], OP.mult),
            EKall + EVall, EVall, "ymul")
        dve(lambda e, ev=EKV_t, sg=sig_t: e.tensor_tensor(
            sg[:, :], ev[:, :], sg[:, :], OP.mult),
            EVall + [f"sig{par}_{i}" for i in range(CC)],
            [f"ry{par}"], "rymul")

    def stage_S4(g):
        buf = g % 3
        par = g % 2
        xg = f"xTg{buf}"
        sig_t = sig_p[par]
        for cco in range(CC):
            po, pon = next_pw()
            pe(lambda e, p=po, co=cco, t=sig_t: mm_contract(
                e, lambda i, co_=co: attnW(3, i)[:, co_ * 128:(co_ + 1) * 128],
                t, p[:, 0:G], CC, cc_s),
               [f"ry{par}"] + [f"sig{par}_{i}" for i in range(CC)] +
               [f"aw3_{i}" for i in range(CC)],
               [pon], f"om{g}_{cco}")
            act(lambda e, p=po, cc=cco: e.activation(
                ATX[:, cc_s(cc)], p[:, 0:G], AF.Copy), [pon],
                [f"at{cco}", f"x1{cco}"])

        ATall = [f"at{i}" for i in range(CC)]
        act(lambda e: e.activation(SCR[:, :], ATX[:, :], AF.Square),
            ATall + ["SCR"], ["SCR"], "sqb")
        pe(lambda e: stat_f32(e, ATX, PSTs),
           ATall + ["onescol", "PSTs"], ["PSTs"], "sum_b")
        pe(lambda e: stat_bf(e, SCR, PSTq),
           ["SCR", "onescolb", "PSTq"], ["PSTq"], "sq_b")
        rowchain("ln_b")
        bcast_pair(tag="bc_b")
        for cci in range(CC):
            dve(lambda e, cc=cci: e.tensor_tensor(
                ATX[:, cc_s(cc)], ATX[:, cc_s(cc)], PB0[:, 0:G],
                OP.mult), [f"at{cci}", "PB0"], [f"x1{cci}", f"at{cci}"])
        for cci in range(CC):
            dve(lambda e, cc=cci: e.tensor_tensor(
                ATX[:, cc_s(cc)], ATX[:, cc_s(cc)], PB1[:, 0:G],
                OP.subtract), [f"x1{cci}", "PB1"], [f"x1{cci}"])
        for cci in range(CC):
            dve(lambda e, cc=cci, b=buf: e.tensor_tensor(
                ATX[:, cc_s(cc)], ATX[:, cc_s(cc)], xTg[b][:, cc_s(cc)],
                OP.add), [f"x1{cci}", xg], [f"x1{cci}"])

        X1ALL = [f"x1{i}" for i in range(CC)]
        act(lambda e: e.activation(SCR[:, :], ATX[:, :], AF.Square),
            X1ALL + ["SCR"], ["SCR"], "sqc")
        pe(lambda e: stat_f32(e, ATX, PSTs),
           X1ALL + ["onescol", "PSTs"], ["PSTs"], "sum_c")
        pe(lambda e: stat_bf(e, SCR, PSTq),
           ["SCR", "onescolb", "PSTq"], ["PSTq"], "sq_c")
        rowchain("ln_c")
        pe(lambda e: mm_contract(e, lambda i: wrt_t[:, i:i + 1], ATX,
                                 PSTs, CC, cc_s),
           X1ALL + ["wrt"], ["PSTs"], "smm")
        dve(lambda e: e.tensor_scalar(row(RW_T1), row(RW_M), bis_t[0:1, 8:9],
                                      None, OP.mult), ["rm", "swr"], ["rt1"])
        dve(lambda e: e.tensor_tensor(row(RW_T1), PSTs,
                                      row(RW_T1), OP.subtract),
            ["PSTs", "rt1"], ["rt1"])
        dve(lambda e: e.tensor_tensor(row(RW_S), row(RW_T1), row(RW_RSTD),
                                      OP.mult), ["rt1", "rrstd"], ["rs"])
        dma("GP", srow_h[0:1, g * G:(g + 1) * G], row(RW_S), ["rs"],
            [f"srow{g}"])
        dma("GP", lncr_h[0:1, g * G:(g + 1) * G], row(RW_RSTD), ["rrstd"],
            [f"lncr{g}"])
        dma("GP", lncm_h[0:1, g * G:(g + 1) * G], row(RW_MR), ["rmr"],
            [f"lncm{g}"])

        dma("SP", cpart_dram(x1T_h, g), cpart_sbuf(ATX[:, :]),
            X1ALL, [f"x1st{g}"])
        for h in range(2):
            for cc in range(CC):
                pt, ptn = next_pw()
                pe(lambda e, p=pt, hh=h, c=cc: e.transpose(
                    p[:, 0:128],
                    ATX[:, c * G + hh * 128:c * G + (hh + 1) * 128],
                    ident_t[:, :]),
                   [f"x1{cc}", "ident"], [ptn], f"x1t{g}_{h}_{cc}")
                act(lambda e, p=pt, hh=h, c=cc: e.activation(
                    SCR[:, hh * 1024 + c * 128:hh * 1024 + (c + 1) * 128],
                    p[:, 0:128], AF.Copy),
                    [ptn, "SCR"], ["SCR"], f"x1e{g}_{h}_{cc}")
        dma("SP", x1rows_h[g * G:(g + 1) * G, :].rearrange(
            "(h p) c -> p h c", p=128),
            SCR[:, :].rearrange("p (h c) -> p h c", h=2),
            ["SCR"], ["x1rows"], f"x1r{g}")

    for i in range(-2, NG):
        if i + 2 < NG:
            load_xT(i + 2)
        if 0 <= i + 1 < NG:
            stage_S2(i + 1)
        if i >= 0:
            stage_S3(i)
        if i + 2 < NG:
            stage_S1(i + 2)
        if i >= 0:
            stage_S4(i)

    attn_end = S.snapshot()
    S.fence("GP", attn_end)
    for cci in range(CC):
        dma("GP", w1W(cci), w1_h[cci * 128:(cci + 1) * 128, :], [], [f"w1_{cci}"])
    for fi in range(FC):
        dma("GP", w2W(fi), w2_h[fi * 128:(fi + 1) * 128, :], [], [f"w2_{fi}"])

    dma("SP", wmtr_t[:, :], srow_h.rearrange("a (c p) -> (a c) p", p=128),
        [f"srow{gg}" for gg in range(NG)], ["wmtr"])
    pe(lambda e: e.transpose(PB0[:, 0:64], wmtr_t[:, :], ident_t[0:64, 0:64]),
       ["wmtr", "ident"], ["PB0"], "str")
    dve(lambda e: e.tensor_copy(scoreT_t[:, :], PB0[:, 0:64]),
        ["PB0"], ["scoreT"])

    dve(lambda e: e.reduce_max(geT_t[:, 0:1], scoreT_t[:, :], axis=AX.X,
                               apply_absolute_value=True),
        ["scoreT"], ["geT"])
    pe(lambda e: e.matmul(PSTS[0:1, 0:1], geT_t[:, 0:1], onescol_t[:, 0:1],
                          start=True, stop=True),
       ["geT", "onescol"], ["PSTs"], "bsum")
    dve(lambda e: e.tensor_scalar(bis_t[0:1, 1:2], PSTS[0:1, 0:1], 1.0, 1.0,
                                  OP.mult, OP.add), ["PSTs"], ["hi"])
    dve(lambda e: e.tensor_scalar(bis_t[0:1, 0:1], bis_t[0:1, 1:2], -1.0,
                                  None, OP.mult), ["hi"], ["lo"])
    dve(lambda e: e.tensor_tensor(bis_t[0:1, 2:3], bis_t[0:1, 0:1],
                                  bis_t[0:1, 1:2], OP.add), ["lo", "hi"], ["th"])
    dve(lambda e: e.tensor_scalar(bis_t[0:1, 2:3], bis_t[0:1, 2:3], 0.5, None,
                                  OP.mult), ["th"], ["th"])

    for it in range(N_BIS):
        pe(lambda e: e.matmul(PB0[:, 0:1], onesrow_t[:, :], bis_t[0:1, 2:3],
                              start=True, stop=True),
           ["th", "onesrow", "PB0"], ["PB0"], f"bth{it}")
        dve(lambda e: e.tensor_scalar(geT_t[:, :], scoreT_t[:, :],
                                      PB0[:, 0:1], None, OP.is_ge),
            ["scoreT", "PB0"], ["geT"])
        dve(lambda e: e.reduce_sum(sigT_t[:, 0:1], geT_t[:, :], axis=AX.X),
            ["geT"], ["sigT"])
        pe(lambda e: e.matmul(PSTS[0:1, 0:1], sigT_t[:, 0:1],
                              onescol_t[:, 0:1], start=True, stop=True),
           ["sigT", "onescol", "PSTs"], ["PSTs"], f"bcnt{it}")
        dve(lambda e: e.tensor_scalar(bis_t[0:1, 3:4], PSTS[0:1, 0:1],
                                      float(KSEL), None, OP.is_ge),
            ["PSTs"], ["flag"])
        dve(lambda e: e.tensor_tensor(bis_t[0:1, 4:5], bis_t[0:1, 2:3],
                                      bis_t[0:1, 0:1], OP.subtract),
            ["th", "lo"], ["t4"])
        dve(lambda e: e.tensor_tensor(bis_t[0:1, 4:5], bis_t[0:1, 4:5],
                                      bis_t[0:1, 3:4], OP.mult),
            ["t4", "flag"], ["t4"])
        dve(lambda e: e.tensor_tensor(bis_t[0:1, 0:1], bis_t[0:1, 0:1],
                                      bis_t[0:1, 4:5], OP.add),
            ["lo", "t4"], ["lo"])
        dve(lambda e: e.tensor_tensor(bis_t[0:1, 5:6], bis_t[0:1, 1:2],
                                      bis_t[0:1, 2:3], OP.subtract),
            ["hi", "th"], ["t5"])
        dve(lambda e: e.tensor_tensor(bis_t[0:1, 5:6], bis_t[0:1, 5:6],
                                      bis_t[0:1, 3:4], OP.mult),
            ["t5", "flag"], ["t5"])
        dve(lambda e: e.tensor_tensor(bis_t[0:1, 1:2], bis_t[0:1, 2:3],
                                      bis_t[0:1, 5:6], OP.add),
            ["th", "t5"], ["hi"])
        dve(lambda e: e.tensor_tensor(bis_t[0:1, 2:3], bis_t[0:1, 0:1],
                                      bis_t[0:1, 1:2], OP.add),
            ["lo", "hi"], ["th"])
        dve(lambda e: e.tensor_scalar(bis_t[0:1, 2:3], bis_t[0:1, 2:3], 0.5,
                                      None, OP.mult), ["th"], ["th"])

    pe(lambda e: e.matmul(PB0[:, 0:1], onesrow_t[:, :], bis_t[0:1, 0:1],
                          start=True, stop=True),
       ["lo", "onesrow", "PB0"], ["PB0"], "bfin")
    dve(lambda e: e.tensor_scalar(geT_t[:, :], scoreT_t[:, :], PB0[:, 0:1],
                                  None, OP.is_ge), ["scoreT", "PB0"], ["geT"])
    act(lambda e: e.activation(sigT_t[:, :], scoreT_t[:, :], AF.Sigmoid),
        ["scoreT", "sigT"], ["sigT"])
    dve(lambda e: e.tensor_tensor(sigT_t[:, :], geT_t[:, :], sigT_t[:, :],
                                  OP.mult), ["geT", "sigT"], ["sigT"])
    pe(lambda e: e.transpose(PB1[0:64, 0:128], sigT_t[:, :], ident_t[:, :]),
       ["sigT", "ident", "PB1"], ["PB1"], "wtr")
    dve(lambda e: e.tensor_copy(wmtr_t[:, :], PB1[0:64, 0:128]),
        ["PB1"], ["wmtr"])
    dma("GP", wmrow_h.rearrange("a (c p) -> (a c) p", p=128), wmtr_t[:, :],
        ["wmtr"], ["wmrow"])

    pe(lambda e: e.matmul(PSTS[0:1, 0:64], onescol_t[:, :], geT_t[:, :],
                          start=True, stop=True),
       ["geT", "onescol", "PSTs"], ["PSTs"], "csum")
    dve(lambda e: e.tensor_copy(csB_t[:, :], PSTS[0:1, 0:64]), ["PSTs"], ["csB"])
    pe(lambda e: e.matmul(PB0[0:64, 0:128], csB_t[:, :], onesrow_t[:, :],
                          start=True, stop=True),
       ["csB", "onesrow", "PB0"], ["PB0"], "csbc")
    dve(lambda e: e.tensor_copy(csC_t[:, :], PB0[0:64, 0:128]), ["PB0"], ["csC"])
    pe(lambda e: (e.matmul(PB1[:, 0:64], triu_t[:, :], geT_t[:, :],
                           start=True, stop=False),
                  e.matmul(PB1[:, 0:64], csC_t[:, :], triu_t[0:64, 0:64],
                           start=False, stop=True))[-1],
       ["triu", "geT", "csC", "PB1"], ["PB1"], "posmm")
    dve(lambda e: e.tensor_scalar(sigT_t[:, :], PB1[:, 0:64], 1.0, None,
                                  OP.add), ["PB1"], ["sigT"])
    dve(lambda e: e.tensor_tensor(geT_t[:, :], geT_t[:, :], sigT_t[:, :],
                                  OP.mult), ["geT", "sigT"], ["geT"])
    dve(lambda e: e.tensor_scalar(sigT_t[:, :], geT_t[:, :], 0.0, None,
                                  OP.is_equal), ["geT"], ["sigT"])
    dve(lambda e: e.scalar_tensor_tensor(geT_t[:, :], sigT_t[:, :], 8192.0,
                                         geT_t[:, :], OP.mult, OP.add),
        ["geT", "sigT"], ["geT"])
    dve(lambda e: e.tensor_scalar(geT_t[:, :], geT_t[:, :], -1.0, None,
                                  OP.add), ["geT"], ["geT"])
    dve(lambda e: e.tensor_copy(posI_t[:, :], geT_t[:, :]), ["geT"], ["posI"])
    for c in range(64):
        S.dma_op("GP", lambda e, cc=c: e.indirect_dma_start(
            idxb_h[0:T, :],
            bass.IndirectOffsetOnAxis(posI_t[:, cc:cc + 1], 0),
            iota_t[:, cc:cc + 1], None),
            ["posI", "iota"], ["idxb"], f"isc{c}", drain=True)

    bis_end = S.snapshot()
    for e in ("DVE", "ACT", "PE", "SP"):
        S.fence(e, bis_end)

    NSG = KSEL // G
    fx1Tg = carve(AR2, 0, 2048, F32)
    x1g_p = [carve(AR2, 8192, 1024, BF16),
             carve(AR2, 10240, 1024, BF16),
             carve(AR2, 12288, 1024, BF16),
             carve(AR2, 14336, 1024, BF16)]
    hbf_t = carve(AR2, 16384, 2048, BF16)
    gar_t = carve(AR2, 20480, FC * G, BF16)
    hu_t = carve(AR2, 36864, 2048, F32)
    SQf_t = carve(AR2, 45056, 2048, BF16)
    x2T_t = carve(AR2, 49152, 2048, F32)
    ubf_t = carve(AR2, 61440, 2048, BF16)

    def w1mm(e, pr_, co0):
        last = None
        for half in range(2):
            for cci in range(CC):
                last = e.matmul(
                    pr_[:, half * G:(half + 1) * G],
                    w1W(cci)[:, (co0 + half) * 128:(co0 + half + 1) * 128],
                    hbf_t[:, cc_s(cci)],
                    start=(cci == 0), stop=(cci == CC - 1))
        return last

    def w2mm(e, pr_, cco_):
        last = None
        for fi in range(FC):
            last = e.matmul(pr_[:, 0:G],
                            w2W(fi)[:, cco_ * 128:(cco_ + 1) * 128],
                            gar_t[:, fi * G:(fi + 1) * G],
                            start=(fi == 0), stop=(fi == FC - 1))
        return last

    def stage_Cf(s):
        sp = s % 2
        for h in range(2):
            dma("SP", idxc_t[:, sp * 2 + h:sp * 2 + h + 1],
                idxb_h[s * G + h * 128:s * G + (h + 1) * 128, :],
                ["idxb", f"idxc{sp}"], [f"idxc{sp}"], f"lix{s}_{h}",
                drain=(h == 1))
        for h in range(2):
            xb = x1g_p[2 * sp + h]
            xbn = f"x1g{2 * sp + h}"
            S.dma_op("GP", lambda e, o=xb, c=sp * 2 + h: e.indirect_dma_start(
                o[:, :], None, x1rows_h[0:T, :],
                bass.IndirectOffsetOnAxis(idxc_t[:, c:c + 1], 0)),
                ["x1rows", f"idxc{sp}", xbn], [xbn], f"gx{s}_{h}")
            dve(lambda e, o=xb: e.reduce_sum(fcol_t[:, 0:1], o[:, :],
                                             axis=AX.X), [xbn], ["fc0"])
            act(lambda e, o=xb: e.activation(ubf_t[:, 0:1024], o[:, :],
                                             AF.Square), [xbn, "ubfR"],
                ["ubfR"])
            dve(lambda e: e.reduce_sum(fcol_t[:, 1:2], ubf_t[:, 0:1024],
                                       axis=AX.X), ["ubfR"], ["fc1"])
            dve(lambda e: e.tensor_scalar(fcol_t[:, 0:1], fcol_t[:, 0:1],
                                          -1.0 / C, None, OP.mult),
                ["fc0"], ["fc0"])
            dve(lambda e: e.tensor_tensor(fcol_t[:, 2:3], fcol_t[:, 0:1],
                                          fcol_t[:, 0:1], OP.mult),
                ["fc0"], ["fc2"])
            dve(lambda e: e.scalar_tensor_tensor(fcol_t[:, 1:2],
                                                 fcol_t[:, 1:2], 1.0 / C,
                                                 fcol_t[:, 2:3],
                                                 OP.mult, OP.subtract),
                ["fc1", "fc2"], ["fc1"])
            act(lambda e: e.activation(fcol_t[:, 1:2], fcol_t[:, 1:2], AF.Ln,
                                       bias=cst_t[:, 1:2]),
                ["fc1", "cst"], ["fc1"])
            act(lambda e: e.activation(fcol_t[:, 1:2], fcol_t[:, 1:2], AF.Exp,
                                       scale=-0.5), ["fc1"], ["fc1"])
            dve(lambda e, o=xb: e.tensor_scalar(o[:, :], o[:, :],
                                                fcol_t[:, 0:1],
                                                fcol_t[:, 1:2],
                                                OP.add, OP.mult),
                [xbn, "fc0", "fc1"], [xbn])
        for h in range(2):
            xb = x1g_p[2 * sp + h]
            xbn = f"x1g{2 * sp + h}"
            for cc in range(CC):
                dma("SP", hbf_t[:, cc * G + h * 128:cc * G + (h + 1) * 128],
                    xb[:, cc * 128:(cc + 1) * 128],
                    [xbn, "hbR"], ["hbR"], f"ht{s}_{h}_{cc}",
                    transpose=True, drain=(h == 1 and cc == CC - 1))

    def stage_Df(s):
        for pair in range(FC // 2):
            pA, pAn = next_p()
            pe(lambda e, p=pA, co0=2 * pair, f=w1mm: f(e, p, co0),
               ["hbR"] + [f"w1_{i}" for i in range(CC)],
               [pAn], f"w1m{s}_{pair}")
            act(lambda e, p=pA, pr_=pair: e.activation(
                gar_t[:, 2 * pr_ * G:(2 * pr_ + 1) * G], p[:, 0:G],
                AF.Gelu_apprx_tanh, bias=b1_t[:, 2 * pr_:2 * pr_ + 1]),
                [pAn, "b1", "garR"], ["garR"])
            act(lambda e, p=pA, pr_=pair: e.activation(
                gar_t[:, (2 * pr_ + 1) * G:(2 * pr_ + 2) * G], p[:, G:2 * G],
                AF.Gelu_apprx_tanh, bias=b1_t[:, 2 * pr_ + 1:2 * pr_ + 2]),
                [pAn, "b1", "garR"], ["garR"])

    def stage_Af(s):
        sp = s % 2
        for cco in range(CC):
            pu, pun = next_pw()
            pe(lambda e, p=pu, co=cco, f=w2mm: f(e, p, co),
               ["garR"] + [f"w2_{i}" for i in range(FC)], [pun],
               f"w2m{s}_{cco}")
            act(lambda e, p=pu, cc=cco: e.activation(
                ubf_t[:, cc_s(cc)], p[:, 0:G], AF.Identity,
                bias=b2_t[:, cc:cc + 1]), [pun, "b2", "ubfR"], ["ubfR"])
        for h in range(2):
            for cc in range(CC):
                dma("AQ", SQf_t[:, h * 1024 + cc * 128:h * 1024 + (cc + 1) * 128],
                    ubf_t[:, cc * G + h * 128:cc * G + (h + 1) * 128],
                    ["ubfR", "SQfR"], ["SQfR"], f"ut{s}_{h}_{cc}",
                    transpose=True, drain=(h == 1 and cc == CC - 1))
        for h in range(2):
            S.dma_op("GP", lambda e, hh=h, c=sp * 2 + h: e.indirect_dma_start(
                ufull_h[0:T, :],
                bass.IndirectOffsetOnAxis(idxc_t[:, c:c + 1], 0),
                SQf_t[:, hh * 1024:(hh + 1) * 1024], None),
                ["SQfR", f"idxc{sp}", "ufull"], ["ufull"], f"us{s}_{h}",
                drain=True)

    fx1b_t = carve(AR2, 8192, 2048, F32)
    stg_t = [SQf_t, hbf_t]
    stg_n = ["SQfR", "hbR"]

    def tail_load(g):
        p = g % 2
        fx1ap = fx1Tg if p == 0 else fx1b_t
        fxn = "fx1" if p == 0 else "fx1b"
        wl = [fxn]
        if p == 1 and g == NSG + 1:
            wl += [f"x1g{i}" for i in range(4)]
        dma("SP", cpart_sbuf(fx1ap[:, :]), cpart_dram(x1T_h, g),
            [f"x1st{g}", fxn], wl, f"lf{g}")
        dma("SP", stg_t[p][:, :].rearrange("p (h c) -> p h c", h=2),
            ufull_h[g * G:(g + 1) * G, :].rearrange("(h p) c -> p h c", p=128),
            ["ufull", stg_n[p]], [stg_n[p]], f"ur{g}")
        load_wm(g, p)
        load_lnc(g)

    def stage_Ft(g):
        p = g % 2
        fx1ap = fx1Tg if p == 0 else fx1b_t
        fxn = "fx1" if p == 0 else "fx1b"
        for h in range(2):
            for cc in range(CC):
                pt, ptn = next_p()
                pe(lambda e, q=pt, hh=h, c=cc, sg=stg_t[p]: e.transpose(
                    q[:, 0:64].bitcast(BF16),
                    sg[:, hh * 1024 + c * 128:hh * 1024 + (c + 1) * 128],
                    identb_t[:, :]),
                   [stg_n[p], "identb"], [ptn], f"uT{g}_{h}_{cc}")
                act(lambda e, q=pt, hh=h, c=cc, b=p: e.activation(
                    gar_t[:, b * 2048 + c * G + hh * 128:
                          b * 2048 + c * G + (hh + 1) * 128],
                    q[:, 0:64].bitcast(BF16), AF.Copy),
                    [ptn, "garR"], ["garR"], f"uE{g}_{h}_{cc}")
        pe(lambda e, b=p: e.matmul(PB1[:, 256:256 + G], onesrow_t[:, :],
                                   wmr_t[0:1, b * G:(b + 1) * G],
                                   start=True, stop=True),
           [f"wmr{p}", "onesrow", "PB1"], ["PB1"], "fwmb")
        dve(lambda e: e.tensor_copy(wmb2_t[:, :], PB1[:, 256:256 + G]),
            ["PB1"], ["wmb"])
        bcast_pair(extra_reads=("wmb",), tag="bc_f")
        for cci in range(CC):
            dve(lambda e, cc=cci, fx=fx1ap: e.tensor_tensor(
                hu_t[:, cc_s(cc)], fx[:, cc_s(cc)], PB0[:, 0:G],
                OP.mult), [fxn, "PB0"], ["huR"])
            dve(lambda e, cc=cci: e.tensor_tensor(
                hu_t[:, cc_s(cc)], hu_t[:, cc_s(cc)], PB1[:, 0:G],
                OP.subtract), ["huR", "PB1"], ["huR"])
            dve(lambda e, cc=cci, b=p: e.tensor_tensor(
                x2T_t[:, cc_s(cc)],
                gar_t[:, b * 2048 + cc * G:b * 2048 + (cc + 1) * G],
                wmb2_t[:, :], OP.mult), ["garR", "wmb"], ["x2R"])
            dve(lambda e, cc=cci: e.tensor_tensor(
                hu_t[:, cc_s(cc)], hu_t[:, cc_s(cc)], x2T_t[:, cc_s(cc)],
                OP.add), ["huR", "x2R"], ["huR"])
        act(lambda e: e.activation(ubf_t[:, :], hu_t[:, :], AF.Square),
            ["huR", "ubfR"], ["ubfR"], "sqd")
        pe(lambda e: stat_f32(e, hu_t, PSTs),
           ["huR", "onescol", "PSTs"], ["PSTs"], "sum_d")
        pe(lambda e: stat_bf(e, ubf_t, PSTq),
           ["ubfR", "onescolb", "PSTq"], ["PSTq"], "sq_d")
        rowchain("ln_d")
        bcast_pair(tag="bc_d")
        for cci in range(CC):
            dve(lambda e, cc=cci: e.tensor_tensor(
                x2T_t[:, cc_s(cc)], hu_t[:, cc_s(cc)], PB0[:, 0:G], OP.mult),
                ["huR", "PB0"], ["x2R"])
            dve(lambda e, cc=cci: e.tensor_tensor(
                x2T_t[:, cc_s(cc)], x2T_t[:, cc_s(cc)], PB1[:, 0:G],
                OP.subtract), ["x2R", "PB1"], ["x2R"])
            dve(lambda e, cc=cci, fx=fx1ap: e.tensor_tensor(
                x2T_t[:, cc_s(cc)], x2T_t[:, cc_s(cc)], fx[:, cc_s(cc)],
                OP.add), ["x2R", fxn], ["x2R"])
        dma("SP", cpart_dram(xo_h, g), cpart_sbuf(x2T_t[:, :]),
            ["x2R"], [f"xost{g}"])

    def load_fx1(g):
        dma("SP", cpart_sbuf(fx1Tg[:, :]), cpart_dram(x1T_h, g),
            [f"x1st{g}", "fx1"], ["fx1"], f"lf{g}")

    def load_wm(g, b):
        dma("GP", wmr_t[0:1, b * G:(b + 1) * G],
            wmrow_h[0:1, g * G:(g + 1) * G], ["wmrow", f"wmr{b}"],
            [f"wmr{b}"], f"lwm{g}")

    def load_lnc(g):
        dma("GP", row(RW_RSTD), lncr_h[0:1, g * G:(g + 1) * G],
            [f"lncr{g}", "rrstd"], ["rrstd"], f"llr{g}")
        dma("GP", row(RW_MR), lncm_h[0:1, g * G:(g + 1) * G],
            [f"lncm{g}", "rmr"], ["rmr"], f"llm{g}")

    def stage_F(g):
        fb = g % 2
        for cc in range(CC):
            dma("SP", gar_t[:, fb * 2048 + cc * G:fb * 2048 + (cc + 1) * G],
                ufull_h[g * G:(g + 1) * G, cc * 128:(cc + 1) * 128],
                ["ufull", "garR"], ["garR"], f"uT{g}_{cc}",
                transpose=True, drain=(cc == CC - 1))
        load_lnc(g)
        pe(lambda e, b=fb: e.matmul(PB1[:, 256:256 + G], onesrow_t[:, :],
                                    wmr_t[0:1, b * G:(b + 1) * G],
                                    start=True, stop=True),
           [f"wmr{fb}", "onesrow", "PB1"], ["PB1"], "fwmb")
        dve(lambda e: e.tensor_copy(wmb2_t[:, :], PB1[:, 256:256 + G]),
            ["PB1"], ["wmb"])
        bcast_pair(extra_reads=("wmb",), tag="bc_f")
        for cci in range(CC):
            dve(lambda e, cc=cci: e.tensor_tensor(
                hu_t[:, cc_s(cc)], fx1Tg[:, cc_s(cc)], PB0[:, 0:G],
                OP.mult), ["fx1", "PB0"], ["huR"])
            dve(lambda e, cc=cci: e.tensor_tensor(
                hu_t[:, cc_s(cc)], hu_t[:, cc_s(cc)], PB1[:, 0:G],
                OP.subtract), ["huR", "PB1"], ["huR"])
            dve(lambda e, cc=cci, b=fb: e.tensor_tensor(
                x2T_t[:, cc_s(cc)],
                gar_t[:, b * 2048 + cc * G:b * 2048 + (cc + 1) * G],
                wmb2_t[:, :], OP.mult), ["garR", "wmb"], ["x2R"])
            dve(lambda e, cc=cci: e.tensor_tensor(
                hu_t[:, cc_s(cc)], hu_t[:, cc_s(cc)], x2T_t[:, cc_s(cc)],
                OP.add), ["huR", "x2R"], ["huR"])
        act(lambda e: e.activation(SQf_t[:, :], hu_t[:, :], AF.Square),
            ["huR", "SQfR"], ["SQfR"], "sqd")
        pe(lambda e: stat_f32(e, hu_t, PSTs),
           ["huR", "onescol", "PSTs"], ["PSTs"], "sum_d")
        pe(lambda e: stat_bf(e, SQf_t, PSTq),
           ["SQfR", "onescolb", "PSTq"], ["PSTq"], "sq_d")
        rowchain("ln_d")
        bcast_pair(tag="bc_d")
        for cci in range(CC):
            dve(lambda e, cc=cci: e.tensor_tensor(
                x2T_t[:, cc_s(cc)], hu_t[:, cc_s(cc)], PB0[:, 0:G], OP.mult),
                ["huR", "PB0"], ["x2R"])
            dve(lambda e, cc=cci: e.tensor_tensor(
                x2T_t[:, cc_s(cc)], x2T_t[:, cc_s(cc)], PB1[:, 0:G],
                OP.subtract), ["x2R", "PB1"], ["x2R"])
            dve(lambda e, cc=cci: e.tensor_tensor(
                x2T_t[:, cc_s(cc)], x2T_t[:, cc_s(cc)], fx1Tg[:, cc_s(cc)],
                OP.add), ["x2R", "fx1"], ["x2R"])
        dma("SP", cpart_dram(xo_h, g), cpart_sbuf(x2T_t[:, :]),
            ["x2R"], [f"xost{g}"])

    load_wm(0, 0)
    stage_Cf(0)
    stage_Df(0)
    for s in range(NSG):
        load_fx1(s)
        stage_Af(s)
        if s + 1 < NSG:
            stage_Cf(s + 1)
        stage_F(s)
        if s + 1 < NSG:
            load_wm(s + 1, (s + 1) % 2)
            stage_Df(s + 1)
    tail_load(NSG)
    for g in range(NSG, NG):
        if g + 1 < NG:
            tail_load(g + 1)
        stage_Ft(g)

    final = S.snapshot()
    S.fence("SP", {s: final[s] for s in ("SP", "GP", "SPX", "GPX")})

    with nc.Block() as blk:
        def emitter(name):
            def run(eng):
                for waits, fn, inc, sem, tag in S.ops[name]:
                    for (de, dv) in waits:
                        eng.wait_ge(sems[de], dv)
                    if fn is not None:
                        inst = fn(eng)
                        if inc:
                            inst.then_inc(sems[sem], inc)
            return run

        blk.tensor(emitter("PE"))
        blk.vector(emitter("DVE"))
        blk.scalar(emitter("ACT"))
        blk.sync(emitter("SP"))
        blk.gpsimd(emitter("GP"))

    stack.close()
    return nc


_NC_CACHE = {}


def kernel(L0_attn_x, L0_attn_time_decay, L0_attn_time_first,
           L0_attn_time_mix_k, L0_attn_time_mix_v, L0_attn_time_mix_r,
           L0_attn_Wk, L0_attn_Wv, L0_attn_Wr, L0_attn_Wo,
           L0_ffn_w_router, L0_ffn_W1, L0_ffn_b1, L0_ffn_W2, L0_ffn_b2):
    import ml_dtypes
    x = np.asarray(L0_attn_x, np.float32)
    B = x.shape[0]
    assert B == 8 and x.shape[1] == T and x.shape[2] == C

    colize = lambda v: np.ascontiguousarray(
        np.asarray(v, np.float32).reshape(CC, 128).T)
    cst = np.zeros((128, 4), np.float32)
    cst[:, 0] = 1.0
    cst[:, 1] = LN_EPS
    cst[:, 2] = 1.0 / C

    common = dict(
        wkT=np.ascontiguousarray(np.asarray(L0_attn_Wk, np.float32).T),
        wvT=np.ascontiguousarray(np.asarray(L0_attn_Wv, np.float32).T),
        wrT=np.ascontiguousarray(np.asarray(L0_attn_Wr, np.float32).T),
        woT=np.ascontiguousarray(np.asarray(L0_attn_Wo, np.float32).T),
        w1=np.ascontiguousarray(np.asarray(L0_ffn_W1, np.float32)),
        w2=np.ascontiguousarray(np.asarray(L0_ffn_W2, np.float32)),
        tmk=colize(np.asarray(L0_attn_time_mix_k).reshape(-1)),
        tmv=colize(np.asarray(L0_attn_time_mix_v).reshape(-1)),
        tmr=colize(np.asarray(L0_attn_time_mix_r).reshape(-1)),
        td=colize(L0_attn_time_decay), tf=colize(L0_attn_time_first),
        wrt=colize(L0_ffn_w_router),
        b1=np.ascontiguousarray(np.asarray(L0_ffn_b1, np.float32).reshape(FC, 128).T),
        b2=colize(L0_ffn_b2),
        iota=np.ascontiguousarray(
            (np.arange(64)[None, :] * 128 +
             np.arange(128)[:, None]).astype(np.int32)),
        triu=np.triu(np.ones((128, 128), np.float32), 1),
        onescol=np.ones((128, 1), np.float32),
        onescolb=np.ones((128, 1), ml_dtypes.bfloat16),
        onesrow=np.ones((1, 128), np.float32),
        ident=np.eye(128, dtype=np.float32),
        cst=cst,
    )
    in_maps = []
    for b in range(B):
        m = dict(common)
        m["xT"] = np.ascontiguousarray(x[b].T)
        in_maps.append(m)

    if "v2" not in _NC_CACHE:
        _NC_CACHE["v2"] = build_program()
    nc = _NC_CACHE["v2"]
    res = run_bass_kernel_spmd(nc, in_maps, core_ids=list(range(B)))
    out = np.empty((B, T, C), np.float32)
    for b in range(B):
        out[b] = res.results[b]["xo"].T
    return out


# version 13
# speedup vs baseline: 1.1565x; 1.0966x over previous
import numpy as np
import concourse.bass as bass
import concourse.mybir as mybir
from concourse.bass_utils import run_bass_kernel_spmd
from concourse.alu_op_type import AluOpType as OP

F32 = mybir.dt.float32
BF16 = mybir.dt.bfloat16
AF = mybir.ActivationFunctionType
AX = mybir.AxisListType

C = 1024
T = 8192
CC = 8
G = 256
NG = T // G
F = 4096
FC = F // 128
KSEL = T // 2
N_BIS = 28
LN_EPS = 1e-5


class Sched:

    ENGINES = ("PE", "DVE", "ACT", "SP", "GP")
    SEMS = ("PE", "DVE", "ACT", "SP", "GP", "SPX", "GPX", "AQ", "AQX")
    QUEUES = {"SP": ("SP", "SP", "SPX"), "GP": ("GP", "GP", "GPX"),
              "AQ": ("ACT", "AQ", "AQX")}

    def __init__(self):
        self.ops = {e: [] for e in self.ENGINES}
        self.cnt = {s: 0 for s in self.SEMS}
        self.buf = {}
        self.mark = {e: {s: 0 for s in self.SEMS} for e in self.ENGINES}

    def _waits_for(self, eng, deps):
        by_e = {}
        for (de, dv) in deps:
            by_e[de] = max(by_e.get(de, 0), dv)
        waits = []
        for de in self.SEMS:
            dv = by_e.get(de, 0)
            if dv > self.mark[eng][de]:
                waits.append((de, dv))
                self.mark[eng][de] = dv
        return waits

    def _deps(self, reads, writes):
        deps = []
        for b in reads:
            st = self.buf.get(b)
            if st and st["w"] is not None:
                deps.append(st["w"])
        for b in writes:
            st = self.buf.get(b)
            if st:
                if st["w"] is not None:
                    deps.append(st["w"])
                deps.extend(st["r"])
        return deps

    def _register(self, reads, writes, sem, val):
        for b in reads:
            self.buf.setdefault(b, {"w": None, "r": []})["r"].append((sem, val))
        for b in writes:
            self.buf[b] = {"w": (sem, val), "r": []}

    def op(self, eng, fn, reads=(), writes=(), tag=""):
        if eng in self.QUEUES:
            return self.dma_op(eng, fn, reads, writes, tag)
        waits = self._waits_for(eng, self._deps(reads, writes))
        self.cnt[eng] += 1
        self._register(reads, writes, eng, self.cnt[eng])
        self.ops[eng].append((waits, fn, 1, eng, tag))

    def dma_op(self, q, fn, reads=(), writes=(), tag="", drain=True):
        stream, sem, semx = self.QUEUES[q]
        waits = self._waits_for(stream, self._deps(reads, writes))
        self.cnt[sem] += 16
        self.ops[stream].append((waits, fn, 16, sem, tag))
        if drain:
            dw = self._waits_for(stream, [(sem, self.cnt[sem])])
            self.cnt[semx] += 1
            self._register(reads, writes, semx, self.cnt[semx])
            self.ops[stream].append((dw, lambda e: e.nop(), 1, semx,
                                     tag + "_dr"))

    def fence(self, eng, counts):
        waits = self._waits_for(eng, [(s, counts.get(s, 0)) for s in self.SEMS])
        if waits:
            self.ops[eng].append((waits, None, 0, eng, "fence"))

    def snapshot(self):
        return dict(self.cnt)


def build_program():
    nc = bass.Bass("TRN2", target_bir_lowering=False)

    xT_h = nc.dram_tensor("xT", [C, T], F32, kind="ExternalInput").ap()
    wkT_h = nc.dram_tensor("wkT", [C, C], F32, kind="ExternalInput").ap()
    wvT_h = nc.dram_tensor("wvT", [C, C], F32, kind="ExternalInput").ap()
    wrT_h = nc.dram_tensor("wrT", [C, C], F32, kind="ExternalInput").ap()
    woT_h = nc.dram_tensor("woT", [C, C], F32, kind="ExternalInput").ap()
    w1_h = nc.dram_tensor("w1", [C, F], F32, kind="ExternalInput").ap()
    w2_h = nc.dram_tensor("w2", [F, C], F32, kind="ExternalInput").ap()
    tmk_h = nc.dram_tensor("tmk", [128, CC], F32, kind="ExternalInput").ap()
    tmv_h = nc.dram_tensor("tmv", [128, CC], F32, kind="ExternalInput").ap()
    tmr_h = nc.dram_tensor("tmr", [128, CC], F32, kind="ExternalInput").ap()
    td_h = nc.dram_tensor("td", [128, CC], F32, kind="ExternalInput").ap()
    tf_h = nc.dram_tensor("tf", [128, CC], F32, kind="ExternalInput").ap()
    wrt_h = nc.dram_tensor("wrt", [128, CC], F32, kind="ExternalInput").ap()
    b1_h = nc.dram_tensor("b1", [128, FC], F32, kind="ExternalInput").ap()
    b2_h = nc.dram_tensor("b2", [128, CC], F32, kind="ExternalInput").ap()
    onescol_h = nc.dram_tensor("onescol", [128, 1], F32, kind="ExternalInput").ap()
    onescolb_h = nc.dram_tensor("onescolb", [128, 1], BF16, kind="ExternalInput").ap()
    onesrow_h = nc.dram_tensor("onesrow", [1, 128], F32, kind="ExternalInput").ap()
    ident_h = nc.dram_tensor("ident", [128, 128], F32, kind="ExternalInput").ap()
    cst_h = nc.dram_tensor("cst", [128, 4], F32, kind="ExternalInput").ap()
    iota_h = nc.dram_tensor("iota", [128, 64], mybir.dt.int32,
                            kind="ExternalInput").ap()
    triu_h = nc.dram_tensor("triu", [128, 128], F32, kind="ExternalInput").ap()
    xo_h = nc.dram_tensor("xo", [C, T], F32, kind="ExternalOutput").ap()
    x1T_h = nc.dram_tensor("x1T", [C, T], F32, kind="Internal").ap()
    srow_h = nc.dram_tensor("srow", [1, T], F32, kind="Internal").ap()
    lncr_h = nc.dram_tensor("lncr", [1, T], F32, kind="Internal").ap()
    lncm_h = nc.dram_tensor("lncm", [1, T], F32, kind="Internal").ap()
    wmrow_h = nc.dram_tensor("wmrow", [1, T], F32, kind="Internal").ap()
    x1rows_h = nc.dram_tensor("x1rows", [T, C], BF16, kind="Internal").ap()
    ufull_h = nc.dram_tensor("ufull", [T, C], BF16, kind="Internal").ap()
    idxb_h = nc.dram_tensor("idxb", [T, 1], mybir.dt.int32, kind="Internal").ap()

    from contextlib import ExitStack
    stack = ExitStack()
    sb = lambda n, s, t: stack.enter_context(nc.sbuf_tensor(n, s, t))
    psm = lambda n, s: stack.enter_context(nc.psum_tensor(n, s, F32))

    AW = sb("AW", [128, 32768], BF16)
    AR1 = sb("AR1", [128, 32768], BF16)
    AR2 = sb("AR2", [128, 32768], BF16)

    ident_t = sb("identt", [128, 128], F32)
    identb_t = sb("identbt", [128, 128], BF16)
    onescol_t = sb("onescolt", [128, 1], F32)
    onescolb_t = sb("onescolbt", [128, 1], BF16)
    onesrow_t = sb("onesrowt", [1, 128], F32)
    cst_t = sb("cstt", [128, 4], F32)
    tmk_t = sb("tmkt", [128, CC], F32)
    tmv_t = sb("tmvt", [128, CC], F32)
    tmr_t = sb("tmrt", [128, CC], F32)
    td_t = sb("tdt", [128, CC], F32)
    tf_t = sb("tft", [128, CC], F32)
    wrt_t = sb("wrtt", [128, CC], F32)
    b1_t = sb("b1t", [128, FC], F32)
    b2_t = sb("b2t", [128, CC], F32)
    eu_t = sb("eut", [128, CC], F32)
    dd_t = sb("ddt", [128, CC], F32)
    rows_t = sb("rowst", [1, 1792], F32)
    scoreT_t = sb("scoreTt", [128, 64], F32)
    geT_t = sb("geTt", [128, 64], F32)
    sigT_t = sb("sigTt", [128, 64], F32)
    bis_t = sb("bist", [1, 16], F32)
    wmtr_t = sb("wmtrt", [64, 128], F32)
    wmr_t = sb("wmrt", [1, 2 * G], F32)
    iota_t = sb("iotat", [128, 64], mybir.dt.int32)
    triu_t = sb("triut", [128, 128], F32)
    posI_t = sb("posIt", [128, 64], mybir.dt.int32)
    idxc_t = sb("idxct", [128, 4], mybir.dt.int32)
    fcol_t = sb("fcolt", [128, 8], F32)
    csB_t = sb("csBt", [1, 64], F32)
    csC_t = sb("csCt", [64, 128], F32)
    wmb2_t = sb("wmb2t", [128, G], BF16)

    def carve(arena, off_bytes, cols, dt):
        el = off_bytes // 2
        w = cols * (2 if dt == F32 else 1)
        ap = arena[:, el:el + w]
        if dt == F32:
            ap = ap.bitcast(F32)
        return ap

    def attnW(m, cci):
        off = (m * CC + cci) * 1024
        return AW[:, off:off + 1024]

    def w2W(fi):
        return AW[:, fi * 1024:(fi + 1) * 1024]

    def w1W(cci):
        return AR1[:, cci * 4096:(cci + 1) * 4096]

    xTg = [carve(AR1, 0, 2048, F32), carve(AR1, 8192, 2048, F32),
           carve(AR1, 16384, 2048, F32)]
    EK_p = [carve(AR1, 24576, 2048, F32), carve(AR1, 32768, 2048, F32)]
    EKV_p = [carve(AR1, 40960, 2048, F32), carve(AR1, 49152, 2048, F32)]
    SCR = carve(AR1, 57344, 2048, BF16)
    XBF = carve(AR1, 61440, 2048, BF16)

    xk_p = [carve(AR2, 0, 2048, BF16), carve(AR2, 12288, 2048, BF16)]
    xv_p = [carve(AR2, 4096, 2048, BF16), carve(AR2, 16384, 2048, BF16)]
    xr_p = [carve(AR2, 8192, 2048, BF16), carve(AR2, 20480, 2048, BF16)]
    sig_p = [carve(AR2, 24576, 2048, BF16), carve(AR2, 28672, 2048, BF16)]
    Asc_t = carve(AR2, 32768, CC * 257, F32)
    Bsc_t = carve(AR2, 40992, CC * 257, F32)
    xsh_t = carve(AR2, 49216, CC * 257, BF16)
    ATX = carve(AR2, 53328, 2048, F32)
    XBF2 = carve(AR2, 57344, 2048, BF16)

    P = [psm(f"PS{i}", [128, 512]) for i in range(2)]
    PN = [f"PS{i}" for i in range(2)]
    PW = [psm(f"PW{i}", [128, 512]) for i in range(2)]
    PWN = [f"PW{i}" for i in range(2)]
    PSTS = psm("PSTS", [1, 256])
    PSTQ = psm("PSTQ", [1, 256])
    PB0 = psm("PB0", [128, 512])
    PB1 = psm("PB1", [128, 512])

    sems = {}
    for e in Sched.SEMS:
        sems[e] = stack.enter_context(nc.semaphore("sem" + e))
    S = Sched()

    def dma(q, out, in_, reads, writes, tag="", drain=True, transpose=False):
        S.dma_op(q, lambda e, o=out, i=in_, tr=transpose: e.dma_start(
            out=o, in_=i, transpose=tr), reads, writes, tag, drain=drain)

    def pe(fn, reads, writes, tag=""):
        S.op("PE", fn, reads, writes, tag)

    def dve(fn, reads, writes, tag=""):
        S.op("DVE", fn, reads, writes, tag)

    def act(fn, reads, writes, tag=""):
        S.op("ACT", fn, reads, writes, tag)

    cc_s = lambda cc: slice(cc * G, (cc + 1) * G)
    cb_s = lambda cc: slice(cc * 257 + 1, cc * 257 + 257)
    cs_s = lambda cc: slice(cc * 257, cc * 257 + 256)
    bnd_s = slice(0, CC * 257, 257)
    lst_s = slice(256, CC * 257, 257)

    RW_M, RW_MSQ, RW_VAR, RW_RSTD, RW_MR, RW_S, RW_T1 = (
        0, 256, 512, 768, 1024, 1280, 1536)
    row = lambda off: rows_t[0:1, off:off + G]
    PSTs = PSTS[0:1, 0:256]
    PSTq = PSTQ[0:1, 0:256]

    def mm_contract(e, lhsT_fn, rhs_t, out_ap, n, rhs_sl_fn):
        last = None
        for i in range(n):
            last = e.matmul(out_ap, lhsT_fn(i), rhs_t[:, rhs_sl_fn(i)],
                            start=(i == 0), stop=(i == n - 1))
        return last

    def stat_bf(e, rhs_t, dst):
        return mm_contract(e, lambda i: onescolb_t[:, :], rhs_t,
                           dst, CC, cc_s)

    def stat_f32(e, rhs_t, dst):
        return mm_contract(e, lambda i: onescol_t[:, :], rhs_t,
                           dst, CC, cc_s)

    def rowchain(tag=""):
        dve(lambda e: e.tensor_scalar(row(RW_M), PSTs, 1.0 / C,
                                      None, OP.mult), ["PSTs"], ["rm"], tag)
        dve(lambda e: e.tensor_tensor(row(RW_MSQ), row(RW_M), row(RW_M),
                                      OP.mult), ["rm"], ["rmsq"], tag)
        dve(lambda e: e.scalar_tensor_tensor(row(RW_VAR), PSTq,
                                             1.0 / C, row(RW_MSQ),
                                             OP.mult, OP.subtract),
            ["PSTq", "rmsq"], ["rvar"], tag)
        act(lambda e: e.activation(row(RW_RSTD), row(RW_VAR), AF.Ln,
                                   bias=cst_t[0:1, 1:2]),
            ["rvar", "cst"], ["rrstd"], tag)
        act(lambda e: e.activation(row(RW_RSTD), row(RW_RSTD), AF.Exp,
                                   scale=-0.5), ["rrstd"], ["rrstd"], tag)
        dve(lambda e: e.tensor_tensor(row(RW_MR), row(RW_M), row(RW_RSTD),
                                      OP.mult), ["rm", "rrstd"], ["rmr"], tag)

    def bcast_pair(extra_reads=(), tag=""):
        pe(lambda e: e.matmul(PB0[:, 0:G], onesrow_t[:, :], row(RW_RSTD),
                              start=True, stop=True),
           ["onesrow", "rrstd"] + list(extra_reads), ["PB0"], tag)
        pe(lambda e: e.matmul(PB1[:, 0:G], onesrow_t[:, :], row(RW_MR),
                              start=True, stop=True),
           ["onesrow", "rmr"] + list(extra_reads), ["PB1"], tag)

    mmro = [0]

    def next_p():
        i = mmro[0] % 2
        mmro[0] += 1
        return P[i], PN[i]

    mmw = [0]

    def next_pw():
        i = mmw[0] % 2
        mmw[0] += 1
        return PW[i], PWN[i]

    dma("SP", ident_t[:, :], ident_h, [], ["ident"])
    dma("SP", onescol_t[:, :], onescol_h, [], ["onescol"])
    dma("SP", onescolb_t[:, :], onescolb_h, [], ["onescolb"])
    dma("SP", onesrow_t[:, :], onesrow_h, [], ["onesrow"])
    dma("SP", cst_t[:, :], cst_h, [], ["cst"])
    for nm, t, h in [("tmk", tmk_t, tmk_h), ("tmv", tmv_t, tmv_h),
                     ("tmr", tmr_t, tmr_h), ("td", td_t, td_h),
                     ("tf", tf_t, tf_h), ("wrt", wrt_t, wrt_h),
                     ("b1", b1_t, b1_h), ("b2", b2_t, b2_h)]:
        dma("SP", t[:, :], h, [], [nm])

    for m, h in enumerate([wkT_h, wvT_h, wrT_h, woT_h]):
        for cci in range(CC):
            dma("GP", attnW(m, cci), h[cci * 128:(cci + 1) * 128, :],
                [], [f"aw{m}_{cci}"])

    dma("SP", iota_t[:, :], iota_h, [], ["iota"])
    dma("SP", triu_t[:, :], triu_h, [], ["triu"])
    dve(lambda e: e.memset(SCR[:, 0:1024], 0.0), [], ["SCR"])
    for k in range(T // 128):
        dma("GP", ufull_h[k * 128:(k + 1) * 128, :],
            SCR[:, 0:1024], ["SCR"], ["ufull"], f"uz{k}",
            drain=(k == T // 128 - 1))
    dve(lambda e: e.tensor_copy(identb_t[:, :], ident_t[:, :]),
        ["ident"], ["identb"])
    act(lambda e: e.activation(eu_t[:, :], tf_t[:, :], AF.Exp), ["tf"], ["eu"])
    act(lambda e: e.activation(dd_t[:, :], td_t[:, :], AF.Exp), ["td"], ["dd"])
    dve(lambda e: e.tensor_scalar(dd_t[:, :], dd_t[:, :], -1.0, None, OP.mult),
        ["dd"], ["dd"])
    act(lambda e: e.activation(dd_t[:, :], dd_t[:, :], AF.Exp), ["dd"], ["dd"])

    dve(lambda e: e.memset(xsh_t[:, bnd_s], 0.0), [], ["xsh_b"])
    dve(lambda e: e.memset(Asc_t[:, bnd_s], 0.0), [],
        [f"A{i}b" for i in range(CC)])
    dve(lambda e: e.memset(Bsc_t[:, bnd_s], 0.0), [],
        [f"B{i}b" for i in range(CC)])

    pe(lambda e: mm_contract(e, lambda i: wrt_t[:, i:i + 1], onescol_t,
                             PSTS[0:1, 0:1], CC, lambda i: slice(0, 1)),
       ["wrt", "onescol"], ["PSTs"], "swr")
    dve(lambda e: e.tensor_copy(bis_t[0:1, 8:9], PSTS[0:1, 0:1]),
        ["PSTs"], ["swr"])

    def cpart_dram(h, g):
        return h[:, g * G:(g + 1) * G].rearrange("(cc p) j -> p cc j", p=128)

    def cpart_sbuf(ap):
        return ap.rearrange("p (cc j) -> p cc j", cc=CC)

    def load_xT(g):
        b = g % 3
        dma("SP", cpart_sbuf(xTg[b][:, :]), cpart_dram(xT_h, g),
            [], [f"xTg{b}"], f"ldx{g}")

    def stage_S1(g):
        buf = g % 3
        par = g % 2
        xg = f"xTg{buf}"
        xk_t, xv_t, xr_t = xk_p[par], xv_p[par], xr_p[par]
        if g > 0:
            dve(lambda e: e.tensor_copy(xsh_t[:, bnd_s], xsh_t[:, lst_s]),
                [f"xsh{i}" for i in range(CC)], ["xsh_b"])
        act(lambda e, b=buf: e.activation(SCR[:, :], xTg[b][:, :], AF.Square),
            [xg, "SCR"], ["SCR"], "sqa")
        dve(lambda e, b=buf: e.tensor_copy(XBF[:, :], xTg[b][:, :]),
            [xg, "XBF"], ["XBF"])
        pe(lambda e: stat_bf(e, XBF, PSTs),
           ["XBF", "onescolb", "PSTs"], ["PSTs"], "sum_a")
        pe(lambda e: stat_bf(e, SCR, PSTq),
           ["SCR", "onescolb", "PSTq"], ["PSTq"], "sq_a")
        rowchain("ln_a")
        bcast_pair(tag="bc_a")

        for cci in range(CC):
            dve(lambda e, cc=cci, b=buf: e.tensor_tensor(
                xsh_t[:, cb_s(cc)], xTg[b][:, cc_s(cc)], PB0[:, 0:G], OP.mult),
                [xg, "PB0"], [f"xsh{cci}"])
        for cci in range(CC):
            dve(lambda e, cc=cci: e.tensor_tensor(
                xsh_t[:, cb_s(cc)], xsh_t[:, cb_s(cc)], PB1[:, 0:G],
                OP.subtract), [f"xsh{cci}", "PB1"], [f"xsh{cci}"])

        for cci in range(CC):
            dve(lambda e, cc=cci, t=xk_t: e.tensor_tensor(
                t[:, cc_s(cc)], xsh_t[:, cb_s(cc)], xsh_t[:, cs_s(cc)],
                OP.subtract), [f"xsh{cci}", "xsh_b"], [f"xk{par}_{cci}"])
        for tname, tile, tmname, tmcol in [("xr", xr_t, "tmr", tmr_t),
                                           ("xv", xv_t, "tmv", tmv_t)]:
            for cci in range(CC):
                dve(lambda e, cc=cci, tl=tile, tm=tmcol, kk=xk_t:
                    e.scalar_tensor_tensor(
                        tl[:, cc_s(cc)], kk[:, cc_s(cc)], tm[:, cc:cc + 1],
                        xsh_t[:, cs_s(cc)], OP.mult, OP.add),
                    [f"xk{par}_{cci}", tmname, f"xsh{cci}", "xsh_b"],
                    [f"{tname}{par}_{cci}"])
        for cci in range(CC):
            dve(lambda e, cc=cci, kk=xk_t: e.scalar_tensor_tensor(
                kk[:, cc_s(cc)], kk[:, cc_s(cc)], tmk_t[:, cc:cc + 1],
                xsh_t[:, cs_s(cc)], OP.mult, OP.add),
                [f"xk{par}_{cci}", "tmk", f"xsh{cci}", "xsh_b"],
                [f"xk{par}_{cci}"])

    def stage_S2(g):
        par = g % 2
        EK_t, EKV_t, sig_t = EK_p[par], EKV_p[par], sig_p[par]
        xk_t, xv_t, xr_t = xk_p[par], xv_p[par], xr_p[par]
        XK = [f"xk{par}_{i}" for i in range(CC)]
        XV = [f"xv{par}_{i}" for i in range(CC)]
        XR = [f"xr{par}_{i}" for i in range(CC)]
        for cco in range(CC):
            pk, pkn = next_p()
            pv, pvn = next_p()
            pe(lambda e, p=pk, co=cco, t=xk_t: mm_contract(
                e, lambda i, co_=co: attnW(0, i)[:, co_ * 128:(co_ + 1) * 128],
                t, p[:, 0:G], CC, cc_s),
               XK + [f"aw0_{i}" for i in range(CC)], [pkn], f"km{g}_{cco}")
            act(lambda e, p=pk, cc=cco, t=EK_t: e.activation(
                t[:, cc_s(cc)], p[:, 0:G], AF.Exp), [pkn], [f"EK{par}_{cco}"])
            pe(lambda e, p=pv, co=cco, t=xv_t: mm_contract(
                e, lambda i, co_=co: attnW(1, i)[:, co_ * 128:(co_ + 1) * 128],
                t, p[:, 0:G], CC, cc_s),
               XV + [f"aw1_{i}" for i in range(CC)], [pvn], f"vm{g}_{cco}")
            dve(lambda e, p=pv, cc=cco, ek=EK_t, ev=EKV_t: e.tensor_tensor(
                ev[:, cc_s(cc)], ek[:, cc_s(cc)], p[:, 0:G], OP.mult),
                [f"EK{par}_{cco}", pvn], [f"EKV{par}_{cco}"])
        for cco in range(CC):
            pr, prn = next_p()
            pe(lambda e, p=pr, co=cco, t=xr_t: mm_contract(
                e, lambda i, co_=co: attnW(2, i)[:, co_ * 128:(co_ + 1) * 128],
                t, p[:, 0:G], CC, cc_s),
               XR + [f"aw2_{i}" for i in range(CC)], [prn], f"rm{g}_{cco}")
            act(lambda e, p=pr, cc=cco, t=sig_t: e.activation(
                t[:, cc_s(cc)], p[:, 0:G], AF.Sigmoid),
                [prn], [f"sig{par}_{cco}"])

    def stage_S3(g):
        par = g % 2
        EK_t, EKV_t, sig_t = EK_p[par], EKV_p[par], sig_p[par]
        ekn = lambda cc: f"EK{par}_{cc}"
        evn = lambda cc: f"EKV{par}_{cc}"
        for cci in range(CC):
            dve(lambda e, cc=cci, ev=EKV_t: e.tensor_tensor_scan(
                Asc_t[:, cb_s(cc)], dd_t[:, cc:cc + 1].broadcast_to([128, G]),
                ev[:, cc_s(cc)], Asc_t[:, cc * 257:cc * 257 + 1],
                OP.mult, OP.add),
                [evn(cci), "dd", f"A{cci}b"], [f"A{cci}"])
            dve(lambda e, cc=cci, ek=EK_t: e.tensor_tensor_scan(
                Bsc_t[:, cb_s(cc)], dd_t[:, cc:cc + 1].broadcast_to([128, G]),
                ek[:, cc_s(cc)], Bsc_t[:, cc * 257:cc * 257 + 1],
                OP.mult, OP.add),
                [ekn(cci), "dd", f"B{cci}b"], [f"B{cci}"])

        for cci in range(CC):
            dve(lambda e, cc=cci, ev=EKV_t: e.scalar_tensor_tensor(
                ev[:, cc_s(cc)], Asc_t[:, cs_s(cc)], eu_t[:, cc:cc + 1],
                ev[:, cc_s(cc)], OP.mult, OP.add),
                [f"A{cci}", f"A{cci}b", "eu", evn(cci)], [evn(cci)])
            dve(lambda e, cc=cci, ek=EK_t: e.scalar_tensor_tensor(
                ek[:, cc_s(cc)], Bsc_t[:, cs_s(cc)], eu_t[:, cc:cc + 1],
                ek[:, cc_s(cc)], OP.mult, OP.add),
                [f"B{cci}", f"B{cci}b", "eu", ekn(cci)], [ekn(cci)])
        dve(lambda e: e.tensor_copy(Asc_t[:, bnd_s], Asc_t[:, lst_s]),
            [f"A{i}" for i in range(CC)], [f"A{i}b" for i in range(CC)])
        dve(lambda e: e.tensor_copy(Bsc_t[:, bnd_s], Bsc_t[:, lst_s]),
            [f"B{i}" for i in range(CC)], [f"B{i}b" for i in range(CC)])
        EKall = [ekn(i) for i in range(CC)]
        EVall = [evn(i) for i in range(CC)]
        act(lambda e, ek=EK_t: e.activation(ek[:, :], ek[:, :], AF.Ln),
            EKall, EKall, "invD")
        act(lambda e, ek=EK_t: e.activation(ek[:, :], ek[:, :], AF.Exp,
                                            scale=-1.0), EKall, EKall, "invD")
        dve(lambda e, ek=EK_t, ev=EKV_t: e.tensor_tensor(
            ev[:, :], ev[:, :], ek[:, :], OP.mult),
            EKall + EVall, EVall, "ymul")
        dve(lambda e, ev=EKV_t, sg=sig_t: e.tensor_tensor(
            sg[:, :], ev[:, :], sg[:, :], OP.mult),
            EVall + [f"sig{par}_{i}" for i in range(CC)],
            [f"ry{par}"], "rymul")

    def stage_S4(g):
        buf = g % 3
        par = g % 2
        xg = f"xTg{buf}"
        sig_t = sig_p[par]
        for cco in range(CC):
            po, pon = next_pw()
            pe(lambda e, p=po, co=cco, t=sig_t: mm_contract(
                e, lambda i, co_=co: attnW(3, i)[:, co_ * 128:(co_ + 1) * 128],
                t, p[:, 0:G], CC, cc_s),
               [f"ry{par}"] + [f"sig{par}_{i}" for i in range(CC)] +
               [f"aw3_{i}" for i in range(CC)],
               [pon], f"om{g}_{cco}")
            act(lambda e, p=po, cc=cco: e.activation(
                ATX[:, cc_s(cc)], p[:, 0:G], AF.Copy), [pon],
                [f"at{cco}", f"x1{cco}"])

        ATall = [f"at{i}" for i in range(CC)]
        act(lambda e: e.activation(SCR[:, :], ATX[:, :], AF.Square),
            ATall + ["SCR"], ["SCR"], "sqb")
        dve(lambda e: e.tensor_copy(XBF[:, :], ATX[:, :]),
            ATall + ["XBF"], ["XBF"])
        pe(lambda e: stat_bf(e, XBF, PSTs),
           ["XBF", "onescolb", "PSTs"], ["PSTs"], "sum_b")
        pe(lambda e: stat_bf(e, SCR, PSTq),
           ["SCR", "onescolb", "PSTq"], ["PSTq"], "sq_b")
        rowchain("ln_b")
        bcast_pair(tag="bc_b")
        for cci in range(CC):
            dve(lambda e, cc=cci: e.tensor_tensor(
                ATX[:, cc_s(cc)], ATX[:, cc_s(cc)], PB0[:, 0:G],
                OP.mult), [f"at{cci}", "PB0"], [f"x1{cci}", f"at{cci}"])
        for cci in range(CC):
            dve(lambda e, cc=cci: e.tensor_tensor(
                ATX[:, cc_s(cc)], ATX[:, cc_s(cc)], PB1[:, 0:G],
                OP.subtract), [f"x1{cci}", "PB1"], [f"x1{cci}"])
        for cci in range(CC):
            dve(lambda e, cc=cci, b=buf: e.tensor_tensor(
                ATX[:, cc_s(cc)], ATX[:, cc_s(cc)], xTg[b][:, cc_s(cc)],
                OP.add), [f"x1{cci}", xg], [f"x1{cci}"])

        X1ALL = [f"x1{i}" for i in range(CC)]
        act(lambda e: e.activation(SCR[:, :], ATX[:, :], AF.Square),
            X1ALL + ["SCR"], ["SCR"], "sqc")
        dve(lambda e: e.tensor_copy(XBF[:, :], ATX[:, :]),
            X1ALL + ["XBF"], ["XBF"])
        pe(lambda e: stat_bf(e, XBF, PSTs),
           ["XBF", "onescolb", "PSTs"], ["PSTs"], "sum_c")
        pe(lambda e: stat_bf(e, SCR, PSTq),
           ["SCR", "onescolb", "PSTq"], ["PSTq"], "sq_c")
        rowchain("ln_c")
        pe(lambda e: mm_contract(e, lambda i: wrt_t[:, i:i + 1], ATX,
                                 PSTs, CC, cc_s),
           X1ALL + ["wrt"], ["PSTs"], "smm")
        dve(lambda e: e.tensor_scalar(row(RW_T1), row(RW_M), bis_t[0:1, 8:9],
                                      None, OP.mult), ["rm", "swr"], ["rt1"])
        dve(lambda e: e.tensor_tensor(row(RW_T1), PSTs,
                                      row(RW_T1), OP.subtract),
            ["PSTs", "rt1"], ["rt1"])
        dve(lambda e: e.tensor_tensor(row(RW_S), row(RW_T1), row(RW_RSTD),
                                      OP.mult), ["rt1", "rrstd"], ["rs"])
        dma("GP", srow_h[0:1, g * G:(g + 1) * G], row(RW_S), ["rs"],
            [f"srow{g}"])
        dma("GP", lncr_h[0:1, g * G:(g + 1) * G], row(RW_RSTD), ["rrstd"],
            [f"lncr{g}"])
        dma("GP", lncm_h[0:1, g * G:(g + 1) * G], row(RW_MR), ["rmr"],
            [f"lncm{g}"])

        dma("SP", cpart_dram(x1T_h, g), cpart_sbuf(ATX[:, :]),
            X1ALL, [f"x1st{g}"])
        for h in range(2):
            for cc in range(CC):
                pt, ptn = next_pw()
                pe(lambda e, p=pt, hh=h, c=cc: e.transpose(
                    p[:, 0:64].bitcast(BF16),
                    XBF[:, c * G + hh * 128:c * G + (hh + 1) * 128],
                    identb_t[:, :]),
                   ["XBF", "identb"], [ptn], f"x1t{g}_{h}_{cc}")
                act(lambda e, p=pt, hh=h, c=cc: e.activation(
                    SCR[:, hh * 1024 + c * 128:hh * 1024 + (c + 1) * 128],
                    p[:, 0:64].bitcast(BF16), AF.Copy),
                    [ptn, "SCR"], ["SCR"], f"x1e{g}_{h}_{cc}")
        dma("SP", x1rows_h[g * G:(g + 1) * G, :].rearrange(
            "(h p) c -> p h c", p=128),
            SCR[:, :].rearrange("p (h c) -> p h c", h=2),
            ["SCR"], ["x1rows"], f"x1r{g}")

    for i in range(-2, NG):
        if i + 2 < NG:
            load_xT(i + 2)
        if 0 <= i + 1 < NG:
            stage_S2(i + 1)
        if i >= 0:
            stage_S3(i)
        if i + 2 < NG:
            stage_S1(i + 2)
        if i >= 0:
            stage_S4(i)

    attn_end = S.snapshot()
    S.fence("GP", attn_end)
    for cci in range(CC):
        dma("GP", w1W(cci), w1_h[cci * 128:(cci + 1) * 128, :], [], [f"w1_{cci}"])
    for fi in range(FC):
        dma("GP", w2W(fi), w2_h[fi * 128:(fi + 1) * 128, :], [], [f"w2_{fi}"])

    dma("SP", wmtr_t[:, :], srow_h.rearrange("a (c p) -> (a c) p", p=128),
        [f"srow{gg}" for gg in range(NG)], ["wmtr"])
    pe(lambda e: e.transpose(PB0[:, 0:64], wmtr_t[:, :], ident_t[0:64, 0:64]),
       ["wmtr", "ident"], ["PB0"], "str")
    dve(lambda e: e.tensor_copy(scoreT_t[:, :], PB0[:, 0:64]),
        ["PB0"], ["scoreT"])

    dve(lambda e: e.reduce_max(geT_t[:, 0:1], scoreT_t[:, :], axis=AX.X,
                               apply_absolute_value=True),
        ["scoreT"], ["geT"])
    pe(lambda e: e.matmul(PSTS[0:1, 0:1], geT_t[:, 0:1], onescol_t[:, 0:1],
                          start=True, stop=True),
       ["geT", "onescol"], ["PSTs"], "bsum")
    dve(lambda e: e.tensor_scalar(bis_t[0:1, 1:2], PSTS[0:1, 0:1], 1.0, 1.0,
                                  OP.mult, OP.add), ["PSTs"], ["hi"])
    dve(lambda e: e.tensor_scalar(bis_t[0:1, 0:1], bis_t[0:1, 1:2], -1.0,
                                  None, OP.mult), ["hi"], ["lo"])
    dve(lambda e: e.tensor_tensor(bis_t[0:1, 2:3], bis_t[0:1, 0:1],
                                  bis_t[0:1, 1:2], OP.add), ["lo", "hi"], ["th"])
    dve(lambda e: e.tensor_scalar(bis_t[0:1, 2:3], bis_t[0:1, 2:3], 0.5, None,
                                  OP.mult), ["th"], ["th"])

    for it in range(N_BIS):
        pe(lambda e: e.matmul(PB0[:, 0:1], onesrow_t[:, :], bis_t[0:1, 2:3],
                              start=True, stop=True),
           ["th", "onesrow", "PB0"], ["PB0"], f"bth{it}")
        dve(lambda e: e.tensor_scalar(geT_t[:, :], scoreT_t[:, :],
                                      PB0[:, 0:1], None, OP.is_ge),
            ["scoreT", "PB0"], ["geT"])
        dve(lambda e: e.reduce_sum(sigT_t[:, 0:1], geT_t[:, :], axis=AX.X),
            ["geT"], ["sigT"])
        pe(lambda e: e.matmul(PSTS[0:1, 0:1], sigT_t[:, 0:1],
                              onescol_t[:, 0:1], start=True, stop=True),
           ["sigT", "onescol", "PSTs"], ["PSTs"], f"bcnt{it}")
        dve(lambda e: e.tensor_scalar(bis_t[0:1, 3:4], PSTS[0:1, 0:1],
                                      float(KSEL), None, OP.is_ge),
            ["PSTs"], ["flag"])
        dve(lambda e: e.tensor_tensor(bis_t[0:1, 4:5], bis_t[0:1, 2:3],
                                      bis_t[0:1, 0:1], OP.subtract),
            ["th", "lo"], ["t4"])
        dve(lambda e: e.tensor_tensor(bis_t[0:1, 4:5], bis_t[0:1, 4:5],
                                      bis_t[0:1, 3:4], OP.mult),
            ["t4", "flag"], ["t4"])
        dve(lambda e: e.tensor_tensor(bis_t[0:1, 0:1], bis_t[0:1, 0:1],
                                      bis_t[0:1, 4:5], OP.add),
            ["lo", "t4"], ["lo"])
        dve(lambda e: e.tensor_tensor(bis_t[0:1, 5:6], bis_t[0:1, 1:2],
                                      bis_t[0:1, 2:3], OP.subtract),
            ["hi", "th"], ["t5"])
        dve(lambda e: e.tensor_tensor(bis_t[0:1, 5:6], bis_t[0:1, 5:6],
                                      bis_t[0:1, 3:4], OP.mult),
            ["t5", "flag"], ["t5"])
        dve(lambda e: e.tensor_tensor(bis_t[0:1, 1:2], bis_t[0:1, 2:3],
                                      bis_t[0:1, 5:6], OP.add),
            ["th", "t5"], ["hi"])
        dve(lambda e: e.tensor_tensor(bis_t[0:1, 2:3], bis_t[0:1, 0:1],
                                      bis_t[0:1, 1:2], OP.add),
            ["lo", "hi"], ["th"])
        dve(lambda e: e.tensor_scalar(bis_t[0:1, 2:3], bis_t[0:1, 2:3], 0.5,
                                      None, OP.mult), ["th"], ["th"])

    pe(lambda e: e.matmul(PB0[:, 0:1], onesrow_t[:, :], bis_t[0:1, 0:1],
                          start=True, stop=True),
       ["lo", "onesrow", "PB0"], ["PB0"], "bfin")
    dve(lambda e: e.tensor_scalar(geT_t[:, :], scoreT_t[:, :], PB0[:, 0:1],
                                  None, OP.is_ge), ["scoreT", "PB0"], ["geT"])
    act(lambda e: e.activation(sigT_t[:, :], scoreT_t[:, :], AF.Sigmoid),
        ["scoreT", "sigT"], ["sigT"])
    dve(lambda e: e.tensor_tensor(sigT_t[:, :], geT_t[:, :], sigT_t[:, :],
                                  OP.mult), ["geT", "sigT"], ["sigT"])
    pe(lambda e: e.transpose(PB1[0:64, 0:128], sigT_t[:, :], ident_t[:, :]),
       ["sigT", "ident", "PB1"], ["PB1"], "wtr")
    dve(lambda e: e.tensor_copy(wmtr_t[:, :], PB1[0:64, 0:128]),
        ["PB1"], ["wmtr"])
    dma("GP", wmrow_h.rearrange("a (c p) -> (a c) p", p=128), wmtr_t[:, :],
        ["wmtr"], ["wmrow"])

    pe(lambda e: e.matmul(PSTS[0:1, 0:64], onescol_t[:, :], geT_t[:, :],
                          start=True, stop=True),
       ["geT", "onescol", "PSTs"], ["PSTs"], "csum")
    dve(lambda e: e.tensor_copy(csB_t[:, :], PSTS[0:1, 0:64]), ["PSTs"], ["csB"])
    pe(lambda e: e.matmul(PB0[0:64, 0:128], csB_t[:, :], onesrow_t[:, :],
                          start=True, stop=True),
       ["csB", "onesrow", "PB0"], ["PB0"], "csbc")
    dve(lambda e: e.tensor_copy(csC_t[:, :], PB0[0:64, 0:128]), ["PB0"], ["csC"])
    pe(lambda e: (e.matmul(PB1[:, 0:64], triu_t[:, :], geT_t[:, :],
                           start=True, stop=False),
                  e.matmul(PB1[:, 0:64], csC_t[:, :], triu_t[0:64, 0:64],
                           start=False, stop=True))[-1],
       ["triu", "geT", "csC", "PB1"], ["PB1"], "posmm")
    dve(lambda e: e.tensor_scalar(sigT_t[:, :], PB1[:, 0:64], 1.0, None,
                                  OP.add), ["PB1"], ["sigT"])
    dve(lambda e: e.tensor_tensor(geT_t[:, :], geT_t[:, :], sigT_t[:, :],
                                  OP.mult), ["geT", "sigT"], ["geT"])
    dve(lambda e: e.tensor_scalar(sigT_t[:, :], geT_t[:, :], 0.0, None,
                                  OP.is_equal), ["geT"], ["sigT"])
    dve(lambda e: e.scalar_tensor_tensor(geT_t[:, :], sigT_t[:, :], 8192.0,
                                         geT_t[:, :], OP.mult, OP.add),
        ["geT", "sigT"], ["geT"])
    dve(lambda e: e.tensor_scalar(geT_t[:, :], geT_t[:, :], -1.0, None,
                                  OP.add), ["geT"], ["geT"])
    dve(lambda e: e.tensor_copy(posI_t[:, :], geT_t[:, :]), ["geT"], ["posI"])
    for c in range(64):
        S.dma_op("GP", lambda e, cc=c: e.indirect_dma_start(
            idxb_h[0:T, :],
            bass.IndirectOffsetOnAxis(posI_t[:, cc:cc + 1], 0),
            iota_t[:, cc:cc + 1], None),
            ["posI", "iota"], ["idxb"], f"isc{c}", drain=True)

    bis_end = S.snapshot()
    for e in ("DVE", "ACT", "PE", "SP"):
        S.fence(e, bis_end)

    NSG = KSEL // G
    fx1Tg = carve(AR2, 0, 2048, F32)
    x1g_p = [carve(AR2, 8192, 1024, BF16),
             carve(AR2, 10240, 1024, BF16),
             carve(AR2, 12288, 1024, BF16),
             carve(AR2, 14336, 1024, BF16)]
    hbf_t = carve(AR2, 16384, 2048, BF16)
    gar_t = carve(AR2, 20480, FC * G, BF16)
    hu_t = carve(AR2, 36864, 2048, F32)
    SQf_t = carve(AR2, 45056, 2048, BF16)
    x2T_t = carve(AR2, 49152, 2048, F32)
    ubf_t = carve(AR2, 61440, 2048, BF16)

    def w1mm(e, pr_, co0):
        last = None
        for half in range(2):
            for cci in range(CC):
                last = e.matmul(
                    pr_[:, half * G:(half + 1) * G],
                    w1W(cci)[:, (co0 + half) * 128:(co0 + half + 1) * 128],
                    hbf_t[:, cc_s(cci)],
                    start=(cci == 0), stop=(cci == CC - 1))
        return last

    def w2mm(e, pr_, cco_):
        last = None
        for fi in range(FC):
            last = e.matmul(pr_[:, 0:G],
                            w2W(fi)[:, cco_ * 128:(cco_ + 1) * 128],
                            gar_t[:, fi * G:(fi + 1) * G],
                            start=(fi == 0), stop=(fi == FC - 1))
        return last

    def stage_Cf(s):
        sp = s % 2
        for h in range(2):
            dma("SP", idxc_t[:, sp * 2 + h:sp * 2 + h + 1],
                idxb_h[s * G + h * 128:s * G + (h + 1) * 128, :],
                ["idxb", f"idxc{sp}"], [f"idxc{sp}"], f"lix{s}_{h}",
                drain=(h == 1))
        for h in range(2):
            xb = x1g_p[2 * sp + h]
            xbn = f"x1g{2 * sp + h}"
            S.dma_op("GP", lambda e, o=xb, c=sp * 2 + h: e.indirect_dma_start(
                o[:, :], None, x1rows_h[0:T, :],
                bass.IndirectOffsetOnAxis(idxc_t[:, c:c + 1], 0)),
                ["x1rows", f"idxc{sp}", xbn], [xbn], f"gx{s}_{h}")
            dve(lambda e, o=xb: e.reduce_sum(fcol_t[:, 0:1], o[:, :],
                                             axis=AX.X), [xbn], ["fc0"])
            act(lambda e, o=xb: e.activation(ubf_t[:, 0:1024], o[:, :],
                                             AF.Square), [xbn, "ubfR"],
                ["ubfR"])
            dve(lambda e: e.reduce_sum(fcol_t[:, 1:2], ubf_t[:, 0:1024],
                                       axis=AX.X), ["ubfR"], ["fc1"])
            dve(lambda e: e.tensor_scalar(fcol_t[:, 0:1], fcol_t[:, 0:1],
                                          -1.0 / C, None, OP.mult),
                ["fc0"], ["fc0"])
            dve(lambda e: e.tensor_tensor(fcol_t[:, 2:3], fcol_t[:, 0:1],
                                          fcol_t[:, 0:1], OP.mult),
                ["fc0"], ["fc2"])
            dve(lambda e: e.scalar_tensor_tensor(fcol_t[:, 1:2],
                                                 fcol_t[:, 1:2], 1.0 / C,
                                                 fcol_t[:, 2:3],
                                                 OP.mult, OP.subtract),
                ["fc1", "fc2"], ["fc1"])
            act(lambda e: e.activation(fcol_t[:, 1:2], fcol_t[:, 1:2], AF.Ln,
                                       bias=cst_t[:, 1:2]),
                ["fc1", "cst"], ["fc1"])
            act(lambda e: e.activation(fcol_t[:, 1:2], fcol_t[:, 1:2], AF.Exp,
                                       scale=-0.5), ["fc1"], ["fc1"])
            dve(lambda e, o=xb: e.tensor_scalar(o[:, :], o[:, :],
                                                fcol_t[:, 0:1],
                                                fcol_t[:, 1:2],
                                                OP.add, OP.mult),
                [xbn, "fc0", "fc1"], [xbn])
        for h in range(2):
            xb = x1g_p[2 * sp + h]
            xbn = f"x1g{2 * sp + h}"
            for cc in range(CC):
                dma("SP", hbf_t[:, cc * G + h * 128:cc * G + (h + 1) * 128],
                    xb[:, cc * 128:(cc + 1) * 128],
                    [xbn, "hbR"], ["hbR"], f"ht{s}_{h}_{cc}",
                    transpose=True, drain=(h == 1 and cc == CC - 1))

    def stage_Df(s):
        for pair in range(FC // 2):
            pA, pAn = next_p()
            pe(lambda e, p=pA, co0=2 * pair, f=w1mm: f(e, p, co0),
               ["hbR"] + [f"w1_{i}" for i in range(CC)],
               [pAn], f"w1m{s}_{pair}")
            act(lambda e, p=pA, pr_=pair: e.activation(
                gar_t[:, 2 * pr_ * G:(2 * pr_ + 1) * G], p[:, 0:G],
                AF.Gelu_apprx_tanh, bias=b1_t[:, 2 * pr_:2 * pr_ + 1]),
                [pAn, "b1", "garR"], ["garR"])
            act(lambda e, p=pA, pr_=pair: e.activation(
                gar_t[:, (2 * pr_ + 1) * G:(2 * pr_ + 2) * G], p[:, G:2 * G],
                AF.Gelu_apprx_tanh, bias=b1_t[:, 2 * pr_ + 1:2 * pr_ + 2]),
                [pAn, "b1", "garR"], ["garR"])

    def stage_Af(s):
        sp = s % 2
        for cco in range(CC):
            pu, pun = next_pw()
            pe(lambda e, p=pu, co=cco, f=w2mm: f(e, p, co),
               ["garR"] + [f"w2_{i}" for i in range(FC)], [pun],
               f"w2m{s}_{cco}")
            act(lambda e, p=pu, cc=cco: e.activation(
                ubf_t[:, cc_s(cc)], p[:, 0:G], AF.Identity,
                bias=b2_t[:, cc:cc + 1]), [pun, "b2", "ubfR"], ["ubfR"])
        for h in range(2):
            for cc in range(CC):
                dma("AQ", SQf_t[:, h * 1024 + cc * 128:h * 1024 + (cc + 1) * 128],
                    ubf_t[:, cc * G + h * 128:cc * G + (h + 1) * 128],
                    ["ubfR", "SQfR"], ["SQfR"], f"ut{s}_{h}_{cc}",
                    transpose=True, drain=(h == 1 and cc == CC - 1))
        for h in range(2):
            S.dma_op("GP", lambda e, hh=h, c=sp * 2 + h: e.indirect_dma_start(
                ufull_h[0:T, :],
                bass.IndirectOffsetOnAxis(idxc_t[:, c:c + 1], 0),
                SQf_t[:, hh * 1024:(hh + 1) * 1024], None),
                ["SQfR", f"idxc{sp}", "ufull"], ["ufull"], f"us{s}_{h}",
                drain=True)

    fx1b_t = carve(AR2, 8192, 2048, F32)
    stg_t = [SQf_t, hbf_t]
    stg_n = ["SQfR", "hbR"]

    def tail_load(g):
        p = g % 2
        fx1ap = fx1Tg if p == 0 else fx1b_t
        fxn = "fx1" if p == 0 else "fx1b"
        wl = [fxn]
        if p == 1 and g == NSG + 1:
            wl += [f"x1g{i}" for i in range(4)]
        dma("SP", cpart_sbuf(fx1ap[:, :]), cpart_dram(x1T_h, g),
            [f"x1st{g}", fxn], wl, f"lf{g}")
        dma("SP", stg_t[p][:, :].rearrange("p (h c) -> p h c", h=2),
            ufull_h[g * G:(g + 1) * G, :].rearrange("(h p) c -> p h c", p=128),
            ["ufull", stg_n[p]], [stg_n[p]], f"ur{g}")
        load_wm(g, p)
        load_lnc(g)

    def stage_Ft(g):
        p = g % 2
        fx1ap = fx1Tg if p == 0 else fx1b_t
        fxn = "fx1" if p == 0 else "fx1b"
        for h in range(2):
            for cc in range(CC):
                pt, ptn = next_p()
                pe(lambda e, q=pt, hh=h, c=cc, sg=stg_t[p]: e.transpose(
                    q[:, 0:64].bitcast(BF16),
                    sg[:, hh * 1024 + c * 128:hh * 1024 + (c + 1) * 128],
                    identb_t[:, :]),
                   [stg_n[p], "identb"], [ptn], f"uT{g}_{h}_{cc}")
                act(lambda e, q=pt, hh=h, c=cc, b=p: e.activation(
                    gar_t[:, b * 2048 + c * G + hh * 128:
                          b * 2048 + c * G + (hh + 1) * 128],
                    q[:, 0:64].bitcast(BF16), AF.Copy),
                    [ptn, "garR"], ["garR"], f"uE{g}_{h}_{cc}")
        pe(lambda e, b=p: e.matmul(PB1[:, 256:256 + G], onesrow_t[:, :],
                                   wmr_t[0:1, b * G:(b + 1) * G],
                                   start=True, stop=True),
           [f"wmr{p}", "onesrow", "PB1"], ["PB1"], "fwmb")
        dve(lambda e: e.tensor_copy(wmb2_t[:, :], PB1[:, 256:256 + G]),
            ["PB1"], ["wmb"])
        bcast_pair(extra_reads=("wmb",), tag="bc_f")
        for cci in range(CC):
            dve(lambda e, cc=cci, fx=fx1ap: e.tensor_tensor(
                hu_t[:, cc_s(cc)], fx[:, cc_s(cc)], PB0[:, 0:G],
                OP.mult), [fxn, "PB0"], ["huR"])
            dve(lambda e, cc=cci: e.tensor_tensor(
                hu_t[:, cc_s(cc)], hu_t[:, cc_s(cc)], PB1[:, 0:G],
                OP.subtract), ["huR", "PB1"], ["huR"])
            dve(lambda e, cc=cci, b=p: e.tensor_tensor(
                x2T_t[:, cc_s(cc)],
                gar_t[:, b * 2048 + cc * G:b * 2048 + (cc + 1) * G],
                wmb2_t[:, :], OP.mult), ["garR", "wmb"], ["x2R"])
            dve(lambda e, cc=cci: e.tensor_tensor(
                hu_t[:, cc_s(cc)], hu_t[:, cc_s(cc)], x2T_t[:, cc_s(cc)],
                OP.add), ["huR", "x2R"], ["huR"])
        act(lambda e: e.activation(ubf_t[:, :], hu_t[:, :], AF.Square),
            ["huR", "ubfR"], ["ubfR"], "sqd")
        dve(lambda e: e.tensor_copy(XBF2[:, :], hu_t[:, :]),
            ["huR", "XBF2"], ["XBF2"])
        pe(lambda e: stat_bf(e, XBF2, PSTs),
           ["XBF2", "onescolb", "PSTs"], ["PSTs"], "sum_d")
        pe(lambda e: stat_bf(e, ubf_t, PSTq),
           ["ubfR", "onescolb", "PSTq"], ["PSTq"], "sq_d")
        rowchain("ln_d")
        bcast_pair(tag="bc_d")
        for cci in range(CC):
            dve(lambda e, cc=cci: e.tensor_tensor(
                x2T_t[:, cc_s(cc)], hu_t[:, cc_s(cc)], PB0[:, 0:G], OP.mult),
                ["huR", "PB0"], ["x2R"])
            dve(lambda e, cc=cci: e.tensor_tensor(
                x2T_t[:, cc_s(cc)], x2T_t[:, cc_s(cc)], PB1[:, 0:G],
                OP.subtract), ["x2R", "PB1"], ["x2R"])
            dve(lambda e, cc=cci, fx=fx1ap: e.tensor_tensor(
                x2T_t[:, cc_s(cc)], x2T_t[:, cc_s(cc)], fx[:, cc_s(cc)],
                OP.add), ["x2R", fxn], ["x2R"])
        dma("SP", cpart_dram(xo_h, g), cpart_sbuf(x2T_t[:, :]),
            ["x2R"], [f"xost{g}"])

    def load_fx1(g):
        dma("SP", cpart_sbuf(fx1Tg[:, :]), cpart_dram(x1T_h, g),
            [f"x1st{g}", "fx1"], ["fx1"], f"lf{g}")

    def load_wm(g, b):
        dma("GP", wmr_t[0:1, b * G:(b + 1) * G],
            wmrow_h[0:1, g * G:(g + 1) * G], ["wmrow", f"wmr{b}"],
            [f"wmr{b}"], f"lwm{g}")

    def load_lnc(g):
        dma("GP", row(RW_RSTD), lncr_h[0:1, g * G:(g + 1) * G],
            [f"lncr{g}", "rrstd"], ["rrstd"], f"llr{g}")
        dma("GP", row(RW_MR), lncm_h[0:1, g * G:(g + 1) * G],
            [f"lncm{g}", "rmr"], ["rmr"], f"llm{g}")

    def stage_F(g):
        fb = g % 2
        for cc in range(CC):
            dma("SP", gar_t[:, fb * 2048 + cc * G:fb * 2048 + (cc + 1) * G],
                ufull_h[g * G:(g + 1) * G, cc * 128:(cc + 1) * 128],
                ["ufull", "garR"], ["garR"], f"uT{g}_{cc}",
                transpose=True, drain=(cc == CC - 1))
        load_lnc(g)
        pe(lambda e, b=fb: e.matmul(PB1[:, 256:256 + G], onesrow_t[:, :],
                                    wmr_t[0:1, b * G:(b + 1) * G],
                                    start=True, stop=True),
           [f"wmr{fb}", "onesrow", "PB1"], ["PB1"], "fwmb")
        dve(lambda e: e.tensor_copy(wmb2_t[:, :], PB1[:, 256:256 + G]),
            ["PB1"], ["wmb"])
        bcast_pair(extra_reads=("wmb",), tag="bc_f")
        for cci in range(CC):
            dve(lambda e, cc=cci: e.tensor_tensor(
                hu_t[:, cc_s(cc)], fx1Tg[:, cc_s(cc)], PB0[:, 0:G],
                OP.mult), ["fx1", "PB0"], ["huR"])
            dve(lambda e, cc=cci: e.tensor_tensor(
                hu_t[:, cc_s(cc)], hu_t[:, cc_s(cc)], PB1[:, 0:G],
                OP.subtract), ["huR", "PB1"], ["huR"])
            dve(lambda e, cc=cci, b=fb: e.tensor_tensor(
                x2T_t[:, cc_s(cc)],
                gar_t[:, b * 2048 + cc * G:b * 2048 + (cc + 1) * G],
                wmb2_t[:, :], OP.mult), ["garR", "wmb"], ["x2R"])
            dve(lambda e, cc=cci: e.tensor_tensor(
                hu_t[:, cc_s(cc)], hu_t[:, cc_s(cc)], x2T_t[:, cc_s(cc)],
                OP.add), ["huR", "x2R"], ["huR"])
        act(lambda e: e.activation(SQf_t[:, :], hu_t[:, :], AF.Square),
            ["huR", "SQfR"], ["SQfR"], "sqd")
        dve(lambda e: e.tensor_copy(XBF2[:, :], hu_t[:, :]),
            ["huR", "XBF2"], ["XBF2"])
        pe(lambda e: stat_bf(e, XBF2, PSTs),
           ["XBF2", "onescolb", "PSTs"], ["PSTs"], "sum_d")
        pe(lambda e: stat_bf(e, SQf_t, PSTq),
           ["SQfR", "onescolb", "PSTq"], ["PSTq"], "sq_d")
        rowchain("ln_d")
        bcast_pair(tag="bc_d")
        for cci in range(CC):
            dve(lambda e, cc=cci: e.tensor_tensor(
                x2T_t[:, cc_s(cc)], hu_t[:, cc_s(cc)], PB0[:, 0:G], OP.mult),
                ["huR", "PB0"], ["x2R"])
            dve(lambda e, cc=cci: e.tensor_tensor(
                x2T_t[:, cc_s(cc)], x2T_t[:, cc_s(cc)], PB1[:, 0:G],
                OP.subtract), ["x2R", "PB1"], ["x2R"])
            dve(lambda e, cc=cci: e.tensor_tensor(
                x2T_t[:, cc_s(cc)], x2T_t[:, cc_s(cc)], fx1Tg[:, cc_s(cc)],
                OP.add), ["x2R", "fx1"], ["x2R"])
        dma("SP", cpart_dram(xo_h, g), cpart_sbuf(x2T_t[:, :]),
            ["x2R"], [f"xost{g}"])

    load_wm(0, 0)
    stage_Cf(0)
    stage_Df(0)
    for s in range(NSG):
        load_fx1(s)
        stage_Af(s)
        if s + 1 < NSG:
            stage_Cf(s + 1)
        stage_F(s)
        if s + 1 < NSG:
            load_wm(s + 1, (s + 1) % 2)
            stage_Df(s + 1)
    tail_load(NSG)
    for g in range(NSG, NG):
        if g + 1 < NG:
            tail_load(g + 1)
        stage_Ft(g)

    final = S.snapshot()
    S.fence("SP", {s: final[s] for s in ("SP", "GP", "SPX", "GPX")})

    with nc.Block() as blk:
        def emitter(name):
            def run(eng):
                for waits, fn, inc, sem, tag in S.ops[name]:
                    for (de, dv) in waits:
                        eng.wait_ge(sems[de], dv)
                    if fn is not None:
                        inst = fn(eng)
                        if inc:
                            inst.then_inc(sems[sem], inc)
            return run

        blk.tensor(emitter("PE"))
        blk.vector(emitter("DVE"))
        blk.scalar(emitter("ACT"))
        blk.sync(emitter("SP"))
        blk.gpsimd(emitter("GP"))

    stack.close()
    return nc


_NC_CACHE = {}


def kernel(L0_attn_x, L0_attn_time_decay, L0_attn_time_first,
           L0_attn_time_mix_k, L0_attn_time_mix_v, L0_attn_time_mix_r,
           L0_attn_Wk, L0_attn_Wv, L0_attn_Wr, L0_attn_Wo,
           L0_ffn_w_router, L0_ffn_W1, L0_ffn_b1, L0_ffn_W2, L0_ffn_b2):
    import ml_dtypes
    x = np.asarray(L0_attn_x, np.float32)
    B = x.shape[0]
    assert B == 8 and x.shape[1] == T and x.shape[2] == C

    colize = lambda v: np.ascontiguousarray(
        np.asarray(v, np.float32).reshape(CC, 128).T)
    cst = np.zeros((128, 4), np.float32)
    cst[:, 0] = 1.0
    cst[:, 1] = LN_EPS
    cst[:, 2] = 1.0 / C

    common = dict(
        wkT=np.ascontiguousarray(np.asarray(L0_attn_Wk, np.float32).T),
        wvT=np.ascontiguousarray(np.asarray(L0_attn_Wv, np.float32).T),
        wrT=np.ascontiguousarray(np.asarray(L0_attn_Wr, np.float32).T),
        woT=np.ascontiguousarray(np.asarray(L0_attn_Wo, np.float32).T),
        w1=np.ascontiguousarray(np.asarray(L0_ffn_W1, np.float32)),
        w2=np.ascontiguousarray(np.asarray(L0_ffn_W2, np.float32)),
        tmk=colize(np.asarray(L0_attn_time_mix_k).reshape(-1)),
        tmv=colize(np.asarray(L0_attn_time_mix_v).reshape(-1)),
        tmr=colize(np.asarray(L0_attn_time_mix_r).reshape(-1)),
        td=colize(L0_attn_time_decay), tf=colize(L0_attn_time_first),
        wrt=colize(L0_ffn_w_router),
        b1=np.ascontiguousarray(np.asarray(L0_ffn_b1, np.float32).reshape(FC, 128).T),
        b2=colize(L0_ffn_b2),
        iota=np.ascontiguousarray(
            (np.arange(64)[None, :] * 128 +
             np.arange(128)[:, None]).astype(np.int32)),
        triu=np.triu(np.ones((128, 128), np.float32), 1),
        onescol=np.ones((128, 1), np.float32),
        onescolb=np.ones((128, 1), ml_dtypes.bfloat16),
        onesrow=np.ones((1, 128), np.float32),
        ident=np.eye(128, dtype=np.float32),
        cst=cst,
    )
    in_maps = []
    for b in range(B):
        m = dict(common)
        m["xT"] = np.ascontiguousarray(x[b].T)
        in_maps.append(m)

    if "v2" not in _NC_CACHE:
        _NC_CACHE["v2"] = build_program()
    nc = _NC_CACHE["v2"]
    res = run_bass_kernel_spmd(nc, in_maps, core_ids=list(range(B)))
    out = np.empty((B, T, C), np.float32)
    for b in range(B):
        out[b] = res.results[b]["xo"].T
    return out


# version 14
# speedup vs baseline: 1.2259x; 1.0601x over previous
import numpy as np
import concourse.bass as bass
import concourse.mybir as mybir
from concourse.bass_utils import run_bass_kernel_spmd
from concourse.alu_op_type import AluOpType as OP

F32 = mybir.dt.float32
BF16 = mybir.dt.bfloat16
AF = mybir.ActivationFunctionType
AX = mybir.AxisListType

C = 1024
T = 8192
CC = 8
G = 256
NG = T // G
F = 4096
FC = F // 128
KSEL = T // 2
N_BIS = 28
LN_EPS = 1e-5


class Sched:

    ENGINES = ("PE", "DVE", "ACT", "SP", "GP")
    SEMS = ("PE", "DVE", "ACT", "SP", "GP", "SPX", "GPX", "AQ", "AQX")
    QUEUES = {"SP": ("SP", "SP", "SPX"), "GP": ("GP", "GP", "GPX"),
              "AQ": ("ACT", "AQ", "AQX")}

    def __init__(self):
        self.ops = {e: [] for e in self.ENGINES}
        self.cnt = {s: 0 for s in self.SEMS}
        self.buf = {}
        self.mark = {e: {s: 0 for s in self.SEMS} for e in self.ENGINES}

    def _waits_for(self, eng, deps):
        by_e = {}
        for (de, dv) in deps:
            by_e[de] = max(by_e.get(de, 0), dv)
        waits = []
        for de in self.SEMS:
            dv = by_e.get(de, 0)
            if dv > self.mark[eng][de]:
                waits.append((de, dv))
                self.mark[eng][de] = dv
        return waits

    def _deps(self, reads, writes):
        deps = []
        for b in reads:
            st = self.buf.get(b)
            if st and st["w"] is not None:
                deps.append(st["w"])
        for b in writes:
            st = self.buf.get(b)
            if st:
                if st["w"] is not None:
                    deps.append(st["w"])
                deps.extend(st["r"])
        return deps

    def _register(self, reads, writes, sem, val):
        for b in reads:
            self.buf.setdefault(b, {"w": None, "r": []})["r"].append((sem, val))
        for b in writes:
            self.buf[b] = {"w": (sem, val), "r": []}

    def op(self, eng, fn, reads=(), writes=(), tag=""):
        if eng in self.QUEUES:
            return self.dma_op(eng, fn, reads, writes, tag)
        waits = self._waits_for(eng, self._deps(reads, writes))
        self.cnt[eng] += 1
        self._register(reads, writes, eng, self.cnt[eng])
        self.ops[eng].append((waits, fn, 1, eng, tag))

    def dma_op(self, q, fn, reads=(), writes=(), tag="", drain=True):
        stream, sem, semx = self.QUEUES[q]
        waits = self._waits_for(stream, self._deps(reads, writes))
        self.cnt[sem] += 16
        self.ops[stream].append((waits, fn, 16, sem, tag))
        if drain:
            dw = self._waits_for(stream, [(sem, self.cnt[sem])])
            self.cnt[semx] += 1
            self._register(reads, writes, semx, self.cnt[semx])
            self.ops[stream].append((dw, lambda e: e.nop(), 1, semx,
                                     tag + "_dr"))

    def fence(self, eng, counts):
        waits = self._waits_for(eng, [(s, counts.get(s, 0)) for s in self.SEMS])
        if waits:
            self.ops[eng].append((waits, None, 0, eng, "fence"))

    def snapshot(self):
        return dict(self.cnt)


def build_program():
    nc = bass.Bass("TRN2", target_bir_lowering=False)

    xT_h = nc.dram_tensor("xT", [C, T], F32, kind="ExternalInput").ap()
    wkT_h = nc.dram_tensor("wkT", [C, C], F32, kind="ExternalInput").ap()
    wvT_h = nc.dram_tensor("wvT", [C, C], F32, kind="ExternalInput").ap()
    wrT_h = nc.dram_tensor("wrT", [C, C], F32, kind="ExternalInput").ap()
    woT_h = nc.dram_tensor("woT", [C, C], F32, kind="ExternalInput").ap()
    w1_h = nc.dram_tensor("w1", [C, F], F32, kind="ExternalInput").ap()
    w2_h = nc.dram_tensor("w2", [F, C], F32, kind="ExternalInput").ap()
    tmk_h = nc.dram_tensor("tmk", [128, CC], F32, kind="ExternalInput").ap()
    tmv_h = nc.dram_tensor("tmv", [128, CC], F32, kind="ExternalInput").ap()
    tmr_h = nc.dram_tensor("tmr", [128, CC], F32, kind="ExternalInput").ap()
    td_h = nc.dram_tensor("td", [128, CC], F32, kind="ExternalInput").ap()
    tf_h = nc.dram_tensor("tf", [128, CC], F32, kind="ExternalInput").ap()
    wrt_h = nc.dram_tensor("wrt", [128, CC], F32, kind="ExternalInput").ap()
    b1_h = nc.dram_tensor("b1", [128, FC], F32, kind="ExternalInput").ap()
    b2_h = nc.dram_tensor("b2", [128, CC], F32, kind="ExternalInput").ap()
    onescol_h = nc.dram_tensor("onescol", [128, 1], F32, kind="ExternalInput").ap()
    onescolb_h = nc.dram_tensor("onescolb", [128, 1], BF16, kind="ExternalInput").ap()
    onesrow_h = nc.dram_tensor("onesrow", [1, 128], F32, kind="ExternalInput").ap()
    ident_h = nc.dram_tensor("ident", [128, 128], F32, kind="ExternalInput").ap()
    cst_h = nc.dram_tensor("cst", [128, 4], F32, kind="ExternalInput").ap()
    iota_h = nc.dram_tensor("iota", [128, 64], mybir.dt.int32,
                            kind="ExternalInput").ap()
    triu_h = nc.dram_tensor("triu", [128, 128], F32, kind="ExternalInput").ap()
    xo_h = nc.dram_tensor("xo", [C, T], F32, kind="ExternalOutput").ap()
    x1T_h = nc.dram_tensor("x1T", [C, T], F32, kind="Internal").ap()
    srow_h = nc.dram_tensor("srow", [1, T], F32, kind="Internal").ap()
    lncr_h = nc.dram_tensor("lncr", [1, T], F32, kind="Internal").ap()
    lncm_h = nc.dram_tensor("lncm", [1, T], F32, kind="Internal").ap()
    wmrow_h = nc.dram_tensor("wmrow", [1, T], F32, kind="Internal").ap()
    x1rows_h = nc.dram_tensor("x1rows", [T, C], BF16, kind="Internal").ap()
    ufull_h = nc.dram_tensor("ufull", [T, C], BF16, kind="Internal").ap()
    idxb_h = nc.dram_tensor("idxb", [T, 1], mybir.dt.int32, kind="Internal").ap()

    from contextlib import ExitStack
    stack = ExitStack()
    sb = lambda n, s, t: stack.enter_context(nc.sbuf_tensor(n, s, t))
    psm = lambda n, s: stack.enter_context(nc.psum_tensor(n, s, F32))

    AW = sb("AW", [128, 32768], BF16)
    AR1 = sb("AR1", [128, 32768], BF16)
    AR2 = sb("AR2", [128, 32768], BF16)

    ident_t = sb("identt", [128, 128], F32)
    identb_t = sb("identbt", [128, 128], BF16)
    onescol_t = sb("onescolt", [128, 1], F32)
    onescolb_t = sb("onescolbt", [128, 1], BF16)
    onesrow_t = sb("onesrowt", [1, 128], F32)
    cst_t = sb("cstt", [128, 4], F32)
    tmk_t = sb("tmkt", [128, CC], F32)
    tmv_t = sb("tmvt", [128, CC], F32)
    tmr_t = sb("tmrt", [128, CC], F32)
    td_t = sb("tdt", [128, CC], F32)
    tf_t = sb("tft", [128, CC], F32)
    wrt_t = sb("wrtt", [128, CC], F32)
    b1_t = sb("b1t", [128, FC], F32)
    b2_t = sb("b2t", [128, CC], F32)
    eu_t = sb("eut", [128, CC], F32)
    dd_t = sb("ddt", [128, CC], F32)
    rows_t = sb("rowst", [1, 1792], F32)
    scoreT_t = sb("scoreTt", [128, 64], F32)
    geT_t = sb("geTt", [128, 64], F32)
    sigT_t = sb("sigTt", [128, 64], F32)
    bis_t = sb("bist", [1, 16], F32)
    wmtr_t = sb("wmtrt", [64, 128], F32)
    wmr_t = sb("wmrt", [1, 2 * G], F32)
    iota_t = sb("iotat", [128, 64], mybir.dt.int32)
    triu_t = sb("triut", [128, 128], F32)
    posI_t = sb("posIt", [128, 64], mybir.dt.int32)
    idxc_t = sb("idxct", [128, 4], mybir.dt.int32)
    fcol_t = sb("fcolt", [128, 8], F32)
    csB_t = sb("csBt", [1, 64], F32)
    csC_t = sb("csCt", [64, 128], F32)
    wmb2_t = sb("wmb2t", [128, G], BF16)

    def carve(arena, off_bytes, cols, dt):
        el = off_bytes // 2
        w = cols * (2 if dt == F32 else 1)
        ap = arena[:, el:el + w]
        if dt == F32:
            ap = ap.bitcast(F32)
        return ap

    def attnW(m, cci):
        off = (m * CC + cci) * 1024
        return AW[:, off:off + 1024]

    def w2W(fi):
        return AW[:, fi * 1024:(fi + 1) * 1024]

    def w1W(cci):
        return AR1[:, cci * 4096:(cci + 1) * 4096]

    xTg = [carve(AR1, 0, 2048, F32), carve(AR1, 8192, 2048, F32),
           carve(AR1, 16384, 2048, F32)]
    EK_p = [carve(AR1, 24576, 2048, F32), carve(AR1, 32768, 2048, F32)]
    EKV_p = [carve(AR1, 40960, 2048, F32), carve(AR1, 49152, 2048, F32)]
    SCR = carve(AR1, 57344, 2048, BF16)
    XBF = carve(AR1, 61440, 2048, BF16)

    xk_p = [carve(AR2, 0, 2048, BF16), carve(AR2, 12288, 2048, BF16)]
    xv_p = [carve(AR2, 4096, 2048, BF16), carve(AR2, 16384, 2048, BF16)]
    xr_p = [carve(AR2, 8192, 2048, BF16), carve(AR2, 20480, 2048, BF16)]
    sig_p = [carve(AR2, 24576, 2048, BF16), carve(AR2, 28672, 2048, BF16)]
    Asc_t = carve(AR2, 32768, CC * 257, F32)
    Bsc_t = carve(AR2, 40992, CC * 257, F32)
    xsh_t = carve(AR2, 49216, CC * 257, BF16)
    ATX = carve(AR2, 53328, 2048, F32)
    XBF2 = carve(AR2, 57344, 2048, BF16)

    P = [psm(f"PS{i}", [128, 512]) for i in range(2)]
    PN = [f"PS{i}" for i in range(2)]
    PW = [psm(f"PW{i}", [128, 512]) for i in range(2)]
    PWN = [f"PW{i}" for i in range(2)]
    PSTS = psm("PSTS", [1, 256])
    PSTQ = psm("PSTQ", [1, 256])
    PB0 = psm("PB0", [128, 512])
    PB1 = psm("PB1", [128, 512])

    sems = {}
    for e in Sched.SEMS:
        sems[e] = stack.enter_context(nc.semaphore("sem" + e))
    S = Sched()

    def dma(q, out, in_, reads, writes, tag="", drain=True, transpose=False):
        S.dma_op(q, lambda e, o=out, i=in_, tr=transpose: e.dma_start(
            out=o, in_=i, transpose=tr), reads, writes, tag, drain=drain)

    def pe(fn, reads, writes, tag=""):
        S.op("PE", fn, reads, writes, tag)

    def dve(fn, reads, writes, tag=""):
        S.op("DVE", fn, reads, writes, tag)

    def act(fn, reads, writes, tag=""):
        S.op("ACT", fn, reads, writes, tag)

    cc_s = lambda cc: slice(cc * G, (cc + 1) * G)
    cb_s = lambda cc: slice(cc * 257 + 1, cc * 257 + 257)
    cs_s = lambda cc: slice(cc * 257, cc * 257 + 256)
    bnd_s = slice(0, CC * 257, 257)
    lst_s = slice(256, CC * 257, 257)

    RW_M, RW_MSQ, RW_VAR, RW_RSTD, RW_MR, RW_S, RW_T1 = (
        0, 256, 512, 768, 1024, 1280, 1536)
    row = lambda off: rows_t[0:1, off:off + G]
    PSTs = PSTS[0:1, 0:256]
    PSTq = PSTQ[0:1, 0:256]

    def mm_contract(e, lhsT_fn, rhs_t, out_ap, n, rhs_sl_fn):
        last = None
        for i in range(n):
            last = e.matmul(out_ap, lhsT_fn(i), rhs_t[:, rhs_sl_fn(i)],
                            start=(i == 0), stop=(i == n - 1))
        return last

    def stat_bf(e, rhs_t, dst):
        return mm_contract(e, lambda i: onescolb_t[:, :], rhs_t,
                           dst, CC, cc_s)

    def stat_f32(e, rhs_t, dst):
        return mm_contract(e, lambda i: onescol_t[:, :], rhs_t,
                           dst, CC, cc_s)

    def rowchain(tag=""):
        dve(lambda e: e.tensor_scalar(row(RW_M), PSTs, 1.0 / C,
                                      None, OP.mult), ["PSTs"], ["rm"], tag)
        dve(lambda e: e.tensor_tensor(row(RW_MSQ), row(RW_M), row(RW_M),
                                      OP.mult), ["rm"], ["rmsq"], tag)
        dve(lambda e: e.scalar_tensor_tensor(row(RW_VAR), PSTq,
                                             1.0 / C, row(RW_MSQ),
                                             OP.mult, OP.subtract),
            ["PSTq", "rmsq"], ["rvar"], tag)
        act(lambda e: e.activation(row(RW_RSTD), row(RW_VAR), AF.Ln,
                                   bias=cst_t[0:1, 1:2]),
            ["rvar", "cst"], ["rrstd"], tag)
        act(lambda e: e.activation(row(RW_RSTD), row(RW_RSTD), AF.Exp,
                                   scale=-0.5), ["rrstd"], ["rrstd"], tag)
        dve(lambda e: e.tensor_tensor(row(RW_MR), row(RW_M), row(RW_RSTD),
                                      OP.mult), ["rm", "rrstd"], ["rmr"], tag)

    def bcast_pair(extra_reads=(), tag=""):
        pe(lambda e: e.matmul(PB0[:, 0:G], onesrow_t[:, :], row(RW_RSTD),
                              start=True, stop=True),
           ["onesrow", "rrstd"] + list(extra_reads), ["PB0"], tag)
        pe(lambda e: e.matmul(PB1[:, 0:G], onesrow_t[:, :], row(RW_MR),
                              start=True, stop=True),
           ["onesrow", "rmr"] + list(extra_reads), ["PB1"], tag)

    mmro = [0]

    def next_p():
        i = mmro[0] % 2
        mmro[0] += 1
        return P[i], PN[i]

    mmw = [0]

    def next_pw():
        i = mmw[0] % 2
        mmw[0] += 1
        return PW[i], PWN[i]

    dma("SP", ident_t[:, :], ident_h, [], ["ident"])
    dma("SP", onescol_t[:, :], onescol_h, [], ["onescol"])
    dma("SP", onescolb_t[:, :], onescolb_h, [], ["onescolb"])
    dma("SP", onesrow_t[:, :], onesrow_h, [], ["onesrow"])
    dma("SP", cst_t[:, :], cst_h, [], ["cst"])
    for nm, t, h in [("tmk", tmk_t, tmk_h), ("tmv", tmv_t, tmv_h),
                     ("tmr", tmr_t, tmr_h), ("td", td_t, td_h),
                     ("tf", tf_t, tf_h), ("wrt", wrt_t, wrt_h),
                     ("b1", b1_t, b1_h), ("b2", b2_t, b2_h)]:
        dma("SP", t[:, :], h, [], [nm])

    for m, h in enumerate([wkT_h, wvT_h, wrT_h, woT_h]):
        for cci in range(CC):
            dma("GP", attnW(m, cci), h[cci * 128:(cci + 1) * 128, :],
                [], [f"aw{m}_{cci}"])

    dma("SP", iota_t[:, :], iota_h, [], ["iota"])
    dma("SP", triu_t[:, :], triu_h, [], ["triu"])
    dve(lambda e: e.memset(SCR[:, 0:1024], 0.0), [], ["SCR"])
    for k in range(T // 128):
        dma("GP", ufull_h[k * 128:(k + 1) * 128, :],
            SCR[:, 0:1024], ["SCR"], ["ufull"], f"uz{k}",
            drain=(k == T // 128 - 1))
    dve(lambda e: e.tensor_copy(identb_t[:, :], ident_t[:, :]),
        ["ident"], ["identb"])
    act(lambda e: e.activation(eu_t[:, :], tf_t[:, :], AF.Exp), ["tf"], ["eu"])
    act(lambda e: e.activation(dd_t[:, :], td_t[:, :], AF.Exp), ["td"], ["dd"])
    dve(lambda e: e.tensor_scalar(dd_t[:, :], dd_t[:, :], -1.0, None, OP.mult),
        ["dd"], ["dd"])
    act(lambda e: e.activation(dd_t[:, :], dd_t[:, :], AF.Exp), ["dd"], ["dd"])

    dve(lambda e: e.memset(xsh_t[:, bnd_s], 0.0), [], ["xsh_b"])
    dve(lambda e: e.memset(Asc_t[:, bnd_s], 0.0), [],
        [f"A{i}b" for i in range(CC)])
    dve(lambda e: e.memset(Bsc_t[:, bnd_s], 0.0), [],
        [f"B{i}b" for i in range(CC)])

    pe(lambda e: mm_contract(e, lambda i: wrt_t[:, i:i + 1], onescol_t,
                             PSTS[0:1, 0:1], CC, lambda i: slice(0, 1)),
       ["wrt", "onescol"], ["PSTs"], "swr")
    dve(lambda e: e.tensor_copy(bis_t[0:1, 8:9], PSTS[0:1, 0:1]),
        ["PSTs"], ["swr"])

    def cpart_dram(h, g):
        return h[:, g * G:(g + 1) * G].rearrange("(cc p) j -> p cc j", p=128)

    def cpart_sbuf(ap):
        return ap.rearrange("p (cc j) -> p cc j", cc=CC)

    def load_xT(g):
        b = g % 3
        dma("SP", cpart_sbuf(xTg[b][:, :]), cpart_dram(xT_h, g),
            [], [f"xTg{b}"], f"ldx{g}")

    def stage_S1(g):
        buf = g % 3
        par = g % 2
        xg = f"xTg{buf}"
        xk_t, xv_t, xr_t = xk_p[par], xv_p[par], xr_p[par]
        if g > 0:
            dve(lambda e: e.tensor_copy(xsh_t[:, bnd_s], xsh_t[:, lst_s]),
                [f"xsh{i}" for i in range(CC)], ["xsh_b"])
        act(lambda e, b=buf: e.activation(SCR[:, :], xTg[b][:, :], AF.Square),
            [xg, "SCR"], ["SCR"], "sqa")
        dve(lambda e, b=buf: e.tensor_copy(XBF[:, :], xTg[b][:, :]),
            [xg, "XBF"], ["XBF"])
        pe(lambda e: stat_bf(e, XBF, PSTs),
           ["XBF", "onescolb", "PSTs"], ["PSTs"], "sum_a")
        pe(lambda e: stat_bf(e, SCR, PSTq),
           ["SCR", "onescolb", "PSTq"], ["PSTq"], "sq_a")
        rowchain("ln_a")
        bcast_pair(tag="bc_a")

        for cci in range(CC):
            dve(lambda e, cc=cci, b=buf: e.tensor_tensor(
                xsh_t[:, cb_s(cc)], xTg[b][:, cc_s(cc)], PB0[:, 0:G], OP.mult),
                [xg, "PB0"], [f"xsh{cci}"])
        for cci in range(CC):
            dve(lambda e, cc=cci: e.tensor_tensor(
                xsh_t[:, cb_s(cc)], xsh_t[:, cb_s(cc)], PB1[:, 0:G],
                OP.subtract), [f"xsh{cci}", "PB1"], [f"xsh{cci}"])

        for cci in range(CC):
            dve(lambda e, cc=cci, t=xk_t: e.tensor_tensor(
                t[:, cc_s(cc)], xsh_t[:, cb_s(cc)], xsh_t[:, cs_s(cc)],
                OP.subtract), [f"xsh{cci}", "xsh_b"], [f"xk{par}_{cci}"])
        for tname, tile, tmname, tmcol in [("xr", xr_t, "tmr", tmr_t),
                                           ("xv", xv_t, "tmv", tmv_t)]:
            for cci in range(CC):
                dve(lambda e, cc=cci, tl=tile, tm=tmcol, kk=xk_t:
                    e.scalar_tensor_tensor(
                        tl[:, cc_s(cc)], kk[:, cc_s(cc)], tm[:, cc:cc + 1],
                        xsh_t[:, cs_s(cc)], OP.mult, OP.add),
                    [f"xk{par}_{cci}", tmname, f"xsh{cci}", "xsh_b"],
                    [f"{tname}{par}_{cci}"])
        for cci in range(CC):
            dve(lambda e, cc=cci, kk=xk_t: e.scalar_tensor_tensor(
                kk[:, cc_s(cc)], kk[:, cc_s(cc)], tmk_t[:, cc:cc + 1],
                xsh_t[:, cs_s(cc)], OP.mult, OP.add),
                [f"xk{par}_{cci}", "tmk", f"xsh{cci}", "xsh_b"],
                [f"xk{par}_{cci}"])

    def stage_S2(g):
        par = g % 2
        EK_t, EKV_t, sig_t = EK_p[par], EKV_p[par], sig_p[par]
        xk_t, xv_t, xr_t = xk_p[par], xv_p[par], xr_p[par]
        XK = [f"xk{par}_{i}" for i in range(CC)]
        XV = [f"xv{par}_{i}" for i in range(CC)]
        XR = [f"xr{par}_{i}" for i in range(CC)]
        for cco in range(CC):
            pk, pkn = next_p()
            pv, pvn = next_p()
            pe(lambda e, p=pk, co=cco, t=xk_t: mm_contract(
                e, lambda i, co_=co: attnW(0, i)[:, co_ * 128:(co_ + 1) * 128],
                t, p[:, 0:G], CC, cc_s),
               XK + [f"aw0_{i}" for i in range(CC)], [pkn], f"km{g}_{cco}")
            act(lambda e, p=pk, cc=cco, t=EK_t: e.activation(
                t[:, cc_s(cc)], p[:, 0:G], AF.Exp), [pkn], [f"EK{par}_{cco}"])
            pe(lambda e, p=pv, co=cco, t=xv_t: mm_contract(
                e, lambda i, co_=co: attnW(1, i)[:, co_ * 128:(co_ + 1) * 128],
                t, p[:, 0:G], CC, cc_s),
               XV + [f"aw1_{i}" for i in range(CC)], [pvn], f"vm{g}_{cco}")
            dve(lambda e, p=pv, cc=cco, ek=EK_t, ev=EKV_t: e.tensor_tensor(
                ev[:, cc_s(cc)], ek[:, cc_s(cc)], p[:, 0:G], OP.mult),
                [f"EK{par}_{cco}", pvn], [f"EKV{par}_{cco}"])
        for cco in range(CC):
            pr, prn = next_p()
            pe(lambda e, p=pr, co=cco, t=xr_t: mm_contract(
                e, lambda i, co_=co: attnW(2, i)[:, co_ * 128:(co_ + 1) * 128],
                t, p[:, 0:G], CC, cc_s),
               XR + [f"aw2_{i}" for i in range(CC)], [prn], f"rm{g}_{cco}")
            act(lambda e, p=pr, cc=cco, t=sig_t: e.activation(
                t[:, cc_s(cc)], p[:, 0:G], AF.Sigmoid),
                [prn], [f"sig{par}_{cco}"])

    def stage_S3(g):
        par = g % 2
        EK_t, EKV_t, sig_t = EK_p[par], EKV_p[par], sig_p[par]
        ekn = lambda cc: f"EK{par}_{cc}"
        evn = lambda cc: f"EKV{par}_{cc}"
        for cci in range(CC):
            dve(lambda e, cc=cci, ev=EKV_t: e.tensor_tensor_scan(
                Asc_t[:, cb_s(cc)], dd_t[:, cc:cc + 1].broadcast_to([128, G]),
                ev[:, cc_s(cc)], Asc_t[:, cc * 257:cc * 257 + 1],
                OP.mult, OP.add),
                [evn(cci), "dd", f"A{cci}b"], [f"A{cci}"])
            dve(lambda e, cc=cci, ek=EK_t: e.tensor_tensor_scan(
                Bsc_t[:, cb_s(cc)], dd_t[:, cc:cc + 1].broadcast_to([128, G]),
                ek[:, cc_s(cc)], Bsc_t[:, cc * 257:cc * 257 + 1],
                OP.mult, OP.add),
                [ekn(cci), "dd", f"B{cci}b"], [f"B{cci}"])

        for cci in range(CC):
            dve(lambda e, cc=cci, ev=EKV_t: e.scalar_tensor_tensor(
                ev[:, cc_s(cc)], Asc_t[:, cs_s(cc)], eu_t[:, cc:cc + 1],
                ev[:, cc_s(cc)], OP.mult, OP.add),
                [f"A{cci}", f"A{cci}b", "eu", evn(cci)], [evn(cci)])
            dve(lambda e, cc=cci, ek=EK_t: e.scalar_tensor_tensor(
                ek[:, cc_s(cc)], Bsc_t[:, cs_s(cc)], eu_t[:, cc:cc + 1],
                ek[:, cc_s(cc)], OP.mult, OP.add),
                [f"B{cci}", f"B{cci}b", "eu", ekn(cci)], [ekn(cci)])
        dve(lambda e: e.tensor_copy(Asc_t[:, bnd_s], Asc_t[:, lst_s]),
            [f"A{i}" for i in range(CC)], [f"A{i}b" for i in range(CC)])
        dve(lambda e: e.tensor_copy(Bsc_t[:, bnd_s], Bsc_t[:, lst_s]),
            [f"B{i}" for i in range(CC)], [f"B{i}b" for i in range(CC)])
        EKall = [ekn(i) for i in range(CC)]
        EVall = [evn(i) for i in range(CC)]
        act(lambda e, ek=EK_t: e.activation(ek[:, :], ek[:, :], AF.Ln),
            EKall, EKall, "invD")
        act(lambda e, ek=EK_t: e.activation(ek[:, :], ek[:, :], AF.Exp,
                                            scale=-1.0), EKall, EKall, "invD")
        dve(lambda e, ek=EK_t, ev=EKV_t: e.tensor_tensor(
            ev[:, :], ev[:, :], ek[:, :], OP.mult),
            EKall + EVall, EVall, "ymul")
        dve(lambda e, ev=EKV_t, sg=sig_t: e.tensor_tensor(
            sg[:, :], ev[:, :], sg[:, :], OP.mult),
            EVall + [f"sig{par}_{i}" for i in range(CC)],
            [f"ry{par}"], "rymul")

    def stage_S4(g):
        buf = g % 3
        par = g % 2
        xg = f"xTg{buf}"
        sig_t = sig_p[par]
        for cco in range(CC):
            po, pon = next_pw()
            pe(lambda e, p=po, co=cco, t=sig_t: mm_contract(
                e, lambda i, co_=co: attnW(3, i)[:, co_ * 128:(co_ + 1) * 128],
                t, p[:, 0:G], CC, cc_s),
               [f"ry{par}"] + [f"sig{par}_{i}" for i in range(CC)] +
               [f"aw3_{i}" for i in range(CC)],
               [pon], f"om{g}_{cco}")
            act(lambda e, p=po, cc=cco: e.activation(
                ATX[:, cc_s(cc)], p[:, 0:G], AF.Copy), [pon],
                [f"at{cco}", f"x1{cco}"])

        ATall = [f"at{i}" for i in range(CC)]
        act(lambda e: e.activation(SCR[:, :], ATX[:, :], AF.Square),
            ATall + ["SCR"], ["SCR"], "sqb")
        dve(lambda e: e.tensor_copy(XBF[:, :], ATX[:, :]),
            ATall + ["XBF"], ["XBF"])
        pe(lambda e: stat_bf(e, XBF, PSTs),
           ["XBF", "onescolb", "PSTs"], ["PSTs"], "sum_b")
        pe(lambda e: stat_bf(e, SCR, PSTq),
           ["SCR", "onescolb", "PSTq"], ["PSTq"], "sq_b")
        rowchain("ln_b")
        bcast_pair(tag="bc_b")
        for cci in range(CC):
            dve(lambda e, cc=cci: e.tensor_tensor(
                ATX[:, cc_s(cc)], ATX[:, cc_s(cc)], PB0[:, 0:G],
                OP.mult), [f"at{cci}", "PB0"], [f"x1{cci}", f"at{cci}"])
        for cci in range(CC):
            dve(lambda e, cc=cci: e.tensor_tensor(
                ATX[:, cc_s(cc)], ATX[:, cc_s(cc)], PB1[:, 0:G],
                OP.subtract), [f"x1{cci}", "PB1"], [f"x1{cci}"])
        for cci in range(CC):
            dve(lambda e, cc=cci, b=buf: e.tensor_tensor(
                ATX[:, cc_s(cc)], ATX[:, cc_s(cc)], xTg[b][:, cc_s(cc)],
                OP.add), [f"x1{cci}", xg], [f"x1{cci}"])

        X1ALL = [f"x1{i}" for i in range(CC)]
        act(lambda e: e.activation(SCR[:, :], ATX[:, :], AF.Square),
            X1ALL + ["SCR"], ["SCR"], "sqc")
        dve(lambda e: e.tensor_copy(XBF[:, :], ATX[:, :]),
            X1ALL + ["XBF"], ["XBF"])
        pe(lambda e: stat_bf(e, XBF, PSTs),
           ["XBF", "onescolb", "PSTs"], ["PSTs"], "sum_c")
        pe(lambda e: stat_bf(e, SCR, PSTq),
           ["SCR", "onescolb", "PSTq"], ["PSTq"], "sq_c")
        rowchain("ln_c")
        pe(lambda e: mm_contract(e, lambda i: wrt_t[:, i:i + 1], ATX,
                                 PSTs, CC, cc_s),
           X1ALL + ["wrt"], ["PSTs"], "smm")
        dve(lambda e: e.tensor_scalar(row(RW_T1), row(RW_M), bis_t[0:1, 8:9],
                                      None, OP.mult), ["rm", "swr"], ["rt1"])
        dve(lambda e: e.tensor_tensor(row(RW_T1), PSTs,
                                      row(RW_T1), OP.subtract),
            ["PSTs", "rt1"], ["rt1"])
        dve(lambda e: e.tensor_tensor(row(RW_S), row(RW_T1), row(RW_RSTD),
                                      OP.mult), ["rt1", "rrstd"], ["rs"])
        dma("GP", srow_h[0:1, g * G:(g + 1) * G], row(RW_S), ["rs"],
            [f"srow{g}"])
        dma("GP", lncr_h[0:1, g * G:(g + 1) * G], row(RW_RSTD), ["rrstd"],
            [f"lncr{g}"])
        dma("GP", lncm_h[0:1, g * G:(g + 1) * G], row(RW_MR), ["rmr"],
            [f"lncm{g}"])

        dma("SP", cpart_dram(x1T_h, g), cpart_sbuf(ATX[:, :]),
            X1ALL, [f"x1st{g}"])
        for h in range(2):
            for cc in range(CC):
                pt, ptn = next_pw()
                pe(lambda e, p=pt, hh=h, c=cc: e.transpose(
                    p[:, 0:64].bitcast(BF16),
                    XBF[:, c * G + hh * 128:c * G + (hh + 1) * 128],
                    identb_t[:, :]),
                   ["XBF", "identb"], [ptn], f"x1t{g}_{h}_{cc}")
                act(lambda e, p=pt, hh=h, c=cc: e.activation(
                    SCR[:, hh * 1024 + c * 128:hh * 1024 + (c + 1) * 128],
                    p[:, 0:64].bitcast(BF16), AF.Copy),
                    [ptn, "SCR"], ["SCR"], f"x1e{g}_{h}_{cc}")
        dma("SP", x1rows_h[g * G:(g + 1) * G, :].rearrange(
            "(h p) c -> p h c", p=128),
            SCR[:, :].rearrange("p (h c) -> p h c", h=2),
            ["SCR"], ["x1rows"], f"x1r{g}")

    for i in range(-2, NG):
        if i + 2 < NG:
            load_xT(i + 2)
        if 0 <= i + 1 < NG:
            stage_S2(i + 1)
        if i >= 0:
            stage_S3(i)
        if i + 2 < NG:
            stage_S1(i + 2)
        if i >= 0:
            stage_S4(i)

    attn_end = S.snapshot()
    S.fence("GP", attn_end)
    for cci in range(CC):
        dma("GP", w1W(cci), w1_h[cci * 128:(cci + 1) * 128, :], [], [f"w1_{cci}"])
    for fi in range(FC):
        dma("GP", w2W(fi), w2_h[fi * 128:(fi + 1) * 128, :], [], [f"w2_{fi}"])

    dma("SP", wmtr_t[:, :], srow_h.rearrange("a (c p) -> (a c) p", p=128),
        [f"srow{gg}" for gg in range(NG)], ["wmtr"])
    pe(lambda e: e.transpose(PB0[:, 0:64], wmtr_t[:, :], ident_t[0:64, 0:64]),
       ["wmtr", "ident"], ["PB0"], "str")
    dve(lambda e: e.tensor_copy(scoreT_t[:, :], PB0[:, 0:64]),
        ["PB0"], ["scoreT"])

    dve(lambda e: e.reduce_max(geT_t[:, 0:1], scoreT_t[:, :], axis=AX.X,
                               apply_absolute_value=True),
        ["scoreT"], ["geT"])
    pe(lambda e: e.matmul(PSTS[0:1, 0:1], geT_t[:, 0:1], onescol_t[:, 0:1],
                          start=True, stop=True),
       ["geT", "onescol"], ["PSTs"], "bsum")
    dve(lambda e: e.tensor_scalar(bis_t[0:1, 1:2], PSTS[0:1, 0:1], 1.0, 1.0,
                                  OP.mult, OP.add), ["PSTs"], ["hi"])
    dve(lambda e: e.tensor_scalar(bis_t[0:1, 0:1], bis_t[0:1, 1:2], -1.0,
                                  None, OP.mult), ["hi"], ["lo"])
    dve(lambda e: e.tensor_tensor(bis_t[0:1, 2:3], bis_t[0:1, 0:1],
                                  bis_t[0:1, 1:2], OP.add), ["lo", "hi"], ["th"])
    dve(lambda e: e.tensor_scalar(bis_t[0:1, 2:3], bis_t[0:1, 2:3], 0.5, None,
                                  OP.mult), ["th"], ["th"])

    for it in range(N_BIS):
        pe(lambda e: e.matmul(PB0[:, 0:1], onesrow_t[:, :], bis_t[0:1, 2:3],
                              start=True, stop=True),
           ["th", "onesrow", "PB0"], ["PB0"], f"bth{it}")
        dve(lambda e: e.tensor_scalar(geT_t[:, :], scoreT_t[:, :],
                                      PB0[:, 0:1], None, OP.is_ge),
            ["scoreT", "PB0"], ["geT"])
        dve(lambda e: e.reduce_sum(sigT_t[:, 0:1], geT_t[:, :], axis=AX.X),
            ["geT"], ["sigT"])
        pe(lambda e: e.matmul(PSTS[0:1, 0:1], sigT_t[:, 0:1],
                              onescol_t[:, 0:1], start=True, stop=True),
           ["sigT", "onescol", "PSTs"], ["PSTs"], f"bcnt{it}")
        dve(lambda e: e.tensor_scalar(bis_t[0:1, 3:4], PSTS[0:1, 0:1],
                                      float(KSEL), None, OP.is_ge),
            ["PSTs"], ["flag"])
        dve(lambda e: e.tensor_tensor(bis_t[0:1, 4:5], bis_t[0:1, 2:3],
                                      bis_t[0:1, 0:1], OP.subtract),
            ["th", "lo"], ["t4"])
        dve(lambda e: e.tensor_tensor(bis_t[0:1, 4:5], bis_t[0:1, 4:5],
                                      bis_t[0:1, 3:4], OP.mult),
            ["t4", "flag"], ["t4"])
        dve(lambda e: e.tensor_tensor(bis_t[0:1, 0:1], bis_t[0:1, 0:1],
                                      bis_t[0:1, 4:5], OP.add),
            ["lo", "t4"], ["lo"])
        dve(lambda e: e.tensor_tensor(bis_t[0:1, 5:6], bis_t[0:1, 1:2],
                                      bis_t[0:1, 2:3], OP.subtract),
            ["hi", "th"], ["t5"])
        dve(lambda e: e.tensor_tensor(bis_t[0:1, 5:6], bis_t[0:1, 5:6],
                                      bis_t[0:1, 3:4], OP.mult),
            ["t5", "flag"], ["t5"])
        dve(lambda e: e.tensor_tensor(bis_t[0:1, 1:2], bis_t[0:1, 2:3],
                                      bis_t[0:1, 5:6], OP.add),
            ["th", "t5"], ["hi"])
        dve(lambda e: e.tensor_tensor(bis_t[0:1, 2:3], bis_t[0:1, 0:1],
                                      bis_t[0:1, 1:2], OP.add),
            ["lo", "hi"], ["th"])
        dve(lambda e: e.tensor_scalar(bis_t[0:1, 2:3], bis_t[0:1, 2:3], 0.5,
                                      None, OP.mult), ["th"], ["th"])

    pe(lambda e: e.matmul(PB0[:, 0:1], onesrow_t[:, :], bis_t[0:1, 0:1],
                          start=True, stop=True),
       ["lo", "onesrow", "PB0"], ["PB0"], "bfin")
    dve(lambda e: e.tensor_scalar(geT_t[:, :], scoreT_t[:, :], PB0[:, 0:1],
                                  None, OP.is_ge), ["scoreT", "PB0"], ["geT"])
    act(lambda e: e.activation(sigT_t[:, :], scoreT_t[:, :], AF.Sigmoid),
        ["scoreT", "sigT"], ["sigT"])
    dve(lambda e: e.tensor_tensor(sigT_t[:, :], geT_t[:, :], sigT_t[:, :],
                                  OP.mult), ["geT", "sigT"], ["sigT"])
    pe(lambda e: e.transpose(PB1[0:64, 0:128], sigT_t[:, :], ident_t[:, :]),
       ["sigT", "ident", "PB1"], ["PB1"], "wtr")
    dve(lambda e: e.tensor_copy(wmtr_t[:, :], PB1[0:64, 0:128]),
        ["PB1"], ["wmtr"])
    dma("GP", wmrow_h.rearrange("a (c p) -> (a c) p", p=128), wmtr_t[:, :],
        ["wmtr"], ["wmrow"])

    pe(lambda e: e.matmul(PSTS[0:1, 0:64], onescol_t[:, :], geT_t[:, :],
                          start=True, stop=True),
       ["geT", "onescol", "PSTs"], ["PSTs"], "csum")
    dve(lambda e: e.tensor_copy(csB_t[:, :], PSTS[0:1, 0:64]), ["PSTs"], ["csB"])
    pe(lambda e: e.matmul(PB0[0:64, 0:128], csB_t[:, :], onesrow_t[:, :],
                          start=True, stop=True),
       ["csB", "onesrow", "PB0"], ["PB0"], "csbc")
    dve(lambda e: e.tensor_copy(csC_t[:, :], PB0[0:64, 0:128]), ["PB0"], ["csC"])
    pe(lambda e: (e.matmul(PB1[:, 0:64], triu_t[:, :], geT_t[:, :],
                           start=True, stop=False),
                  e.matmul(PB1[:, 0:64], csC_t[:, :], triu_t[0:64, 0:64],
                           start=False, stop=True))[-1],
       ["triu", "geT", "csC", "PB1"], ["PB1"], "posmm")
    dve(lambda e: e.tensor_scalar(sigT_t[:, :], PB1[:, 0:64], 1.0, None,
                                  OP.add), ["PB1"], ["sigT"])
    dve(lambda e: e.tensor_tensor(geT_t[:, :], geT_t[:, :], sigT_t[:, :],
                                  OP.mult), ["geT", "sigT"], ["geT"])
    dve(lambda e: e.tensor_scalar(sigT_t[:, :], geT_t[:, :], 0.0, None,
                                  OP.is_equal), ["geT"], ["sigT"])
    dve(lambda e: e.scalar_tensor_tensor(geT_t[:, :], sigT_t[:, :], 8192.0,
                                         geT_t[:, :], OP.mult, OP.add),
        ["geT", "sigT"], ["geT"])
    dve(lambda e: e.tensor_scalar(geT_t[:, :], geT_t[:, :], -1.0, None,
                                  OP.add), ["geT"], ["geT"])
    dve(lambda e: e.tensor_copy(posI_t[:, :], geT_t[:, :]), ["geT"], ["posI"])
    for c in range(64):
        S.dma_op("GP", lambda e, cc=c: e.indirect_dma_start(
            idxb_h[0:T, :],
            bass.IndirectOffsetOnAxis(posI_t[:, cc:cc + 1], 0),
            iota_t[:, cc:cc + 1], None),
            ["posI", "iota"], ["idxb"], f"isc{c}", drain=(c == 63))

    bis_end = S.snapshot()
    for e in ("DVE", "ACT", "PE", "SP"):
        S.fence(e, bis_end)

    NSG = KSEL // G
    fx1Tg = carve(AR2, 0, 2048, F32)
    x1g_p = [carve(AR2, 8192, 1024, BF16),
             carve(AR2, 10240, 1024, BF16),
             carve(AR2, 12288, 1024, BF16),
             carve(AR2, 14336, 1024, BF16)]
    hbf_t = carve(AR2, 16384, 2048, BF16)
    gar_t = carve(AR2, 20480, FC * G, BF16)
    hu_t = carve(AR2, 36864, 2048, F32)
    SQf_t = carve(AR2, 45056, 2048, BF16)
    x2T_t = carve(AR2, 49152, 2048, F32)
    ubf_t = carve(AR2, 61440, 2048, BF16)

    def w1mm(e, pr_, co0):
        last = None
        for half in range(2):
            for cci in range(CC):
                last = e.matmul(
                    pr_[:, half * G:(half + 1) * G],
                    w1W(cci)[:, (co0 + half) * 128:(co0 + half + 1) * 128],
                    hbf_t[:, cc_s(cci)],
                    start=(cci == 0), stop=(cci == CC - 1))
        return last

    def w2mm(e, pr_, cco_):
        last = None
        for fi in range(FC):
            last = e.matmul(pr_[:, 0:G],
                            w2W(fi)[:, cco_ * 128:(cco_ + 1) * 128],
                            gar_t[:, fi * G:(fi + 1) * G],
                            start=(fi == 0), stop=(fi == FC - 1))
        return last

    def stage_Cf(s):
        sp = s % 2
        for h in range(2):
            dma("SP", idxc_t[:, sp * 2 + h:sp * 2 + h + 1],
                idxb_h[s * G + h * 128:s * G + (h + 1) * 128, :],
                ["idxb", f"idxc{sp}"], [f"idxc{sp}"], f"lix{s}_{h}",
                drain=(h == 1))
        for h in range(2):
            xb = x1g_p[2 * sp + h]
            xbn = f"x1g{2 * sp + h}"
            S.dma_op("GP", lambda e, o=xb, c=sp * 2 + h: e.indirect_dma_start(
                o[:, :], None, x1rows_h[0:T, :],
                bass.IndirectOffsetOnAxis(idxc_t[:, c:c + 1], 0)),
                ["x1rows", f"idxc{sp}", xbn], [xbn], f"gx{s}_{h}")
            dve(lambda e, o=xb: e.reduce_sum(fcol_t[:, 0:1], o[:, :],
                                             axis=AX.X), [xbn], ["fc0"])
            act(lambda e, o=xb: e.activation(ubf_t[:, 0:1024], o[:, :],
                                             AF.Square), [xbn, "ubfR"],
                ["ubfR"])
            dve(lambda e: e.reduce_sum(fcol_t[:, 1:2], ubf_t[:, 0:1024],
                                       axis=AX.X), ["ubfR"], ["fc1"])
            dve(lambda e: e.tensor_scalar(fcol_t[:, 0:1], fcol_t[:, 0:1],
                                          -1.0 / C, None, OP.mult),
                ["fc0"], ["fc0"])
            dve(lambda e: e.tensor_tensor(fcol_t[:, 2:3], fcol_t[:, 0:1],
                                          fcol_t[:, 0:1], OP.mult),
                ["fc0"], ["fc2"])
            dve(lambda e: e.scalar_tensor_tensor(fcol_t[:, 1:2],
                                                 fcol_t[:, 1:2], 1.0 / C,
                                                 fcol_t[:, 2:3],
                                                 OP.mult, OP.subtract),
                ["fc1", "fc2"], ["fc1"])
            act(lambda e: e.activation(fcol_t[:, 1:2], fcol_t[:, 1:2], AF.Ln,
                                       bias=cst_t[:, 1:2]),
                ["fc1", "cst"], ["fc1"])
            act(lambda e: e.activation(fcol_t[:, 1:2], fcol_t[:, 1:2], AF.Exp,
                                       scale=-0.5), ["fc1"], ["fc1"])
            dve(lambda e, o=xb: e.tensor_scalar(o[:, :], o[:, :],
                                                fcol_t[:, 0:1],
                                                fcol_t[:, 1:2],
                                                OP.add, OP.mult),
                [xbn, "fc0", "fc1"], [xbn])
    def stage_ht(s):
        sp = s % 2
        for h in range(2):
            xb = x1g_p[2 * sp + h]
            xbn = f"x1g{2 * sp + h}"
            for cc in range(CC):
                pt, ptn = next_p()
                pe(lambda e, q=pt, c=cc, o=xb: e.transpose(
                    q[:, 0:64].bitcast(BF16),
                    o[:, c * 128:(c + 1) * 128], identb_t[:, :]),
                   [xbn, "identb"], [ptn], f"ht{s}_{h}_{cc}")
                act(lambda e, q=pt, hh=h, c=cc: e.activation(
                    hbf_t[:, c * G + hh * 128:c * G + (hh + 1) * 128],
                    q[:, 0:64].bitcast(BF16), AF.Copy),
                    [ptn, "hbR"], ["hbR"], f"he{s}_{h}_{cc}")

    def stage_Df(s):
        for pair in range(FC // 2):
            pA, pAn = next_p()
            pe(lambda e, p=pA, co0=2 * pair, f=w1mm: f(e, p, co0),
               ["hbR"] + [f"w1_{i}" for i in range(CC)],
               [pAn], f"w1m{s}_{pair}")
            act(lambda e, p=pA, pr_=pair: e.activation(
                gar_t[:, 2 * pr_ * G:(2 * pr_ + 1) * G], p[:, 0:G],
                AF.Gelu_apprx_tanh, bias=b1_t[:, 2 * pr_:2 * pr_ + 1]),
                [pAn, "b1", "garR"], ["garR"])
            act(lambda e, p=pA, pr_=pair: e.activation(
                gar_t[:, (2 * pr_ + 1) * G:(2 * pr_ + 2) * G], p[:, G:2 * G],
                AF.Gelu_apprx_tanh, bias=b1_t[:, 2 * pr_ + 1:2 * pr_ + 2]),
                [pAn, "b1", "garR"], ["garR"])

    def stage_Af(s):
        sp = s % 2
        for cco in range(CC):
            pu, pun = next_pw()
            pe(lambda e, p=pu, co=cco, f=w2mm: f(e, p, co),
               ["garR"] + [f"w2_{i}" for i in range(FC)], [pun],
               f"w2m{s}_{cco}")
            act(lambda e, p=pu, cc=cco: e.activation(
                ubf_t[:, cc_s(cc)], p[:, 0:G], AF.Identity,
                bias=b2_t[:, cc:cc + 1]), [pun, "b2", "ubfR"], ["ubfR"])
        for h in range(2):
            for cc in range(CC):
                pt, ptn = next_p()
                pe(lambda e, q=pt, hh=h, c=cc: e.transpose(
                    q[:, 0:64].bitcast(BF16),
                    ubf_t[:, c * G + hh * 128:c * G + (hh + 1) * 128],
                    identb_t[:, :]),
                   ["ubfR", "identb"], [ptn], f"ut{s}_{h}_{cc}")
                act(lambda e, q=pt, hh=h, c=cc: e.activation(
                    SQf_t[:, hh * 1024 + c * 128:hh * 1024 + (c + 1) * 128],
                    q[:, 0:64].bitcast(BF16), AF.Copy),
                    [ptn, "SQfR"], ["SQfR"], f"ue{s}_{h}_{cc}")
        for h in range(2):
            S.dma_op("GP", lambda e, hh=h, c=sp * 2 + h: e.indirect_dma_start(
                ufull_h[0:T, :],
                bass.IndirectOffsetOnAxis(idxc_t[:, c:c + 1], 0),
                SQf_t[:, hh * 1024:(hh + 1) * 1024], None),
                ["SQfR", f"idxc{sp}", "ufull"], ["ufull"], f"us{s}_{h}",
                drain=(h == 1))

    fx1b_t = carve(AR2, 8192, 2048, F32)
    stg_t = [SQf_t, hbf_t]
    stg_n = ["SQfR", "hbR"]

    def tail_load(g):
        p = g % 2
        fx1ap = fx1Tg if p == 0 else fx1b_t
        fxn = "fx1" if p == 0 else "fx1b"
        wl = [fxn]
        if p == 1 and g == NSG + 1:
            wl += [f"x1g{i}" for i in range(4)]
        dma("SP", cpart_sbuf(fx1ap[:, :]), cpart_dram(x1T_h, g),
            [f"x1st{g}", fxn], wl, f"lf{g}")
        dma("SP", stg_t[p][:, :].rearrange("p (h c) -> p h c", h=2),
            ufull_h[g * G:(g + 1) * G, :].rearrange("(h p) c -> p h c", p=128),
            ["ufull", stg_n[p]], [stg_n[p]], f"ur{g}")
        load_wm(g, p)
        load_lnc(g)

    def stage_Ft(g):
        p = g % 2
        fx1ap = fx1Tg if p == 0 else fx1b_t
        fxn = "fx1" if p == 0 else "fx1b"
        for h in range(2):
            for cc in range(CC):
                pt, ptn = next_p()
                pe(lambda e, q=pt, hh=h, c=cc, sg=stg_t[p]: e.transpose(
                    q[:, 0:64].bitcast(BF16),
                    sg[:, hh * 1024 + c * 128:hh * 1024 + (c + 1) * 128],
                    identb_t[:, :]),
                   [stg_n[p], "identb"], [ptn], f"uT{g}_{h}_{cc}")
                act(lambda e, q=pt, hh=h, c=cc, b=p: e.activation(
                    gar_t[:, b * 2048 + c * G + hh * 128:
                          b * 2048 + c * G + (hh + 1) * 128],
                    q[:, 0:64].bitcast(BF16), AF.Copy),
                    [ptn, "garR"], ["garR"], f"uE{g}_{h}_{cc}")
        pe(lambda e, b=p: e.matmul(PB1[:, 256:256 + G], onesrow_t[:, :],
                                   wmr_t[0:1, b * G:(b + 1) * G],
                                   start=True, stop=True),
           [f"wmr{p}", "onesrow", "PB1"], ["PB1"], "fwmb")
        dve(lambda e: e.tensor_copy(wmb2_t[:, :], PB1[:, 256:256 + G]),
            ["PB1"], ["wmb"])
        bcast_pair(extra_reads=("wmb",), tag="bc_f")
        for cci in range(CC):
            dve(lambda e, cc=cci, fx=fx1ap: e.tensor_tensor(
                hu_t[:, cc_s(cc)], fx[:, cc_s(cc)], PB0[:, 0:G],
                OP.mult), [fxn, "PB0"], ["huR"])
            dve(lambda e, cc=cci: e.tensor_tensor(
                hu_t[:, cc_s(cc)], hu_t[:, cc_s(cc)], PB1[:, 0:G],
                OP.subtract), ["huR", "PB1"], ["huR"])
            dve(lambda e, cc=cci, b=p: e.tensor_tensor(
                x2T_t[:, cc_s(cc)],
                gar_t[:, b * 2048 + cc * G:b * 2048 + (cc + 1) * G],
                wmb2_t[:, :], OP.mult), ["garR", "wmb"], ["x2R"])
            dve(lambda e, cc=cci: e.tensor_tensor(
                hu_t[:, cc_s(cc)], hu_t[:, cc_s(cc)], x2T_t[:, cc_s(cc)],
                OP.add), ["huR", "x2R"], ["huR"])
        act(lambda e: e.activation(ubf_t[:, :], hu_t[:, :], AF.Square),
            ["huR", "ubfR"], ["ubfR"], "sqd")
        dve(lambda e: e.tensor_copy(XBF2[:, :], hu_t[:, :]),
            ["huR", "XBF2"], ["XBF2"])
        pe(lambda e: stat_bf(e, XBF2, PSTs),
           ["XBF2", "onescolb", "PSTs"], ["PSTs"], "sum_d")
        pe(lambda e: stat_bf(e, ubf_t, PSTq),
           ["ubfR", "onescolb", "PSTq"], ["PSTq"], "sq_d")
        rowchain("ln_d")
        bcast_pair(tag="bc_d")
        for cci in range(CC):
            dve(lambda e, cc=cci: e.tensor_tensor(
                x2T_t[:, cc_s(cc)], hu_t[:, cc_s(cc)], PB0[:, 0:G], OP.mult),
                ["huR", "PB0"], ["x2R"])
            dve(lambda e, cc=cci: e.tensor_tensor(
                x2T_t[:, cc_s(cc)], x2T_t[:, cc_s(cc)], PB1[:, 0:G],
                OP.subtract), ["x2R", "PB1"], ["x2R"])
            dve(lambda e, cc=cci, fx=fx1ap: e.tensor_tensor(
                x2T_t[:, cc_s(cc)], x2T_t[:, cc_s(cc)], fx[:, cc_s(cc)],
                OP.add), ["x2R", fxn], ["x2R"])
        dma("SP", cpart_dram(xo_h, g), cpart_sbuf(x2T_t[:, :]),
            ["x2R"], [f"xost{g}"])

    def load_fx1(g):
        dma("SP", cpart_sbuf(fx1Tg[:, :]), cpart_dram(x1T_h, g),
            [f"x1st{g}", "fx1"], ["fx1"], f"lf{g}")

    def load_wm(g, b):
        dma("GP", wmr_t[0:1, b * G:(b + 1) * G],
            wmrow_h[0:1, g * G:(g + 1) * G], ["wmrow", f"wmr{b}"],
            [f"wmr{b}"], f"lwm{g}")

    def load_lnc(g):
        dma("GP", row(RW_RSTD), lncr_h[0:1, g * G:(g + 1) * G],
            [f"lncr{g}", "rrstd"], ["rrstd"], f"llr{g}")
        dma("GP", row(RW_MR), lncm_h[0:1, g * G:(g + 1) * G],
            [f"lncm{g}", "rmr"], ["rmr"], f"llm{g}")

    def stage_F(g):
        fb = g % 2
        for cc in range(CC):
            dma("SP", gar_t[:, fb * 2048 + cc * G:fb * 2048 + (cc + 1) * G],
                ufull_h[g * G:(g + 1) * G, cc * 128:(cc + 1) * 128],
                ["ufull", "garR"], ["garR"], f"uT{g}_{cc}",
                transpose=True, drain=(cc == CC - 1))
        load_lnc(g)
        pe(lambda e, b=fb: e.matmul(PB1[:, 256:256 + G], onesrow_t[:, :],
                                    wmr_t[0:1, b * G:(b + 1) * G],
                                    start=True, stop=True),
           [f"wmr{fb}", "onesrow", "PB1"], ["PB1"], "fwmb")
        dve(lambda e: e.tensor_copy(wmb2_t[:, :], PB1[:, 256:256 + G]),
            ["PB1"], ["wmb"])
        bcast_pair(extra_reads=("wmb",), tag="bc_f")
        for cci in range(CC):
            dve(lambda e, cc=cci: e.tensor_tensor(
                hu_t[:, cc_s(cc)], fx1Tg[:, cc_s(cc)], PB0[:, 0:G],
                OP.mult), ["fx1", "PB0"], ["huR"])
            dve(lambda e, cc=cci: e.tensor_tensor(
                hu_t[:, cc_s(cc)], hu_t[:, cc_s(cc)], PB1[:, 0:G],
                OP.subtract), ["huR", "PB1"], ["huR"])
            dve(lambda e, cc=cci, b=fb: e.tensor_tensor(
                x2T_t[:, cc_s(cc)],
                gar_t[:, b * 2048 + cc * G:b * 2048 + (cc + 1) * G],
                wmb2_t[:, :], OP.mult), ["garR", "wmb"], ["x2R"])
            dve(lambda e, cc=cci: e.tensor_tensor(
                hu_t[:, cc_s(cc)], hu_t[:, cc_s(cc)], x2T_t[:, cc_s(cc)],
                OP.add), ["huR", "x2R"], ["huR"])
        act(lambda e: e.activation(SQf_t[:, :], hu_t[:, :], AF.Square),
            ["huR", "SQfR"], ["SQfR"], "sqd")
        dve(lambda e: e.tensor_copy(XBF2[:, :], hu_t[:, :]),
            ["huR", "XBF2"], ["XBF2"])
        pe(lambda e: stat_bf(e, XBF2, PSTs),
           ["XBF2", "onescolb", "PSTs"], ["PSTs"], "sum_d")
        pe(lambda e: stat_bf(e, SQf_t, PSTq),
           ["SQfR", "onescolb", "PSTq"], ["PSTq"], "sq_d")
        rowchain("ln_d")
        bcast_pair(tag="bc_d")
        for cci in range(CC):
            dve(lambda e, cc=cci: e.tensor_tensor(
                x2T_t[:, cc_s(cc)], hu_t[:, cc_s(cc)], PB0[:, 0:G], OP.mult),
                ["huR", "PB0"], ["x2R"])
            dve(lambda e, cc=cci: e.tensor_tensor(
                x2T_t[:, cc_s(cc)], x2T_t[:, cc_s(cc)], PB1[:, 0:G],
                OP.subtract), ["x2R", "PB1"], ["x2R"])
            dve(lambda e, cc=cci: e.tensor_tensor(
                x2T_t[:, cc_s(cc)], x2T_t[:, cc_s(cc)], fx1Tg[:, cc_s(cc)],
                OP.add), ["x2R", "fx1"], ["x2R"])
        dma("SP", cpart_dram(xo_h, g), cpart_sbuf(x2T_t[:, :]),
            ["x2R"], [f"xost{g}"])

    load_wm(0, 0)
    stage_Cf(0)
    stage_Cf(1)
    stage_ht(0)
    stage_Df(0)
    for s in range(NSG):
        load_fx1(s)
        stage_Af(s)
        if s + 2 < NSG:
            stage_Cf(s + 2)
        stage_F(s)
        if s + 1 < NSG:
            stage_ht(s + 1)
            load_wm(s + 1, (s + 1) % 2)
            stage_Df(s + 1)
    tail_load(NSG)
    for g in range(NSG, NG):
        if g + 1 < NG:
            tail_load(g + 1)
        stage_Ft(g)

    final = S.snapshot()
    S.fence("SP", {s: final[s] for s in ("SP", "GP", "SPX", "GPX")})

    with nc.Block() as blk:
        def emitter(name):
            def run(eng):
                for waits, fn, inc, sem, tag in S.ops[name]:
                    for (de, dv) in waits:
                        eng.wait_ge(sems[de], dv)
                    if fn is not None:
                        inst = fn(eng)
                        if inc:
                            inst.then_inc(sems[sem], inc)
            return run

        blk.tensor(emitter("PE"))
        blk.vector(emitter("DVE"))
        blk.scalar(emitter("ACT"))
        blk.sync(emitter("SP"))
        blk.gpsimd(emitter("GP"))

    stack.close()
    return nc


_NC_CACHE = {}


def kernel(L0_attn_x, L0_attn_time_decay, L0_attn_time_first,
           L0_attn_time_mix_k, L0_attn_time_mix_v, L0_attn_time_mix_r,
           L0_attn_Wk, L0_attn_Wv, L0_attn_Wr, L0_attn_Wo,
           L0_ffn_w_router, L0_ffn_W1, L0_ffn_b1, L0_ffn_W2, L0_ffn_b2):
    import ml_dtypes
    x = np.asarray(L0_attn_x, np.float32)
    B = x.shape[0]
    assert B == 8 and x.shape[1] == T and x.shape[2] == C

    colize = lambda v: np.ascontiguousarray(
        np.asarray(v, np.float32).reshape(CC, 128).T)
    cst = np.zeros((128, 4), np.float32)
    cst[:, 0] = 1.0
    cst[:, 1] = LN_EPS
    cst[:, 2] = 1.0 / C

    common = dict(
        wkT=np.ascontiguousarray(np.asarray(L0_attn_Wk, np.float32).T),
        wvT=np.ascontiguousarray(np.asarray(L0_attn_Wv, np.float32).T),
        wrT=np.ascontiguousarray(np.asarray(L0_attn_Wr, np.float32).T),
        woT=np.ascontiguousarray(np.asarray(L0_attn_Wo, np.float32).T),
        w1=np.ascontiguousarray(np.asarray(L0_ffn_W1, np.float32)),
        w2=np.ascontiguousarray(np.asarray(L0_ffn_W2, np.float32)),
        tmk=colize(np.asarray(L0_attn_time_mix_k).reshape(-1)),
        tmv=colize(np.asarray(L0_attn_time_mix_v).reshape(-1)),
        tmr=colize(np.asarray(L0_attn_time_mix_r).reshape(-1)),
        td=colize(L0_attn_time_decay), tf=colize(L0_attn_time_first),
        wrt=colize(L0_ffn_w_router),
        b1=np.ascontiguousarray(np.asarray(L0_ffn_b1, np.float32).reshape(FC, 128).T),
        b2=colize(L0_ffn_b2),
        iota=np.ascontiguousarray(
            (np.arange(64)[None, :] * 128 +
             np.arange(128)[:, None]).astype(np.int32)),
        triu=np.triu(np.ones((128, 128), np.float32), 1),
        onescol=np.ones((128, 1), np.float32),
        onescolb=np.ones((128, 1), ml_dtypes.bfloat16),
        onesrow=np.ones((1, 128), np.float32),
        ident=np.eye(128, dtype=np.float32),
        cst=cst,
    )
    in_maps = []
    for b in range(B):
        m = dict(common)
        m["xT"] = np.ascontiguousarray(x[b].T)
        in_maps.append(m)

    if "v2" not in _NC_CACHE:
        _NC_CACHE["v2"] = build_program()
    nc = _NC_CACHE["v2"]
    res = run_bass_kernel_spmd(nc, in_maps, core_ids=list(range(B)))
    out = np.empty((B, T, C), np.float32)
    for b in range(B):
        out[b] = res.results[b]["xo"].T
    return out
